# version 23
# baseline (speedup 1.0000x reference)
import numpy as np
from contextlib import ExitStack

import concourse.bass as bass
import concourse.mybir as mybir
from concourse.bass_utils import run_bass_kernel_spmd

F32 = mybir.dt.float32
BF16 = mybir.dt.bfloat16
I32 = mybir.dt.int32
AF = mybir.ActivationFunctionType
ALU = mybir.AluOpType
AX = mybir.AxisListType

B, L, DM, DI, DS, DC, DTR, INTER = 4, 2048, 2048, 4096, 16, 4, 128, 8192
DG = DI // 128
K16 = DM // 128
MI = INTER // 128
S = 1024
W = S + 128
T = S
NCHUNK = [(0, 512), (512, 512)]
ICHUNK = [(125, 3), (128, 512), (640, 512)]
NT = W // 128
EPS = 1e-5
POLL_BOUND = 2_000_000


class Ctr:
    def __init__(self, sem):
        self.sem = sem
        self.n = 0

    def inc(self, instr, k=1):
        instr.then_inc(self.sem, k)
        self.n += k
        return self.n


def build_program(dbg=False, no_hs=False, hs_poll=True, hs_send=True, hs_local=False):
    nc = bass.Bass("TRN2", target_bir_lowering=False, debug=False)
    KIND = "ExternalOutput" if dbg else "Internal"

    x0 = nc.dram_tensor("x0", [W, DM], F32, kind="ExternalInput").ap()
    w_inT = nc.dram_tensor("w_inT", [2 * DI // 128, 128, K16 * 128], F32, kind="ExternalInput").ap()
    conv_w = nc.dram_tensor("conv_w", [128, DG * DC], F32, kind="ExternalInput").ap()
    conv_b = nc.dram_tensor("conv_b", [128, DG], F32, kind="ExternalInput").ap()
    x_projT = nc.dram_tensor("x_projT", [128, DG * 160], F32, kind="ExternalInput").ap()
    dt_projT = nc.dram_tensor("dt_projT", [DTR, DI], F32, kind="ExternalInput").ap()
    dt_b = nc.dram_tensor("dt_b", [128, DG], F32, kind="ExternalInput").ap()
    a_log = nc.dram_tensor("a_log", [128, DG * DS], F32, kind="ExternalInput").ap()
    d_skip = nc.dram_tensor("d_skip", [128, DG], F32, kind="ExternalInput").ap()
    w_outT = nc.dram_tensor("w_outT", [K16, 128, DG * 128], F32, kind="ExternalInput").ap()
    w_gate = nc.dram_tensor("w_gate", [MI, 128, K16 * 128], mybir.dt.int8, kind="ExternalInput").ap()
    w_up = nc.dram_tensor("w_up", [MI, 128, K16 * 128], mybir.dt.int8, kind="ExternalInput").ap()
    w_down = nc.dram_tensor("w_down", [K16, 128, MI * 128], mybir.dt.int8, kind="ExternalInput").ap()
    s_gate = nc.dram_tensor("s_gate", [128, MI], F32, kind="ExternalInput").ap()
    s_up = nc.dram_tensor("s_up", [128, MI], F32, kind="ExternalInput").ap()
    s_down = nc.dram_tensor("s_down", [128, K16], F32, kind="ExternalInput").ap()
    ident_in = nc.dram_tensor("ident", [128, 128], F32, kind="ExternalInput").ap()
    role_in = nc.dram_tensor("role", [1, 1], I32, kind="ExternalInput").ap()
    rolev_in = nc.dram_tensor("rolev", [128, 1], F32, kind="ExternalInput").ap()
    runid_in = nc.dram_tensor("runid", [1, 1], I32, kind="ExternalInput").ap()

    y_out = nc.dram_tensor("y_out", [S, DM], F32, kind="ExternalOutput").ap()

    xpre_scr = nc.dram_tensor("xpre_scr", [DI, S + 3], BF16, kind=KIND)
    z_scr = nc.dram_tensor("z_scr", [DI, S], BF16, kind=KIND)
    zs_scr = nc.dram_tensor("zs_scr", [DI, S], BF16, kind=KIND)
    xc_scr = nc.dram_tensor("xc_scr", [DI, S], BF16, kind=KIND)
    delta_scr = nc.dram_tensor("delta_scr", [DI, S], F32, kind=KIND)
    bc_scr = nc.dram_tensor("bc_scr", [2 * DS, S], F32, kind=KIND)
    yg_scr = nc.dram_tensor("yg_scr", [DI, S], BF16, kind=KIND)
    x1_scr = nc.dram_tensor("x1_scr", [S, DM], F32, kind=KIND)
    v2t_scr = nc.dram_tensor("v2t_scr", [DM, S], BF16, kind=KIND)
    wo_bf_scr = nc.dram_tensor("wo_bf_scr", [K16, 128, DG * 128], BF16, kind=KIND)
    wd_bf_scr = nc.dram_tensor("wd_bf_scr", [K16, 128, MI * 128], BF16, kind=KIND)

    if hs_local:
        sh_carry = nc.dram_tensor("sh_carry", [128, DG * DS], F32)
        sh_flag = nc.dram_tensor("sh_flag", [1, DG * 16], I32)
    else:
        sh_carry = nc.dram_tensor("sh_carry", [128, DG * DS], F32, addr_space="Shared")
        sh_flag = nc.dram_tensor("sh_flag", [1, DG * 16], I32, addr_space="Shared")
    poll_scr = nc.dram_tensor("poll_scr", [1, DG + 2], I32, kind=KIND)

    ctx = ExitStack()
    with ctx:
        sbuf = lambda name, shape, dt: ctx.enter_context(nc.sbuf_tensor(name, shape, dt))
        ident_f = sbuf("ident_f", [128, 128], F32)
        ident_b = sbuf("ident_b", [128, 128], BF16)
        a_sb = sbuf("a_sb", [128, DG * DS], F32)
        dsk_sb = sbuf("dsk_sb", [128, DG], F32)
        cw_sb = sbuf("cw_sb", [128, DG * DC], F32)
        cb_sb = sbuf("cb_sb", [128, DG], F32)
        dtb_sb = sbuf("dtb_sb", [128, DG], F32)
        sg_sb = sbuf("sg_sb", [128, MI], F32)
        su_sb = sbuf("su_sb", [128, MI], F32)
        sd_sb = sbuf("sd_sb", [128, K16], F32)
        carry = sbuf("carry", [128, DG * DS], F32)
        role_sb = sbuf("role_sb", [1, 1], I32)
        rolev_sb = sbuf("rolev_sb", [128, 1], F32)
        runid_sb = sbuf("runid_sb", [1, 1], I32)
        flag_sb = sbuf("flag_sb", [1, 1], I32)
        pstat_sb = sbuf("pstat_sb", [1, DG + 2], I32)
        cstg = sbuf("cstg", [128, DS], F32)
        ctmp = sbuf("ctmp", [128, DS], F32)
        stgb = [sbuf(f"stgb{i}", [128, 512], BF16) for i in range(3)]
        lnt1 = sbuf("lnt1", [128, 1], F32)
        lnt2 = sbuf("lnt2", [128, 1], F32)
        lnc1 = sbuf("lnc1", [128, 1], F32)
        lnc2 = sbuf("lnc2", [128, 1], F32)
        junk = sbuf("junk", [128, 2048], BF16)

        pb = [ctx.enter_context(nc.psum_tensor(f"pb{i}", [128, 512], F32)) for i in range(6)]
        pt = [ctx.enter_context(nc.psum_tensor(f"pt{i}", [128, 1024], BF16)) for i in range(2)]
        ptf = [p.ap().bitcast(F32) for p in pt]

        C = {}
        for name in ["LDA", "LDB", "LDC", "LDD", "MM", "EVA", "EVB", "EVC", "SCA",
                     "VEC", "VSY", "GST", "BAR", "TRP", "TCO", "OUT", "VSC", "CAR", "PSM", "SND", "GPX"]:
            C[name] = Ctr(ctx.enter_context(nc.semaphore(name)))
        LDA, LDB, LDC, LDD = C["LDA"], C["LDB"], C["LDC"], C["LDD"]
        MM, EVA, EVB, EVC = C["MM"], C["EVA"], C["EVB"], C["EVC"]
        SCA, VEC, VSY = C["SCA"], C["VEC"], C["VSY"]
        GST, BAR, TRP, TCO, OUT, VSC = C["GST"], C["BAR"], C["TRP"], C["TCO"], C["OUT"], C["VSC"]
        CAR, PSM, SND, GPX = C["CAR"], C["PSM"], C["SND"], C["GPX"]

        block = ctx.enter_context(nc.Block())
        ENGS = ["tensor", "vector", "scalar", "gpsimd", "sync"]

        def barrier():
            tgt = BAR.n + 5
            for en in ENGS:
                def mk(en=en, tgt=tgt):
                    def _(e):
                        BAR.inc(e.nop())
                        e.wait_ge(BAR.sem, tgt)
                    return _
                getattr(block, en)(mk())

        @block.sync
        def _(sync):
            for dst, src in [(ident_f, ident_in), (cw_sb, conv_w), (cb_sb, conv_b),
                             (dtb_sb, dt_b), (a_sb, a_log), (dsk_sb, d_skip),
                             (sg_sb, s_gate), (su_sb, s_up), (sd_sb, s_down),
                             (role_sb, role_in), (runid_sb, runid_in), (rolev_sb, rolev_in)]:
                sync.dma_start(out=dst[:], in_=src).then_inc(LDA.sem, 16)
        LDA.n += 16 * 12

        @block.scalar
        def _(scalar):
            scalar.wait_ge(LDA.sem, LDA.n)
            SCA.inc(scalar.activation(a_sb[:], a_sb[:], AF.Exp))

        @block.vector
        def _(vector):
            vector.wait_ge(SCA.sem, SCA.n)
            VEC.inc(vector.tensor_scalar(a_sb[:], a_sb[:], -1.0, None, ALU.mult))
            VEC.inc(vector.memset(carry[:], 0.0))
            VEC.inc(vector.tensor_copy(ident_b[:], ident_f[:]))

        g1 = ExitStack()
        lnxT = g1.enter_context(nc.sbuf_tensor("lnxT", [128, K16 * W], BF16))
        x0t = [g1.enter_context(nc.sbuf_tensor(f"x0t{i}", [128, DM], F32)) for i in range(2)]
        lnxt = g1.enter_context(nc.sbuf_tensor("lnxt", [128, DM], BF16))
        wpool = [g1.enter_context(nc.sbuf_tensor(f"wpool{i}", [128, K16 * 128], BF16)) for i in range(3)]
        wstg = [g1.enter_context(nc.sbuf_tensor(f"wstg{i}", [128, K16 * 128], F32)) for i in range(2)]

        for tt in range(NT):
            buf = x0t[tt % 2]

            @block.sync
            def _(sync, tt=tt, buf=buf):
                if tt >= 2:
                    sync.wait_ge(TRP.sem, (tt - 1) * K16)
                sync.dma_start(out=buf[:], in_=x0[tt * 128:(tt + 1) * 128, :]).then_inc(LDB.sem, 16)
            LDB.n += 16

            @block.vector
            def _(vector, buf=buf):
                vector.wait_ge(LDB.sem, LDB.n)
                VEC.inc(vector.tensor_reduce(lnt1[:], buf[:], AX.X, ALU.add))

            @block.scalar
            def _(scalar, buf=buf):
                scalar.wait_ge(LDB.sem, LDB.n)
                scalar.wait_ge(VEC.sem, VEC.n)
                SCA.inc(scalar.activation(junk[:], buf[:], AF.Square, accum_out=lnt2[:]))

            @block.vector
            def _(vector):
                vector.wait_ge(SCA.sem, SCA.n)
                VEC.inc(vector.tensor_scalar(lnt1[:], lnt1[:], 1.0 / DM, None, ALU.mult))
                vector.wait_ge(VEC.sem, VEC.n)
                VEC.inc(vector.tensor_scalar(lnt2[:], lnt2[:], 1.0 / DM, None, ALU.mult))
                vector.wait_ge(VEC.sem, VEC.n)
                VEC.inc(vector.scalar_tensor_tensor(lnc1[:], lnt1[:], lnt1[:], lnt2[:], ALU.mult, ALU.subtract))
                vector.wait_ge(VEC.sem, VEC.n)
                VEC.inc(vector.tensor_scalar(lnc1[:], lnc1[:], -1.0, EPS, ALU.mult, ALU.add))

            @block.scalar
            def _(scalar):
                scalar.wait_ge(VEC.sem, VEC.n)
                SCA.inc(scalar.activation(lnc1[:], lnc1[:], AF.Ln))
                scalar.wait_ge(SCA.sem, SCA.n)
                SCA.inc(scalar.activation(lnc1[:], lnc1[:], AF.Exp, scale=-0.5))

            @block.vector
            def _(vector, tt=tt, buf=buf):
                vector.wait_ge(SCA.sem, SCA.n)
                VEC.inc(vector.tensor_scalar(lnc2[:], lnt1[:], -1.0, lnc1[:], ALU.mult, ALU.mult))
                vector.wait_ge(VEC.sem, VEC.n)
                if tt >= 1:
                    vector.wait_ge(TRP.sem, tt * K16)
                VEC.inc(vector.tensor_scalar(lnxt[:], buf[:], lnc1[:], lnc2[:], ALU.mult, ALU.add))

            for k in range(K16):
                @block.tensor
                def _(tensor, tt=tt, k=k):
                    tensor.wait_ge(VEC.sem, VEC.n)
                    if TRP.n >= 2:
                        tensor.wait_ge(TCO.sem, TRP.n - 1)
                    TRP.inc(tensor.transpose(pt[TRP.n % 2].ap()[:, 0:128], lnxt[:, k * 128:(k + 1) * 128], ident_b[:]))

                @block.scalar
                def _(scalar, tt=tt, k=k):
                    scalar.wait_ge(TRP.sem, TRP.n)
                    TCO.inc(scalar.copy(lnxT[:, k * W + tt * 128: k * W + tt * 128 + 128],
                                        pt[(TRP.n - 1) % 2].ap()[:, 0:128]))

        TCO_P1 = TCO.n
        xpt = [g1.enter_context(nc.sbuf_tensor(f"xpt{i}", [128, S + 3], BF16)) for i in range(2)]
        acc_t = g1.enter_context(nc.sbuf_tensor("acc_t", [128, S], F32))
        xct = g1.enter_context(nc.sbuf_tensor("xct", [128, S], BF16))
        cstore_gst = []
        csil_sca = []

        def emit_conv(dg):
            xb = xpt[dg % 2]

            @block.sync
            def _(sync, dg=dg, xb=xb):
                sync.wait_ge(GST.sem, xpre_gst[dg])
                if dg >= 2:
                    sync.wait_ge(VEC.sem, VEC.n)
                sync.dma_start(out=xb[:], in_=xpre_scr.ap()[dg * 128:(dg + 1) * 128, :]).then_inc(LDD.sem, 16)
            LDD.n += 16

            @block.vector
            def _(vector, dg=dg, xb=xb):
                vector.wait_ge(LDD.sem, LDD.n)
                if dg >= 1:
                    vector.wait_ge(SCA.sem, csil_sca[dg - 1])
                VEC.inc(vector.tensor_scalar(acc_t[:], xb[:, 0:S], cw_sb[:, dg * DC: dg * DC + 1],
                                             cb_sb[:, dg:dg + 1], ALU.mult, ALU.add))
                for k in range(1, DC):
                    vector.wait_ge(VEC.sem, VEC.n)
                    VEC.inc(vector.scalar_tensor_tensor(acc_t[:], xb[:, k:k + S],
                                                        cw_sb[:, dg * DC + k: dg * DC + k + 1],
                                                        acc_t[:], ALU.mult, ALU.add))

            @block.scalar
            def _(scalar, dg=dg):
                scalar.wait_ge(VEC.sem, VEC.n)
                if dg >= 1:
                    scalar.wait_ge(GST.sem, cstore_gst[dg - 1])
                ins = scalar.activation(xct[:], acc_t[:], AF.Silu)
                SCA.inc(ins)
                scalar.wait_ge(SCA.sem, SCA.n)
                scalar.dma_start(out=xc_scr.ap()[dg * 128:(dg + 1) * 128, :], in_=xct[:]).then_inc(GST.sem, 16)
            csil_sca.append(SCA.n)
            GST.n += 16
            cstore_gst.append(GST.n)

        mm_g = 0
        inproj_end = []
        xpre_gst = []
        cast_marks = []
        NM = 2 * DI // 128

        def emit_wload(m):
            ws = wstg[m % 2]
            wb = wpool[m % 3]

            @block.sync
            def _(sync, m=m, ws=ws):
                if m >= 2:
                    sync.wait_ge(VEC.sem, cast_marks[m - 2])
                sync.dma_start(out=ws[:], in_=w_inT[m]).then_inc(LDC.sem, 16)
            LDC.n += 16

            @block.vector
            def _(vector, m=m, wb=wb, ws=ws):
                vector.wait_ge(LDC.sem, LDC.n)
                if m >= 3:
                    vector.wait_ge(MM.sem, inproj_end[m - 3])
                VEC.inc(vector.tensor_copy(wb[:], ws[:]))
            cast_marks.append(VEC.n)

        emit_wload(0)
        emit_wload(1)
        for m in range(NM):
            wb = wpool[m % 3]
            chunks = ICHUNK if m < DG else ICHUNK[1:]
            if m + 2 < NM:
                emit_wload(m + 2)

            for (ns, nw) in chunks:
                g = mm_g
                bank = pb[g % 4]

                @block.tensor
                def _(tensor, m=m, ns=ns, nw=nw, g=g, bank=bank, wb=wb):
                    tensor.wait_ge(VEC.sem, cast_marks[m])
                    tensor.wait_ge(TCO.sem, TCO_P1)
                    if g >= 4:
                        tensor.wait_ge(EVB.sem, g - 3)
                    for k in range(K16):
                        ins = tensor.matmul(bank.ap()[:, 0:nw], wb[:, k * 128:(k + 1) * 128],
                                            lnxT[:, k * W + ns: k * W + ns + nw],
                                            start=(k == 0), stop=(k == K16 - 1))
                    MM.inc(ins)

                if m < DG:
                    @block.vector
                    def _(vector, m=m, g=g, nw=nw, bank=bank):
                        vector.wait_ge(MM.sem, MM.n)
                        if g >= 3:
                            vector.wait_ge(GST.sem, GST.n)
                        EVB.inc(vector.tensor_copy(stgb[g % 3][:, 0:nw], bank.ap()[:, 0:nw]))
                else:
                    @block.scalar
                    def _(scalar, m=m, g=g, nw=nw, bank=bank):
                        scalar.wait_ge(MM.sem, MM.n)
                        if g >= 3:
                            scalar.wait_ge(GST.sem, GST.n)
                        EVB.inc(scalar.activation(stgb[g % 3][:, 0:nw], bank.ap()[:, 0:nw], AF.Silu))

                @block.gpsimd
                def _(gp, m=m, ns=ns, nw=nw, g=g):
                    gp.wait_ge(EVB.sem, EVB.n)
                    sg = stgb[g % 3]
                    if m < DG:
                        dst = xpre_scr.ap()[m * 128:(m + 1) * 128, ns - 125: ns - 125 + nw]
                    else:
                        dst = zs_scr.ap()[(m - DG) * 128:(m - DG + 1) * 128, ns - 128: ns - 128 + nw]
                    gp.dma_start(out=dst, in_=sg[:, 0:nw]).then_inc(GST.sem, 16)
                GST.n += 16
                mm_g += 1
            inproj_end.append(MM.n)
            if m < DG:
                xpre_gst.append(GST.n)
                if m >= 1:
                    emit_conv(m - 1)
            if m == DG:
                emit_conv(DG - 1)

        g2 = ExitStack()
        xp_sb = g2.enter_context(nc.sbuf_tensor("xp_sb", [128, DG * 160], BF16))
        dtw_sb = g2.enter_context(nc.sbuf_tensor("dtw_sb", [128, DG * 128], BF16))
        dtlow = g2.enter_context(nc.sbuf_tensor("dtlow", [128, S], BF16))
        bc_sb = g2.enter_context(nc.sbuf_tensor("bc_sb", [32, S], F32))
        xcr = [g2.enter_context(nc.sbuf_tensor(f"xcr{i}", [128, 512], BF16)) for i in range(2)]
        sp1 = g2.enter_context(nc.sbuf_tensor("sp1", [128, 512], F32))
        sp2 = g2.enter_context(nc.sbuf_tensor("sp2", [128, 512], F32))
        sp3 = g2.enter_context(nc.sbuf_tensor("sp3", [128, 512], F32))

        @block.gpsimd
        def _(gp):
            gp.dma_start(out=xp_sb[:], in_=x_projT).then_inc(LDC.sem, 16)
            gp.dma_start(out=dtw_sb[:], in_=dt_projT).then_inc(LDC.sem, 16)
        LDC.n += 32

        xp_mm0 = MM.n
        for ci, (ns, nw) in enumerate(NCHUNK):
            ld0 = LDD.n

            @block.sync
            def _(sync, ci=ci, ns=ns, nw=nw, ld0=ld0):
                sync.wait_ge(GST.sem, GST.n)
                for k in range(DG):
                    if k >= 2:
                        sync.wait_ge(MM.sem, xp_mm0 + ci * DG + k - 1)
                    elif ci > 0:
                        sync.wait_ge(MM.sem, xp_mm0 + ci * DG)
                    sync.dma_start(out=xcr[k % 2][:, 0:nw],
                                   in_=xc_scr.ap()[k * 128:(k + 1) * 128, ns:ns + nw]).then_inc(LDD.sem, 16)

            @block.tensor
            def _(tensor, ci=ci, ns=ns, nw=nw, ld0=ld0):
                tensor.wait_ge(LDC.sem, LDC.n)
                if ci >= 1:
                    tensor.wait_ge(EVB.sem, EVB.n)
                for k in range(DG):
                    tensor.wait_ge(LDD.sem, ld0 + (k + 1) * 16)
                    tensor.matmul(pb[4].ap()[:, 0:nw], xp_sb[:, k * 160:k * 160 + 128],
                                  xcr[k % 2][:, 0:nw], start=(k == 0), stop=(k == DG - 1))
                    ins = tensor.matmul(pb[5].ap()[0:32, 0:nw], xp_sb[:, k * 160 + 128:(k + 1) * 160],
                                        xcr[k % 2][:, 0:nw], start=(k == 0), stop=(k == DG - 1))
                    MM.inc(ins)
            LDD.n += 16 * DG

            @block.vector
            def _(vector, ns=ns, nw=nw):
                vector.wait_ge(MM.sem, MM.n)
                VEC.inc(vector.tensor_copy(dtlow[:, ns:ns + nw], pb[4].ap()[:, 0:nw]))
                vector.wait_ge(VEC.sem, VEC.n)
                EVB.inc(vector.tensor_copy(bc_sb[0:32, ns:ns + nw], pb[5].ap()[0:32, 0:nw]))

        @block.gpsimd
        def _(gp):
            gp.wait_ge(EVB.sem, EVB.n)
            gp.dma_start(out=bc_scr.ap(), in_=bc_sb[0:32, :]).then_inc(GST.sem, 16)
        GST.n += 16

        for m in range(DG):
            for ci, (ns, nw) in enumerate(NCHUNK):
                g = m * len(NCHUNK) + ci
                bank = pb[g % 4]

                @block.tensor
                def _(tensor, m=m, ns=ns, nw=nw, g=g, bank=bank):
                    tensor.wait_ge(LDC.sem, LDC.n)
                    tensor.wait_ge(VEC.sem, VEC.n)
                    if g >= 4:
                        tensor.wait_ge(EVC.sem, EVC.n)
                    ins = tensor.matmul(bank.ap()[:, 0:nw], dtw_sb[:, m * 128:(m + 1) * 128],
                                        dtlow[:, ns:ns + nw], start=True, stop=True)
                    MM.inc(ins)

                @block.scalar
                def _(scalar, m=m, nw=nw, bank=bank):
                    scalar.wait_ge(MM.sem, MM.n)
                    scalar.wait_ge(EVC.sem, EVC.n)
                    SCA.inc(scalar.activation(sp1[:, 0:nw], bank.ap()[:, 0:nw], AF.Abs, bias=dtb_sb[:, m:m + 1]))
                    scalar.wait_ge(SCA.sem, SCA.n)
                    SCA.inc(scalar.activation(sp2[:, 0:nw], sp1[:, 0:nw], AF.Exp, scale=-1.0))
                    scalar.wait_ge(SCA.sem, SCA.n)
                    SCA.inc(scalar.activation(sp2[:, 0:nw], sp2[:, 0:nw], AF.Ln, bias=1.0))

                @block.vector
                def _(vector, m=m, nw=nw, bank=bank):
                    vector.wait_ge(MM.sem, MM.n)
                    vector.wait_ge(GST.sem, GST.n)
                    VEC.inc(vector.tensor_scalar(sp3[:, 0:nw], bank.ap()[:, 0:nw], dtb_sb[:, m:m + 1],
                                                 0.0, ALU.add, ALU.max))
                    vector.wait_ge(SCA.sem, SCA.n)
                    vector.wait_ge(VEC.sem, VEC.n)
                    EVC.inc(vector.tensor_tensor(sp3[:, 0:nw], sp3[:, 0:nw], sp2[:, 0:nw], ALU.add))

                @block.gpsimd
                def _(gp, m=m, ns=ns, nw=nw):
                    gp.wait_ge(EVC.sem, EVC.n)
                    gp.dma_start(out=delta_scr.ap()[m * 128:(m + 1) * 128, ns:ns + nw],
                                 in_=sp3[:, 0:nw]).then_inc(GST.sem, 16)
                GST.n += 16

        @block.sync
        def _(sync):
            sync.wait_ge(GST.sem, GST.n)
        barrier()
        g2.close()
        g1.close()

        g3 = ExitStack()
        brep = g3.enter_context(nc.sbuf_tensor("brep", [128, DS * T], BF16))
        crep = g3.enter_context(nc.sbuf_tensor("crep", [128, DS * T], BF16))
        dl_t = [g3.enter_context(nc.sbuf_tensor(f"dl{i}", [128, T], F32)) for i in range(2)]
        xc_t = [g3.enter_context(nc.sbuf_tensor(f"xcs{i}", [128, T], BF16)) for i in range(3)]
        z_t = [g3.enter_context(nc.sbuf_tensor(f"zt{i}", [128, T], BF16)) for i in range(3)]
        dxu_t = g3.enter_context(nc.sbuf_tensor("dxu", [128, T], BF16))
        da_f = g3.enter_context(nc.sbuf_tensor("da_f", [128, DS * T], BF16))
        xb_f = g3.enter_context(nc.sbuf_tensor("xb_f", [128, DS * T], BF16))
        h_sb = g3.enter_context(nc.sbuf_tensor("h_sb", [128, DS * T], BF16))
        htmp = g3.enter_context(nc.sbuf_tensor("htmp", [128, 4 * T], BF16))
        yd_t = g3.enter_context(nc.sbuf_tensor("yd", [128, T], BF16))

        regs = ExitStack()
        r_role = regs.enter_context(nc.sync.register("r_role"))
        r_run = regs.enter_context(nc.sync.register("r_run"))
        r_flag = regs.enter_context(nc.sync.register("r_flag"))
        r_go = regs.enter_context(nc.sync.register("r_go"))
        r_it = regs.enter_context(nc.sync.register("r_it"))
        r_tmp = regs.enter_context(nc.sync.register("r_tmp"))

        @block.sync
        def _(gp):
            gp.wait_ge(LDA.sem, 16 * 12)
            gp.reg_load(r_role, role_sb[0:1, 0:1])
            gp.reg_load(r_run, runid_sb[0:1, 0:1])
            gp.store(pstat_sb[0:1, DG:DG + 1], r_role)
            gp.store(pstat_sb[0:1, DG + 1:DG + 2], r_run)

        @block.gpsimd
        def _(gp):
            src_b = bc_scr.ap()[0:DS, 0:T].partition_broadcast(128)
            src_c = bc_scr.ap()[DS:2 * DS, 0:T].partition_broadcast(128)
            gp.dma_start(out=brep[:].rearrange("p (n t) -> p n t", t=T), in_=src_b).then_inc(LDC.sem, 16)
            gp.dma_start(out=crep[:].rearrange("p (n t) -> p n t", t=T), in_=src_c).then_inc(LDC.sem, 16)
        LDC.n += 32
        ldc_brep = LDC.n

        vsc_end, gpx_end, dg_sca = [], [], []
        for dg in range(DG):
            it = dg
            dl, xc_b, z_b = dl_t[it % 2], xc_t[it % 3], z_t[it % 3]
            vsc0 = VSC.n

            @block.sync
            def _(sync, dg=dg, dl=dl, xc_b=xc_b, z_b=z_b, it=it):
                if it >= 2:
                    sync.wait_ge(VSC.sem, vsc_end[it - 2])
                    sync.wait_ge(SCA.sem, dg_sca[it - 2])
                sync.dma_start(out=dl[:], in_=delta_scr.ap()[dg * 128:(dg + 1) * 128, :]).then_inc(LDD.sem, 16)
                if it >= 3:
                    sync.wait_ge(VSY.sem, it - 2)
                sync.dma_start(out=xc_b[:], in_=xc_scr.ap()[dg * 128:(dg + 1) * 128, :]).then_inc(LDD.sem, 16)
                sync.dma_start(out=z_b[:], in_=zs_scr.ap()[dg * 128:(dg + 1) * 128, :]).then_inc(LDD.sem, 16)
            LDD.n += 48

            @block.sync
            def _(gp, dg=dg):
                if no_hs or not hs_poll:
                    CAR.inc(gp.nop())
                    return
                gp.reg_mov(r_go, r_role)
                gp.reg_mov(r_it, 0)
                with gp.While(r_go):
                    gp.sem_clear(PSM.sem)
                    gp.dma_start(out=flag_sb[:], in_=sh_flag.ap()[0:1, dg * 16:dg * 16 + 1]).then_inc(PSM.sem, 16)
                    gp.wait_ge(PSM.sem, 16)
                    gp.reg_load(r_flag, flag_sb[0:1, 0:1])
                    gp.reg_alu(r_it, r_it, 1, ALU.add)
                    gp.reg_alu(r_go, r_flag, r_run, ALU.not_equal)
                    gp.reg_alu(r_tmp, r_it, POLL_BOUND, ALU.is_lt)
                    gp.reg_alu(r_go, r_go, r_tmp, ALU.mult)
                    gp.reg_alu(r_go, r_go, r_role, ALU.mult)
                gp.store(pstat_sb[0:1, dg:dg + 1], r_it)
                gp.reg_mov(r_go, r_role)
                with gp.While(r_go):
                    gp.sem_clear(PSM.sem)
                    ins = gp.dma_start(out=carry[:, dg * DS:(dg + 1) * DS],
                                       in_=sh_carry.ap()[:, dg * DS:(dg + 1) * DS])
                    ins.then_inc(PSM.sem, 16)
                    gp.wait_ge(PSM.sem, 16)
                    gp.reg_mov(r_go, 0)
                CAR.inc(gp.nop())

            @block.vector
            def _(vector, dg=dg, dl=dl, xc_b=xc_b, ldd_now=LDD.n):
                vector.wait_ge(LDD.sem, ldd_now)
                vector.wait_ge(LDC.sem, ldc_brep)
                VEC.inc(vector.tensor_tensor(dxu_t[:], dl[:], xc_b[:], ALU.mult))
                for n in range(DS):
                    ins = vector.tensor_tensor(xb_f[:, n * T:(n + 1) * T], dxu_t[:],
                                               brep[:, n * T:(n + 1) * T], ALU.mult)
                VEC.inc(ins)

            sca_marks = []
            @block.scalar
            def _(scalar, dg=dg, dl=dl):
                scalar.wait_ge(LDD.sem, LDD.n)
                for half in range(2):
                    if dg >= 1:
                        scalar.wait_ge(VSC.sem, vsc_end[dg - 1] - 8 + 8 * half)
                    for n in range(half * 8, half * 8 + 8):
                        ins = scalar.activation(da_f[:, n * T:(n + 1) * T], dl[:],
                                                AF.Exp, scale=a_sb[:, dg * DS + n: dg * DS + n + 1])
                    SCA.inc(ins)
                    sca_marks.append(SCA.n)
            dg_sca.append(SCA.n)

            @block.vector
            def _(vector, dg=dg):
                vector.wait_ge(CAR.sem, CAR.n)
                for n in range(DS):
                    if n % 8 == 0:
                        vector.wait_ge(SCA.sem, sca_marks[n // 8])
                    VSC.inc(vector.tensor_tensor_scan(h_sb[:, n * T:(n + 1) * T],
                                                      da_f[:, n * T:(n + 1) * T],
                                                      xb_f[:, n * T:(n + 1) * T],
                                                      carry[:, dg * DS + n: dg * DS + n + 1],
                                                      ALU.mult, ALU.add))
            vsc_end.append(VSC.n)

            @block.vector
            def _(vector, dg=dg):
                vector.wait_ge(VSC.sem, VSC.n)
                VEC.inc(vector.tensor_copy(carry[:, dg * DS:(dg + 1) * DS], h_sb[:, T - 1::T]))

            carx = VEC.n

            @block.sync
            def _(gp, dg=dg, carx=carx):
                if no_hs or not hs_send:
                    SND.inc(gp.nop())
                    return
                gp.wait_ge(VEC.sem, carx)
                gp.reg_alu(r_go, r_role, 0, ALU.is_equal)
                with gp.While(r_go):
                    gp.sem_clear(PSM.sem)
                    gp.dma_start(out=sh_carry.ap()[:, dg * DS:(dg + 1) * DS],
                                 in_=carry[:, dg * DS:(dg + 1) * DS]).then_inc(PSM.sem, 16)
                    gp.wait_ge(PSM.sem, 16)
                    gp.sem_clear(PSM.sem)
                    gp.dma_start(out=sh_flag.ap()[0:1, dg * 16:dg * 16 + 1], in_=runid_sb[:]).then_inc(PSM.sem, 16)
                    gp.wait_ge(PSM.sem, 16)
                    gp.reg_mov(r_go, 0)
                SND.inc(gp.nop())

            @block.vector
            def _(vector, dg=dg):
                vector.wait_ge(VEC.sem, VEC.n)
                for q in range(4):
                    q0 = q * 4 * T
                    VEC.inc(vector.tensor_tensor(htmp[:], h_sb[:, q0:q0 + 4 * T],
                                                 crep[:, q0:q0 + 4 * T], ALU.mult))
                    vector.wait_ge(VEC.sem, VEC.n)
                    VEC.inc(vector.tensor_tensor(h_sb[:, q0:q0 + 2 * T], htmp[:, 0:2 * T],
                                                 htmp[:, 2 * T:4 * T], ALU.add))
                    vector.wait_ge(VEC.sem, VEC.n)
                    VEC.inc(vector.tensor_tensor(htmp[:, 0:T], h_sb[:, q0:q0 + T],
                                                 h_sb[:, q0 + T:q0 + 2 * T], ALU.add))
                    vector.wait_ge(VEC.sem, VEC.n)
                    if q == 0:
                        VEC.inc(vector.tensor_copy(h_sb[:, 0:T], htmp[:, 0:T]))
                    else:
                        VEC.inc(vector.tensor_tensor(h_sb[:, 0:T], h_sb[:, 0:T], htmp[:, 0:T], ALU.add))
                    vector.wait_ge(VEC.sem, VEC.n)

            @block.vector
            def _(vector, dg=dg, xc_b=xc_b, z_b=z_b):
                vector.wait_ge(VEC.sem, VEC.n)
                vector.wait_ge(GST.sem, GST.n)
                VEC.inc(vector.scalar_tensor_tensor(yd_t[:], xc_b[:], dsk_sb[:, dg:dg + 1],
                                                    h_sb[:, 0:T], ALU.mult, ALU.add))
                vector.wait_ge(VEC.sem, VEC.n)
                VSY.inc(vector.tensor_tensor(yd_t[:], yd_t[:], z_b[:], ALU.mult))

            @block.gpsimd
            def _(gp, dg=dg):
                gp.wait_ge(VSY.sem, VSY.n)
                gp.dma_start(out=yg_scr.ap()[dg * 128:(dg + 1) * 128, :],
                             in_=yd_t[:]).then_inc(GST.sem, 16)
                if dg < K16:
                    gp.dma_start(out=wo_bf_scr.ap()[dg], in_=w_outT[dg]).then_inc(LDA.sem, 16)
                else:
                    gp.dma_start(out=wd_bf_scr.ap()[dg - K16], in_=w_down[dg - K16]).then_inc(LDA.sem, 16)
            GST.n += 16
            LDA.n += 16

        @block.sync
        def _(gp):
            gp.wait_ge(SND.sem, SND.n)
            gp.dma_start(out=poll_scr.ap(), in_=pstat_sb[:]).then_inc(GST.sem, 16)
        GST.n += 16
        regs.close()

        @block.sync
        def _(sync):
            sync.wait_ge(GST.sem, GST.n)
        barrier()
        g3.close()

        g4 = ExitStack()
        ygsb_b = [g4.enter_context(nc.sbuf_tensor(f"ygsb{i}", [128, DG * 512], BF16)) for i in range(2)]
        attn_b = [g4.enter_context(nc.sbuf_tensor(f"attn{i}", [128, K16 * 512], F32)) for i in range(2)]
        wo_sb = [g4.enter_context(nc.sbuf_tensor(f"wo{i}", [128, DG * 128], BF16)) for i in range(2)]
        x1t = [g4.enter_context(nc.sbuf_tensor(f"x1t{i}", [128, DM], F32)) for i in range(2)]
        v2tt = g4.enter_context(nc.sbuf_tensor("v2tt", [128, DM], BF16))
        v2stg = [g4.enter_context(nc.sbuf_tensor(f"v2stg{i}", [128, K16 * 128], BF16)) for i in range(2)]

        oev = []
        v2_gst_marks = []
        for ci, (ns, nw) in enumerate(NCHUNK):
            ygsb = ygsb_b[ci % 2]

            @block.sync
            def _(sync, ci=ci, ns=ns, nw=nw, ygsb=ygsb):
                for k in range(DG):
                    sync.dma_start(out=ygsb[:, k * 512:k * 512 + nw],
                                   in_=yg_scr.ap()[k * 128:(k + 1) * 128, ns:ns + nw]).then_inc(LDD.sem, 16)
            LDD.n += 16 * DG

        for m in range(K16):
            wb = wo_sb[m % 2]

            @block.scalar
            def _(scalar, m=m, wb=wb):
                if m >= 2:
                    scalar.wait_ge(MM.sem, MM.n - 1)
                else:
                    scalar.wait_ge(LDA.sem, LDA.n)
                scalar.dma_start(out=wb[:], in_=wo_bf_scr.ap()[m]).then_inc(LDC.sem, 16)
            LDC.n += 16

            b0, b1 = pb[(2 * m) % 4], pb[(2 * m + 1) % 4]

            @block.tensor
            def _(tensor, m=m, wb=wb, b0=b0, b1=b1):
                tensor.wait_ge(LDC.sem, LDC.n)
                tensor.wait_ge(LDD.sem, LDD.n)
                if m >= 2:
                    tensor.wait_ge(EVB.sem, oev[2 * m - 3])
                for k in range(DG):
                    tensor.matmul(b0.ap()[:], wb[:, k * 128:(k + 1) * 128],
                                  ygsb_b[0][:, k * 512:(k + 1) * 512],
                                  start=(k == 0), stop=(k == DG - 1))
                    ins = tensor.matmul(b1.ap()[:], wb[:, k * 128:(k + 1) * 128],
                                        ygsb_b[1][:, k * 512:(k + 1) * 512],
                                        start=(k == 0), stop=(k == DG - 1))
                MM.inc(ins)

            for cc in range(2):
                @block.vector
                def _(vector, m=m, cc=cc, bank=(b0, b1)[cc]):
                    vector.wait_ge(MM.sem, MM.n)
                    EVB.inc(vector.tensor_copy(attn_b[cc][:, m * 512:(m + 1) * 512], bank.ap()[:]))
                oev.append(EVB.n)

        for ci, (ns, nw) in enumerate(NCHUNK):
            attn_sb = attn_b[ci]

            for tt in range(nw // 128):
                xt = x1t[tt % 2]

                @block.sync
                def _(sync, ci=ci, ns=ns, tt=tt, xt=xt):
                    sync.wait_ge(GST.sem, GST.n)
                    sync.dma_start(out=xt[:], in_=x0[128 + ns + tt * 128: 128 + ns + tt * 128 + 128, :]).then_inc(LDB.sem, 16)
                LDB.n += 16

                rb4 = [ptf[0], ptf[1], pb[4].ap(), pb[5].ap()]
                for m in range(K16):
                    @block.tensor
                    def _(tensor, m=m, tt=tt):
                        tensor.wait_ge(EVB.sem, EVB.n)
                        lag = 1 if (m < 2 or TRP.n < 4) else 3
                        if TRP.n >= 2:
                            tensor.wait_ge(TCO.sem, TRP.n - lag)
                        TRP.inc(tensor.transpose(rb4[TRP.n % 4][:, 0:128],
                                                 attn_sb[:, m * 512 + tt * 128: m * 512 + tt * 128 + 128],
                                                 ident_f[:]))

                    @block.vector
                    def _(vector, m=m, xt=xt):
                        vector.wait_ge(TRP.sem, TRP.n)
                        vector.wait_ge(LDB.sem, LDB.n)
                        TCO.inc(vector.tensor_tensor(xt[:, m * 128:(m + 1) * 128],
                                                     rb4[(TRP.n - 1) % 4][:, 0:128],
                                                     xt[:, m * 128:(m + 1) * 128], ALU.add))

                @block.scalar
                def _(scalar, ns=ns, tt=tt, xt=xt):
                    scalar.wait_ge(TCO.sem, TCO.n)
                    scalar.dma_start(out=x1_scr.ap()[ns + tt * 128: ns + tt * 128 + 128, :], in_=xt[:]).then_inc(GST.sem, 16)
                GST.n += 16

                @block.vector
                def _(vector, xt=xt):
                    vector.wait_ge(TCO.sem, TCO.n)
                    VEC.inc(vector.tensor_reduce(lnt1[:], xt[:], AX.X, ALU.add))

                @block.scalar
                def _(scalar, xt=xt):
                    scalar.wait_ge(VEC.sem, VEC.n)
                    SCA.inc(scalar.activation(junk[:], xt[:], AF.Square, accum_out=lnt2[:]))

                @block.vector
                def _(vector):
                    vector.wait_ge(SCA.sem, SCA.n)
                    VEC.inc(vector.tensor_scalar(lnt1[:], lnt1[:], 1.0 / DM, None, ALU.mult))
                    vector.wait_ge(VEC.sem, VEC.n)
                    VEC.inc(vector.tensor_scalar(lnt2[:], lnt2[:], 1.0 / DM, None, ALU.mult))
                    vector.wait_ge(VEC.sem, VEC.n)
                    VEC.inc(vector.scalar_tensor_tensor(lnc1[:], lnt1[:], lnt1[:], lnt2[:], ALU.mult, ALU.subtract))
                    vector.wait_ge(VEC.sem, VEC.n)
                    VEC.inc(vector.tensor_scalar(lnc1[:], lnc1[:], -1.0, EPS, ALU.mult, ALU.add))

                @block.scalar
                def _(scalar):
                    scalar.wait_ge(VEC.sem, VEC.n)
                    SCA.inc(scalar.activation(lnc1[:], lnc1[:], AF.Ln))
                    scalar.wait_ge(SCA.sem, SCA.n)
                    SCA.inc(scalar.activation(lnc1[:], lnc1[:], AF.Exp, scale=-0.5))

                @block.vector
                def _(vector, xt=xt):
                    vector.wait_ge(SCA.sem, SCA.n)
                    VEC.inc(vector.tensor_scalar(lnc2[:], lnt1[:], -1.0, lnc1[:], ALU.mult, ALU.mult))
                    vector.wait_ge(VEC.sem, VEC.n)
                    vector.wait_ge(TRP.sem, TRP.n)
                    VEC.inc(vector.tensor_scalar(v2tt[:], xt[:], lnc1[:], lnc2[:], ALU.mult, ALU.add))

                gtt = ci * 4 + tt
                vsb = v2stg[gtt % 2]
                for m in range(K16):
                    @block.tensor
                    def _(tensor, m=m):
                        tensor.wait_ge(VEC.sem, VEC.n)
                        if TRP.n >= 2:
                            tensor.wait_ge(TCO.sem, TRP.n - 1)
                        TRP.inc(tensor.transpose(pt[TRP.n % 2].ap()[:, 0:128],
                                                 v2tt[:, m * 128:(m + 1) * 128], ident_b[:]))

                    @block.vector
                    def _(vector, m=m, gtt=gtt, vsb=vsb):
                        vector.wait_ge(TRP.sem, TRP.n)
                        if m == 0 and gtt >= 2:
                            vector.wait_ge(GST.sem, v2_gst_marks[gtt - 2])
                        TCO.inc(vector.tensor_copy(vsb[:, m * 128:(m + 1) * 128],
                                                   pt[(TRP.n - 1) % 2].ap()[:, 0:128]))

                @block.gpsimd
                def _(gp, ci=ci, ns=ns, tt=tt, vsb=vsb):
                    gp.wait_ge(TCO.sem, TCO.n)
                    gp.dma_start(out=v2t_scr.ap()[:, ns + tt * 128: ns + tt * 128 + 128]
                                 .rearrange("(m p) c -> p m c", m=K16),
                                 in_=vsb[:].rearrange("p (m c) -> p m c", m=K16)).then_inc(GST.sem, 16)
                GST.n += 16
                v2_gst_marks.append(GST.n)

        @block.sync
        def _(sync):
            sync.wait_ge(GST.sem, GST.n)
        barrier()
        g4.close()

        g5 = ExitStack()
        v2k = g5.enter_context(nc.sbuf_tensor("v2k", [128, K16 * 512], BF16))
        fuse = g5.enter_context(nc.sbuf_tensor("fuse", [128, MI * 512], BF16))
        wg_sb = [g5.enter_context(nc.sbuf_tensor(f"wg{i}", [128, K16 * 128], BF16)) for i in range(2)]
        wu_sb = [g5.enter_context(nc.sbuf_tensor(f"wu{i}", [128, K16 * 128], BF16)) for i in range(2)]
        wd_sb = [g5.enter_context(nc.sbuf_tensor(f"wd{i}", [128, MI * 128], BF16)) for i in range(2)]
        gate_t = [g5.enter_context(nc.sbuf_tensor(f"gate_t{i}", [128, 512], BF16)) for i in range(2)]
        ds_t = [g5.enter_context(nc.sbuf_tensor(f"ds_t{i}", [128, 512], F32)) for i in range(2)]
        x1f = [g5.enter_context(nc.sbuf_tensor(f"x1f{i}", [128, DM], F32)) for i in range(4)]

        for ci, (ns, nw) in enumerate(NCHUNK):
            @block.sync
            def _(sync, ci=ci, ns=ns, nw=nw):
                if ci > 0:
                    sync.wait_ge(OUT.sem, OUT.n)
                for k in range(K16):
                    sync.dma_start(out=v2k[:, k * 512:k * 512 + nw],
                                   in_=v2t_scr.ap()[k * 128:(k + 1) * 128, ns:ns + nw]).then_inc(LDD.sem, 16)
                for tt in range(4):
                    sync.dma_start(out=x1f[tt][:],
                                   in_=x1_scr.ap()[ns + tt * 128: ns + tt * 128 + 128, :]).then_inc(LDB.sem, 16)
            LDD.n += 16 * K16
            LDB.n += 16 * 4

            mm_end, sca_end, evb_end = [], [], []
            for mi in range(MI):
                wgb, wub = wg_sb[mi % 2], wu_sb[mi % 2]

                @block.gpsimd
                def _(gp, ci=ci, mi=mi, wgb=wgb, wub=wub):
                    if mi >= 2:
                        gp.wait_ge(MM.sem, mm_end[mi - 2])
                    elif ci > 0:
                        gp.wait_ge(MM.sem, MM.n)
                    gp.dma_start(out=wgb[:], in_=w_gate[mi]).then_inc(LDC.sem, 16)
                    gp.dma_start(out=wub[:], in_=w_up[mi]).then_inc(LDC.sem, 16)
                LDC.n += 32

                @block.tensor
                def _(tensor, ci=ci, mi=mi, nw=nw, wgb=wgb, wub=wub):
                    tensor.wait_ge(LDC.sem, LDC.n)
                    tensor.wait_ge(LDD.sem, LDD.n)
                    if mi >= 2:
                        tensor.wait_ge(SCA.sem, sca_end[mi - 2])
                        tensor.wait_ge(EVB.sem, evb_end[mi - 2])
                    elif ci > 0:
                        tensor.wait_ge(SCA.sem, SCA.n)
                        tensor.wait_ge(EVB.sem, EVB.n)
                    gb, ub = pb[2 + mi % 2], pb[4 + mi % 2]
                    for k in range(K16):
                        tensor.matmul(gb.ap()[:, 0:nw], wgb[:, k * 128:(k + 1) * 128],
                                      v2k[:, k * 512:k * 512 + nw], start=(k == 0), stop=(k == K16 - 1))
                    for k in range(K16):
                        ins = tensor.matmul(ub.ap()[:, 0:nw], wub[:, k * 128:(k + 1) * 128],
                                            v2k[:, k * 512:k * 512 + nw], start=(k == 0), stop=(k == K16 - 1))
                    MM.inc(ins)
                mm_end.append(MM.n)

                @block.scalar
                def _(scalar, mi=mi, nw=nw):
                    scalar.wait_ge(MM.sem, mm_end[mi])
                    if mi >= 1:
                        scalar.wait_ge(EVB.sem, evb_end[mi - 1])
                    SCA.inc(scalar.activation(gate_t[mi % 2][:, 0:nw], pb[2 + mi % 2].ap()[:, 0:nw], AF.Silu,
                                              scale=sg_sb[:, mi:mi + 1]))
                sca_end.append(SCA.n)

                @block.vector
                def _(vector, mi=mi, nw=nw):
                    vector.wait_ge(SCA.sem, sca_end[mi])
                    EVB.inc(vector.scalar_tensor_tensor(fuse[:, mi * 512:mi * 512 + nw],
                                                        pb[4 + mi % 2].ap()[:, 0:nw], su_sb[:, mi:mi + 1],
                                                        gate_t[mi % 2][:, 0:nw], ALU.mult, ALU.mult))
                evb_end.append(EVB.n)

            dn_end, evc_end, trp_end = [], [], []

            def emit_dn_transposes(mh, nw=nw):
                for tt in range(nw // 128):
                    @block.tensor
                    def _(tensor, mh=mh, tt=tt):
                        tensor.wait_ge(EVC.sem, evc_end[mh])
                        if TRP.n >= 2:
                            tensor.wait_ge(TCO.sem, TRP.n - 1)
                        TRP.inc(tensor.transpose(ptf[TRP.n % 2][:, 0:128],
                                                 ds_t[mh % 2][:, tt * 128:(tt + 1) * 128], ident_f[:]))

                    @block.vector
                    def _(vector, mh=mh, tt=tt):
                        vector.wait_ge(TRP.sem, TRP.n)
                        vector.wait_ge(LDB.sem, LDB.n)
                        TCO.inc(vector.tensor_tensor(x1f[tt][:, mh * 128:(mh + 1) * 128],
                                                     ptf[(TRP.n - 1) % 2][:, 0:128],
                                                     x1f[tt][:, mh * 128:(mh + 1) * 128], ALU.add))
                trp_end.append(TRP.n)

            for mh in range(K16):
                wdb = wd_sb[mh % 2]

                @block.scalar
                def _(scalar, ci=ci, mh=mh, wdb=wdb):
                    if mh >= 2:
                        scalar.wait_ge(MM.sem, dn_end[mh - 2])
                    elif ci > 0:
                        scalar.wait_ge(MM.sem, MM.n)
                    else:
                        scalar.wait_ge(LDA.sem, LDA.n)
                    scalar.dma_start(out=wdb[:], in_=wd_bf_scr.ap()[mh]).then_inc(LDC.sem, 16)
                LDC.n += 16

                bank = pb[mh % 2]

                @block.tensor
                def _(tensor, ci=ci, mh=mh, nw=nw, bank=bank, wdb=wdb):
                    tensor.wait_ge(LDC.sem, LDC.n)
                    tensor.wait_ge(EVB.sem, EVB.n)
                    if mh >= 2:
                        tensor.wait_ge(EVC.sem, evc_end[mh - 2])
                    elif ci > 0:
                        tensor.wait_ge(EVC.sem, EVC.n)
                    for k in range(MI):
                        ins = tensor.matmul(bank.ap()[:, 0:nw], wdb[:, k * 128:(k + 1) * 128],
                                            fuse[:, k * 512:k * 512 + nw], start=(k == 0), stop=(k == MI - 1))
                    MM.inc(ins)
                dn_end.append(MM.n)

                @block.vector
                def _(vector, mh=mh, nw=nw, bank=bank):
                    vector.wait_ge(MM.sem, dn_end[mh])
                    if mh >= 2:
                        vector.wait_ge(TRP.sem, trp_end[mh - 2])
                    EVC.inc(vector.tensor_scalar(ds_t[mh % 2][:, 0:nw], bank.ap()[:, 0:nw], sd_sb[:, mh:mh + 1],
                                                 None, ALU.mult))
                evc_end.append(EVC.n)

                if mh >= 1:
                    emit_dn_transposes(mh - 1)
            emit_dn_transposes(K16 - 1)

            @block.gpsimd
            def _(gp, ns=ns):
                gp.wait_ge(TCO.sem, TCO.n)
                for tt in range(4):
                    gp.dma_start(out=y_out[ns + tt * 128: ns + tt * 128 + 128, :],
                                 in_=x1f[tt][:]).then_inc(OUT.sem, 16)
            OUT.n += 16 * 4

        @block.gpsimd
        def _(gp):
            gp.wait_ge(OUT.sem, OUT.n)

        g5.close()

    return nc


def stage_in_maps(inputs):
    hs = np.asarray(inputs["L0_attn_hidden_states"])
    in_proj_w = np.asarray(inputs["L0_attn_in_proj_w"])
    conv_w = np.asarray(inputs["L0_attn_conv_w"])
    conv_b = np.asarray(inputs["L0_attn_conv_b"])
    x_proj_w = np.asarray(inputs["L0_attn_x_proj_w"])
    dt_proj_w = np.asarray(inputs["L0_attn_dt_proj_w"])
    dt_proj_b = np.asarray(inputs["L0_attn_dt_proj_b"])
    a_log = np.asarray(inputs["L0_attn_A_log"])
    d_skip = np.asarray(inputs["L0_attn_D"])
    out_proj_w = np.asarray(inputs["L0_attn_out_proj_w"])
    w_gate = np.asarray(inputs["L0_ffn_w_gate"])
    w_up = np.asarray(inputs["L0_ffn_w_up"])
    w_down = np.asarray(inputs["L0_ffn_w_down"])
    s_gate = np.asarray(inputs["L0_ffn_s_gate"])
    s_up = np.asarray(inputs["L0_ffn_s_up"])
    s_down = np.asarray(inputs["L0_ffn_s_down"])

    Ct = np.ascontiguousarray

    def colmajor(v, g):
        return Ct(v.reshape(g, 128).T)

    runid = int(np.random.default_rng().integers(1 << 16, 1 << 30))
    common = dict(
        w_inT=Ct(in_proj_w.T.reshape(K16, 128, 2 * DI // 128, 128).transpose(2, 1, 0, 3).reshape(2 * DI // 128, 128, K16 * 128)),
        conv_w=Ct(conv_w.reshape(DG, 128, DC).transpose(1, 0, 2).reshape(128, DG * DC)),
        conv_b=colmajor(conv_b, DG),
        x_projT=Ct(x_proj_w.T.reshape(DG, 128, 160).transpose(1, 0, 2).reshape(128, DG * 160)),
        dt_projT=Ct(dt_proj_w.T),
        dt_b=colmajor(dt_proj_b, DG),
        a_log=Ct(a_log.reshape(DG, 128, DS).transpose(1, 0, 2).reshape(128, DG * DS)),
        d_skip=colmajor(d_skip, DG),
        w_outT=Ct(out_proj_w.T.reshape(DG, 128, K16, 128).transpose(2, 1, 0, 3).reshape(K16, 128, DG * 128)),
        w_gate=Ct(w_gate.reshape(K16, 128, MI, 128).transpose(2, 1, 0, 3).reshape(MI, 128, K16 * 128)),
        w_up=Ct(w_up.reshape(K16, 128, MI, 128).transpose(2, 1, 0, 3).reshape(MI, 128, K16 * 128)),
        w_down=Ct(w_down.reshape(MI, 128, K16, 128).transpose(2, 1, 0, 3).reshape(K16, 128, MI * 128)),
        s_gate=colmajor(s_gate, MI),
        s_up=colmajor(s_up, MI),
        s_down=colmajor(s_down, K16),
        ident=np.eye(128, dtype=np.float32),
        runid=np.array([[runid]], np.int32),
    )

    in_maps = []
    for core in range(8):
        b, role = core // 2, core % 2
        o0 = role * S
        xh = np.zeros((W, DM), np.float32)
        if o0 >= 3:
            xh[125:128] = hs[b, o0 - 3:o0]
        xh[128:] = hs[b, o0:o0 + S]
        in_maps.append(dict(common, x0=xh, role=np.array([[role]], np.int32),
                            rolev=np.full((128, 1), float(role), np.float32)))
    return in_maps


def kernel(**inputs):
    nc = build_program()
    in_maps = stage_in_maps(inputs)
    res = run_bass_kernel_spmd(nc, in_maps, core_ids=list(range(8)))
    out = np.empty((B, L, DM), np.float32)
    for core in range(8):
        b, role = core // 2, core % 2
        out[b, role * S:(role + 1) * S] = res.results[core]["y_out"]
    return out


# version 24
# speedup vs baseline: 1.0872x; 1.0872x over previous
import numpy as np
from contextlib import ExitStack

import concourse.bass as bass
import concourse.mybir as mybir
from concourse.bass_utils import run_bass_kernel_spmd

F32 = mybir.dt.float32
BF16 = mybir.dt.bfloat16
I32 = mybir.dt.int32
AF = mybir.ActivationFunctionType
ALU = mybir.AluOpType
AX = mybir.AxisListType

B, L, DM, DI, DS, DC, DTR, INTER = 4, 2048, 2048, 4096, 16, 4, 128, 8192
DG = DI // 128
K16 = DM // 128
MI = INTER // 128
S = 1024
W = S + 128
T = S
NCHUNK = [(0, 512), (512, 512)]
ICHUNK = [(125, 3), (128, 512), (640, 512)]
NT = W // 128
EPS = 1e-5
POLL_BOUND = 2_000_000


class Ctr:
    def __init__(self, sem):
        self.sem = sem
        self.n = 0

    def inc(self, instr, k=1):
        instr.then_inc(self.sem, k)
        self.n += k
        return self.n


def build_program(dbg=False, no_hs=False, hs_poll=True, hs_send=True, hs_local=False):
    nc = bass.Bass("TRN2", target_bir_lowering=False, debug=False)
    KIND = "ExternalOutput" if dbg else "Internal"

    x0 = nc.dram_tensor("x0", [W, DM], F32, kind="ExternalInput").ap()
    w_inT = nc.dram_tensor("w_inT", [2 * DI // 128, 128, K16 * 128], F32, kind="ExternalInput").ap()
    conv_w = nc.dram_tensor("conv_w", [128, DG * DC], F32, kind="ExternalInput").ap()
    conv_b = nc.dram_tensor("conv_b", [128, DG], F32, kind="ExternalInput").ap()
    x_projT = nc.dram_tensor("x_projT", [128, DG * 160], F32, kind="ExternalInput").ap()
    dt_projT = nc.dram_tensor("dt_projT", [DTR, DI], F32, kind="ExternalInput").ap()
    dt_b = nc.dram_tensor("dt_b", [128, DG], F32, kind="ExternalInput").ap()
    a_log = nc.dram_tensor("a_log", [128, DG * DS], F32, kind="ExternalInput").ap()
    d_skip = nc.dram_tensor("d_skip", [128, DG], F32, kind="ExternalInput").ap()
    w_outT = nc.dram_tensor("w_outT", [K16, 128, DG * 128], F32, kind="ExternalInput").ap()
    w_gate = nc.dram_tensor("w_gate", [MI, 128, K16 * 128], mybir.dt.int8, kind="ExternalInput").ap()
    w_up = nc.dram_tensor("w_up", [MI, 128, K16 * 128], mybir.dt.int8, kind="ExternalInput").ap()
    w_down = nc.dram_tensor("w_down", [K16, 128, MI * 128], mybir.dt.int8, kind="ExternalInput").ap()
    s_gate = nc.dram_tensor("s_gate", [128, MI], F32, kind="ExternalInput").ap()
    s_up = nc.dram_tensor("s_up", [128, MI], F32, kind="ExternalInput").ap()
    s_down = nc.dram_tensor("s_down", [128, K16], F32, kind="ExternalInput").ap()
    ident_in = nc.dram_tensor("ident", [128, 128], F32, kind="ExternalInput").ap()
    role_in = nc.dram_tensor("role", [1, 1], I32, kind="ExternalInput").ap()
    rolev_in = nc.dram_tensor("rolev", [128, 1], F32, kind="ExternalInput").ap()
    runid_in = nc.dram_tensor("runid", [1, 1], I32, kind="ExternalInput").ap()

    y_out = nc.dram_tensor("y_out", [S, DM], F32, kind="ExternalOutput").ap()

    xpre_scr = nc.dram_tensor("xpre_scr", [DI, S + 3], BF16, kind=KIND)
    z_scr = nc.dram_tensor("z_scr", [DI, S], BF16, kind=KIND)
    zs_scr = nc.dram_tensor("zs_scr", [DI, S], BF16, kind=KIND)
    xc_scr = nc.dram_tensor("xc_scr", [DI, S], BF16, kind=KIND)
    delta_scr = nc.dram_tensor("delta_scr", [DI, S], F32, kind=KIND)
    bc_scr = nc.dram_tensor("bc_scr", [2 * DS, S], F32, kind=KIND)
    yg_scr = nc.dram_tensor("yg_scr", [DI, S], BF16, kind=KIND)
    x1_scr = nc.dram_tensor("x1_scr", [S, DM], F32, kind=KIND)
    v2t_scr = nc.dram_tensor("v2t_scr", [DM, S], BF16, kind=KIND)
    wo_bf_scr = nc.dram_tensor("wo_bf_scr", [K16, 128, DG * 128], BF16, kind=KIND)
    wd_bf_scr = nc.dram_tensor("wd_bf_scr", [K16, 128, MI * 128], BF16, kind=KIND)

    if hs_local:
        sh_carry = nc.dram_tensor("sh_carry", [128, DG * DS], F32)
        sh_flag = nc.dram_tensor("sh_flag", [1, DG * 16], I32)
    else:
        sh_carry = nc.dram_tensor("sh_carry", [128, DG * DS], F32, addr_space="Shared")
        sh_flag = nc.dram_tensor("sh_flag", [1, DG * 16], I32, addr_space="Shared")
    poll_scr = nc.dram_tensor("poll_scr", [1, DG + 2], I32, kind=KIND)

    ctx = ExitStack()
    with ctx:
        sbuf = lambda name, shape, dt: ctx.enter_context(nc.sbuf_tensor(name, shape, dt))
        ident_f = sbuf("ident_f", [128, 128], F32)
        ident_b = sbuf("ident_b", [128, 128], BF16)
        a_sb = sbuf("a_sb", [128, DG * DS], F32)
        dsk_sb = sbuf("dsk_sb", [128, DG], F32)
        cw_sb = sbuf("cw_sb", [128, DG * DC], F32)
        cb_sb = sbuf("cb_sb", [128, DG], F32)
        dtb_sb = sbuf("dtb_sb", [128, DG], F32)
        sg_sb = sbuf("sg_sb", [128, MI], F32)
        su_sb = sbuf("su_sb", [128, MI], F32)
        sd_sb = sbuf("sd_sb", [128, K16], F32)
        carry = sbuf("carry", [128, DG * DS], F32)
        role_sb = sbuf("role_sb", [1, 1], I32)
        rolev_sb = sbuf("rolev_sb", [128, 1], F32)
        runid_sb = sbuf("runid_sb", [1, 1], I32)
        flag_sb = sbuf("flag_sb", [1, 1], I32)
        pstat_sb = sbuf("pstat_sb", [1, DG + 2], I32)
        cstg = sbuf("cstg", [128, DS], F32)
        ctmp = sbuf("ctmp", [128, DS], F32)
        stgb = [sbuf(f"stgb{i}", [128, 512], BF16) for i in range(5)]
        lnt1 = sbuf("lnt1", [128, 1], F32)
        lnt2 = sbuf("lnt2", [128, 1], F32)
        lnc1 = sbuf("lnc1", [128, 1], F32)
        lnc2 = sbuf("lnc2", [128, 1], F32)
        junk = sbuf("junk", [128, 2048], BF16)

        pb = [ctx.enter_context(nc.psum_tensor(f"pb{i}", [128, 512], F32)) for i in range(6)]
        pt = [ctx.enter_context(nc.psum_tensor(f"pt{i}", [128, 1024], BF16)) for i in range(2)]
        ptf = [p.ap().bitcast(F32) for p in pt]

        C = {}
        for name in ["LDA", "LDB", "LDC", "LDD", "MM", "EVA", "EVB", "EVC", "SCA",
                     "VEC", "VSY", "GST", "BAR", "TRP", "TCO", "OUT", "VSC", "CAR", "PSM", "SND", "GPX"]:
            C[name] = Ctr(ctx.enter_context(nc.semaphore(name)))
        LDA, LDB, LDC, LDD = C["LDA"], C["LDB"], C["LDC"], C["LDD"]
        MM, EVA, EVB, EVC = C["MM"], C["EVA"], C["EVB"], C["EVC"]
        SCA, VEC, VSY = C["SCA"], C["VEC"], C["VSY"]
        GST, BAR, TRP, TCO, OUT, VSC = C["GST"], C["BAR"], C["TRP"], C["TCO"], C["OUT"], C["VSC"]
        CAR, PSM, SND, GPX = C["CAR"], C["PSM"], C["SND"], C["GPX"]

        block = ctx.enter_context(nc.Block())
        ENGS = ["tensor", "vector", "scalar", "gpsimd", "sync"]

        def barrier():
            tgt = BAR.n + 5
            for en in ENGS:
                def mk(en=en, tgt=tgt):
                    def _(e):
                        BAR.inc(e.nop())
                        e.wait_ge(BAR.sem, tgt)
                    return _
                getattr(block, en)(mk())

        @block.sync
        def _(sync):
            for dst, src in [(ident_f, ident_in), (cw_sb, conv_w), (cb_sb, conv_b),
                             (dtb_sb, dt_b), (a_sb, a_log), (dsk_sb, d_skip),
                             (sg_sb, s_gate), (su_sb, s_up), (sd_sb, s_down),
                             (role_sb, role_in), (runid_sb, runid_in), (rolev_sb, rolev_in)]:
                sync.dma_start(out=dst[:], in_=src).then_inc(LDA.sem, 16)
        LDA.n += 16 * 12

        @block.scalar
        def _(scalar):
            scalar.wait_ge(LDA.sem, LDA.n)
            SCA.inc(scalar.activation(a_sb[:], a_sb[:], AF.Exp))

        @block.vector
        def _(vector):
            vector.wait_ge(SCA.sem, SCA.n)
            VEC.inc(vector.tensor_scalar(a_sb[:], a_sb[:], -1.0, None, ALU.mult))
            VEC.inc(vector.memset(carry[:], 0.0))
            VEC.inc(vector.tensor_copy(ident_b[:], ident_f[:]))

        g1 = ExitStack()
        lnxT = g1.enter_context(nc.sbuf_tensor("lnxT", [128, K16 * W], BF16))
        x0t = [g1.enter_context(nc.sbuf_tensor(f"x0t{i}", [128, DM], F32)) for i in range(2)]
        lnxt = g1.enter_context(nc.sbuf_tensor("lnxt", [128, DM], BF16))
        wpool = [g1.enter_context(nc.sbuf_tensor(f"wpool{i}", [128, K16 * 128], BF16)) for i in range(3)]
        wstg = [g1.enter_context(nc.sbuf_tensor(f"wstg{i}", [128, K16 * 128], F32)) for i in range(2)]

        for tt in range(NT):
            buf = x0t[tt % 2]

            @block.sync
            def _(sync, tt=tt, buf=buf):
                if tt >= 2:
                    sync.wait_ge(TRP.sem, (tt - 1) * K16)
                sync.dma_start(out=buf[:], in_=x0[tt * 128:(tt + 1) * 128, :]).then_inc(LDB.sem, 16)
            LDB.n += 16

            @block.vector
            def _(vector, buf=buf):
                vector.wait_ge(LDB.sem, LDB.n)
                VEC.inc(vector.tensor_reduce(lnt1[:], buf[:], AX.X, ALU.add))

            @block.scalar
            def _(scalar, buf=buf):
                scalar.wait_ge(LDB.sem, LDB.n)
                scalar.wait_ge(VEC.sem, VEC.n)
                SCA.inc(scalar.activation(junk[:], buf[:], AF.Square, accum_out=lnt2[:]))

            @block.vector
            def _(vector):
                vector.wait_ge(SCA.sem, SCA.n)
                VEC.inc(vector.tensor_scalar(lnt1[:], lnt1[:], 1.0 / DM, None, ALU.mult))
                vector.wait_ge(VEC.sem, VEC.n)
                VEC.inc(vector.tensor_scalar(lnt2[:], lnt2[:], 1.0 / DM, None, ALU.mult))
                vector.wait_ge(VEC.sem, VEC.n)
                VEC.inc(vector.scalar_tensor_tensor(lnc1[:], lnt1[:], lnt1[:], lnt2[:], ALU.mult, ALU.subtract))
                vector.wait_ge(VEC.sem, VEC.n)
                VEC.inc(vector.tensor_scalar(lnc1[:], lnc1[:], -1.0, EPS, ALU.mult, ALU.add))

            @block.scalar
            def _(scalar):
                scalar.wait_ge(VEC.sem, VEC.n)
                SCA.inc(scalar.activation(lnc1[:], lnc1[:], AF.Ln))
                scalar.wait_ge(SCA.sem, SCA.n)
                SCA.inc(scalar.activation(lnc1[:], lnc1[:], AF.Exp, scale=-0.5))

            @block.vector
            def _(vector, tt=tt, buf=buf):
                vector.wait_ge(SCA.sem, SCA.n)
                VEC.inc(vector.tensor_scalar(lnc2[:], lnt1[:], -1.0, lnc1[:], ALU.mult, ALU.mult))
                vector.wait_ge(VEC.sem, VEC.n)
                if tt >= 1:
                    vector.wait_ge(TRP.sem, tt * K16)
                VEC.inc(vector.tensor_scalar(lnxt[:], buf[:], lnc1[:], lnc2[:], ALU.mult, ALU.add))

            for k in range(K16):
                @block.tensor
                def _(tensor, tt=tt, k=k):
                    tensor.wait_ge(VEC.sem, VEC.n)
                    if TRP.n >= 2:
                        tensor.wait_ge(TCO.sem, TRP.n - 1)
                    TRP.inc(tensor.transpose(pt[TRP.n % 2].ap()[:, 0:128], lnxt[:, k * 128:(k + 1) * 128], ident_b[:]))

                @block.scalar
                def _(scalar, tt=tt, k=k):
                    scalar.wait_ge(TRP.sem, TRP.n)
                    TCO.inc(scalar.copy(lnxT[:, k * W + tt * 128: k * W + tt * 128 + 128],
                                        pt[(TRP.n - 1) % 2].ap()[:, 0:128]))

        TCO_P1 = TCO.n
        xpt = [g1.enter_context(nc.sbuf_tensor(f"xpt{i}", [128, S + 3], BF16)) for i in range(2)]
        acc_t = g1.enter_context(nc.sbuf_tensor("acc_t", [128, S], F32))
        xct = g1.enter_context(nc.sbuf_tensor("xct", [128, S], BF16))
        cstore_gst = []
        csil_sca = []

        def emit_conv(dg):
            xb = xpt[dg % 2]

            @block.sync
            def _(sync, dg=dg, xb=xb):
                sync.wait_ge(GST.sem, xpre_gst[dg])
                if dg >= 2:
                    sync.wait_ge(VEC.sem, VEC.n)
                sync.dma_start(out=xb[:], in_=xpre_scr.ap()[dg * 128:(dg + 1) * 128, :]).then_inc(LDD.sem, 16)
            LDD.n += 16

            @block.vector
            def _(vector, dg=dg, xb=xb):
                vector.wait_ge(LDD.sem, LDD.n)
                if dg >= 1:
                    vector.wait_ge(SCA.sem, csil_sca[dg - 1])
                VEC.inc(vector.tensor_scalar(acc_t[:], xb[:, 0:S], cw_sb[:, dg * DC: dg * DC + 1],
                                             cb_sb[:, dg:dg + 1], ALU.mult, ALU.add))
                for k in range(1, DC):
                    vector.wait_ge(VEC.sem, VEC.n)
                    VEC.inc(vector.scalar_tensor_tensor(acc_t[:], xb[:, k:k + S],
                                                        cw_sb[:, dg * DC + k: dg * DC + k + 1],
                                                        acc_t[:], ALU.mult, ALU.add))

            @block.scalar
            def _(scalar, dg=dg):
                scalar.wait_ge(VEC.sem, VEC.n)
                if dg >= 1:
                    scalar.wait_ge(GST.sem, cstore_gst[dg - 1])
                ins = scalar.activation(xct[:], acc_t[:], AF.Silu)
                SCA.inc(ins)
                scalar.wait_ge(SCA.sem, SCA.n)
                scalar.dma_start(out=xc_scr.ap()[dg * 128:(dg + 1) * 128, :], in_=xct[:]).then_inc(GST.sem, 16)
            csil_sca.append(SCA.n)
            GST.n += 16
            cstore_gst.append(GST.n)

        mm_g = 0
        inproj_end = []
        xpre_gst = []
        cast_marks = []
        store_marks = []
        NM = 2 * DI // 128

        def emit_wload(m):
            ws = wstg[m % 2]
            wb = wpool[m % 3]

            @block.sync
            def _(sync, m=m, ws=ws):
                if m >= 2:
                    sync.wait_ge(VEC.sem, cast_marks[m - 2])
                sync.dma_start(out=ws[:], in_=w_inT[m]).then_inc(LDC.sem, 16)
            LDC.n += 16

            @block.vector
            def _(vector, m=m, wb=wb, ws=ws):
                vector.wait_ge(LDC.sem, LDC.n)
                if m >= 3:
                    vector.wait_ge(MM.sem, inproj_end[m - 3])
                VEC.inc(vector.tensor_copy(wb[:], ws[:]))
            cast_marks.append(VEC.n)

        emit_wload(0)
        emit_wload(1)
        for m in range(NM):
            wb = wpool[m % 3]
            chunks = ICHUNK if m < DG else ICHUNK[1:]
            if m + 2 < NM:
                emit_wload(m + 2)

            for (ns, nw) in chunks:
                g = mm_g
                bank = pb[g % 4]

                @block.tensor
                def _(tensor, m=m, ns=ns, nw=nw, g=g, bank=bank, wb=wb):
                    tensor.wait_ge(VEC.sem, cast_marks[m])
                    tensor.wait_ge(TCO.sem, TCO_P1)
                    if g >= 4:
                        tensor.wait_ge(EVB.sem, g - 3)
                    for k in range(K16):
                        ins = tensor.matmul(bank.ap()[:, 0:nw], wb[:, k * 128:(k + 1) * 128],
                                            lnxT[:, k * W + ns: k * W + ns + nw],
                                            start=(k == 0), stop=(k == K16 - 1))
                    MM.inc(ins)

                if m < DG:
                    @block.vector
                    def _(vector, m=m, g=g, nw=nw, bank=bank):
                        vector.wait_ge(MM.sem, MM.n)
                        if g >= 5:
                            vector.wait_ge(GST.sem, store_marks[g - 5])
                        EVB.inc(vector.tensor_copy(stgb[g % 5][:, 0:nw], bank.ap()[:, 0:nw]))
                else:
                    @block.scalar
                    def _(scalar, m=m, g=g, nw=nw, bank=bank):
                        scalar.wait_ge(MM.sem, MM.n)
                        if g >= 5:
                            scalar.wait_ge(GST.sem, store_marks[g - 5])
                        EVB.inc(scalar.activation(stgb[g % 5][:, 0:nw], bank.ap()[:, 0:nw], AF.Silu))

                @block.gpsimd
                def _(gp, m=m, ns=ns, nw=nw, g=g):
                    gp.wait_ge(EVB.sem, EVB.n)
                    sg = stgb[g % 5]
                    if m < DG:
                        dst = xpre_scr.ap()[m * 128:(m + 1) * 128, ns - 125: ns - 125 + nw]
                    else:
                        dst = zs_scr.ap()[(m - DG) * 128:(m - DG + 1) * 128, ns - 128: ns - 128 + nw]
                    gp.dma_start(out=dst, in_=sg[:, 0:nw]).then_inc(GST.sem, 16)
                GST.n += 16
                store_marks.append(GST.n)
                mm_g += 1
            inproj_end.append(MM.n)
            if m < DG:
                xpre_gst.append(GST.n)
                if m >= 1:
                    emit_conv(m - 1)
            if m == DG:
                emit_conv(DG - 1)

        g2 = ExitStack()
        xp_sb = g2.enter_context(nc.sbuf_tensor("xp_sb", [128, DG * 160], BF16))
        dtw_sb = g2.enter_context(nc.sbuf_tensor("dtw_sb", [128, DG * 128], BF16))
        dtlow = g2.enter_context(nc.sbuf_tensor("dtlow", [128, S], BF16))
        bc_sb = g2.enter_context(nc.sbuf_tensor("bc_sb", [32, S], F32))
        xcr = [g2.enter_context(nc.sbuf_tensor(f"xcr{i}", [128, 512], BF16)) for i in range(2)]
        sp1 = g2.enter_context(nc.sbuf_tensor("sp1", [128, 512], F32))
        sp2 = g2.enter_context(nc.sbuf_tensor("sp2", [128, 512], F32))
        sp3 = g2.enter_context(nc.sbuf_tensor("sp3", [128, 512], F32))

        @block.gpsimd
        def _(gp):
            gp.dma_start(out=xp_sb[:], in_=x_projT).then_inc(LDC.sem, 16)
            gp.dma_start(out=dtw_sb[:], in_=dt_projT).then_inc(LDC.sem, 16)
        LDC.n += 32

        xp_mm0 = MM.n
        for ci, (ns, nw) in enumerate(NCHUNK):
            ld0 = LDD.n

            @block.sync
            def _(sync, ci=ci, ns=ns, nw=nw, ld0=ld0):
                sync.wait_ge(GST.sem, GST.n)
                for k in range(DG):
                    if k >= 2:
                        sync.wait_ge(MM.sem, xp_mm0 + ci * DG + k - 1)
                    elif ci > 0:
                        sync.wait_ge(MM.sem, xp_mm0 + ci * DG)
                    sync.dma_start(out=xcr[k % 2][:, 0:nw],
                                   in_=xc_scr.ap()[k * 128:(k + 1) * 128, ns:ns + nw]).then_inc(LDD.sem, 16)

            @block.tensor
            def _(tensor, ci=ci, ns=ns, nw=nw, ld0=ld0):
                tensor.wait_ge(LDC.sem, LDC.n)
                if ci >= 1:
                    tensor.wait_ge(EVB.sem, EVB.n)
                for k in range(DG):
                    tensor.wait_ge(LDD.sem, ld0 + (k + 1) * 16)
                    tensor.matmul(pb[4].ap()[:, 0:nw], xp_sb[:, k * 160:k * 160 + 128],
                                  xcr[k % 2][:, 0:nw], start=(k == 0), stop=(k == DG - 1))
                    ins = tensor.matmul(pb[5].ap()[0:32, 0:nw], xp_sb[:, k * 160 + 128:(k + 1) * 160],
                                        xcr[k % 2][:, 0:nw], start=(k == 0), stop=(k == DG - 1))
                    MM.inc(ins)
            LDD.n += 16 * DG

            @block.vector
            def _(vector, ns=ns, nw=nw):
                vector.wait_ge(MM.sem, MM.n)
                VEC.inc(vector.tensor_copy(dtlow[:, ns:ns + nw], pb[4].ap()[:, 0:nw]))
                vector.wait_ge(VEC.sem, VEC.n)
                EVB.inc(vector.tensor_copy(bc_sb[0:32, ns:ns + nw], pb[5].ap()[0:32, 0:nw]))

        @block.gpsimd
        def _(gp):
            gp.wait_ge(EVB.sem, EVB.n)
            gp.dma_start(out=bc_scr.ap(), in_=bc_sb[0:32, :]).then_inc(GST.sem, 16)
        GST.n += 16

        for m in range(DG):
            for ci, (ns, nw) in enumerate(NCHUNK):
                g = m * len(NCHUNK) + ci
                bank = pb[g % 4]

                @block.tensor
                def _(tensor, m=m, ns=ns, nw=nw, g=g, bank=bank):
                    tensor.wait_ge(LDC.sem, LDC.n)
                    tensor.wait_ge(VEC.sem, VEC.n)
                    if g >= 4:
                        tensor.wait_ge(EVC.sem, EVC.n)
                    ins = tensor.matmul(bank.ap()[:, 0:nw], dtw_sb[:, m * 128:(m + 1) * 128],
                                        dtlow[:, ns:ns + nw], start=True, stop=True)
                    MM.inc(ins)

                @block.scalar
                def _(scalar, m=m, nw=nw, bank=bank):
                    scalar.wait_ge(MM.sem, MM.n)
                    scalar.wait_ge(EVC.sem, EVC.n)
                    SCA.inc(scalar.activation(sp1[:, 0:nw], bank.ap()[:, 0:nw], AF.Abs, bias=dtb_sb[:, m:m + 1]))
                    scalar.wait_ge(SCA.sem, SCA.n)
                    SCA.inc(scalar.activation(sp2[:, 0:nw], sp1[:, 0:nw], AF.Exp, scale=-1.0))
                    scalar.wait_ge(SCA.sem, SCA.n)
                    SCA.inc(scalar.activation(sp2[:, 0:nw], sp2[:, 0:nw], AF.Ln, bias=1.0))

                @block.vector
                def _(vector, m=m, nw=nw, bank=bank):
                    vector.wait_ge(MM.sem, MM.n)
                    vector.wait_ge(GST.sem, GST.n)
                    VEC.inc(vector.tensor_scalar(sp3[:, 0:nw], bank.ap()[:, 0:nw], dtb_sb[:, m:m + 1],
                                                 0.0, ALU.add, ALU.max))
                    vector.wait_ge(SCA.sem, SCA.n)
                    vector.wait_ge(VEC.sem, VEC.n)
                    EVC.inc(vector.tensor_tensor(sp3[:, 0:nw], sp3[:, 0:nw], sp2[:, 0:nw], ALU.add))

                @block.gpsimd
                def _(gp, m=m, ns=ns, nw=nw):
                    gp.wait_ge(EVC.sem, EVC.n)
                    gp.dma_start(out=delta_scr.ap()[m * 128:(m + 1) * 128, ns:ns + nw],
                                 in_=sp3[:, 0:nw]).then_inc(GST.sem, 16)
                GST.n += 16

        @block.sync
        def _(sync):
            sync.wait_ge(GST.sem, GST.n)
        barrier()
        g2.close()
        g1.close()

        g3 = ExitStack()
        brep = g3.enter_context(nc.sbuf_tensor("brep", [128, DS * T], BF16))
        crep = g3.enter_context(nc.sbuf_tensor("crep", [128, DS * T], BF16))
        dl_t = [g3.enter_context(nc.sbuf_tensor(f"dl{i}", [128, T], F32)) for i in range(2)]
        xc_t = [g3.enter_context(nc.sbuf_tensor(f"xcs{i}", [128, T], BF16)) for i in range(3)]
        z_t = [g3.enter_context(nc.sbuf_tensor(f"zt{i}", [128, T], BF16)) for i in range(3)]
        dxu_t = g3.enter_context(nc.sbuf_tensor("dxu", [128, T], BF16))
        da_f = g3.enter_context(nc.sbuf_tensor("da_f", [128, DS * T], BF16))
        xb_f = g3.enter_context(nc.sbuf_tensor("xb_f", [128, DS * T], BF16))
        h_sb = g3.enter_context(nc.sbuf_tensor("h_sb", [128, DS * T], BF16))
        htmp = g3.enter_context(nc.sbuf_tensor("htmp", [128, 4 * T], BF16))
        yd_t = g3.enter_context(nc.sbuf_tensor("yd", [128, T], BF16))

        regs = ExitStack()
        r_role = regs.enter_context(nc.sync.register("r_role"))
        r_run = regs.enter_context(nc.sync.register("r_run"))
        r_flag = regs.enter_context(nc.sync.register("r_flag"))
        r_go = regs.enter_context(nc.sync.register("r_go"))
        r_it = regs.enter_context(nc.sync.register("r_it"))
        r_tmp = regs.enter_context(nc.sync.register("r_tmp"))

        @block.sync
        def _(gp):
            gp.wait_ge(LDA.sem, 16 * 12)
            gp.reg_load(r_role, role_sb[0:1, 0:1])
            gp.reg_load(r_run, runid_sb[0:1, 0:1])
            gp.store(pstat_sb[0:1, DG:DG + 1], r_role)
            gp.store(pstat_sb[0:1, DG + 1:DG + 2], r_run)

        @block.gpsimd
        def _(gp):
            src_b = bc_scr.ap()[0:DS, 0:T].partition_broadcast(128)
            src_c = bc_scr.ap()[DS:2 * DS, 0:T].partition_broadcast(128)
            gp.dma_start(out=brep[:].rearrange("p (n t) -> p n t", t=T), in_=src_b).then_inc(LDC.sem, 16)
            gp.dma_start(out=crep[:].rearrange("p (n t) -> p n t", t=T), in_=src_c).then_inc(LDC.sem, 16)
        LDC.n += 32
        ldc_brep = LDC.n

        vsc_end, gpx_end, dg_sca = [], [], []
        for dg in range(DG):
            it = dg
            dl, xc_b, z_b = dl_t[it % 2], xc_t[it % 3], z_t[it % 3]
            vsc0 = VSC.n

            @block.sync
            def _(sync, dg=dg, dl=dl, xc_b=xc_b, z_b=z_b, it=it):
                if it >= 2:
                    sync.wait_ge(VSC.sem, vsc_end[it - 2])
                    sync.wait_ge(SCA.sem, dg_sca[it - 2])
                sync.dma_start(out=dl[:], in_=delta_scr.ap()[dg * 128:(dg + 1) * 128, :]).then_inc(LDD.sem, 16)
                if it >= 3:
                    sync.wait_ge(VSY.sem, it - 2)
                sync.dma_start(out=xc_b[:], in_=xc_scr.ap()[dg * 128:(dg + 1) * 128, :]).then_inc(LDD.sem, 16)
                sync.dma_start(out=z_b[:], in_=zs_scr.ap()[dg * 128:(dg + 1) * 128, :]).then_inc(LDD.sem, 16)
            LDD.n += 48

            @block.sync
            def _(gp, dg=dg):
                if no_hs or not hs_poll:
                    CAR.inc(gp.nop())
                    return
                gp.reg_mov(r_go, r_role)
                gp.reg_mov(r_it, 0)
                with gp.While(r_go):
                    gp.sem_clear(PSM.sem)
                    gp.dma_start(out=flag_sb[:], in_=sh_flag.ap()[0:1, dg * 16:dg * 16 + 1]).then_inc(PSM.sem, 16)
                    gp.wait_ge(PSM.sem, 16)
                    gp.reg_load(r_flag, flag_sb[0:1, 0:1])
                    gp.reg_alu(r_it, r_it, 1, ALU.add)
                    gp.reg_alu(r_go, r_flag, r_run, ALU.not_equal)
                    gp.reg_alu(r_tmp, r_it, POLL_BOUND, ALU.is_lt)
                    gp.reg_alu(r_go, r_go, r_tmp, ALU.mult)
                    gp.reg_alu(r_go, r_go, r_role, ALU.mult)
                gp.store(pstat_sb[0:1, dg:dg + 1], r_it)
                gp.reg_mov(r_go, r_role)
                with gp.While(r_go):
                    gp.sem_clear(PSM.sem)
                    ins = gp.dma_start(out=carry[:, dg * DS:(dg + 1) * DS],
                                       in_=sh_carry.ap()[:, dg * DS:(dg + 1) * DS])
                    ins.then_inc(PSM.sem, 16)
                    gp.wait_ge(PSM.sem, 16)
                    gp.reg_mov(r_go, 0)
                CAR.inc(gp.nop())

            @block.vector
            def _(vector, dg=dg, dl=dl, xc_b=xc_b, ldd_now=LDD.n):
                vector.wait_ge(LDD.sem, ldd_now)
                vector.wait_ge(LDC.sem, ldc_brep)
                VEC.inc(vector.tensor_tensor(dxu_t[:], dl[:], xc_b[:], ALU.mult))
                for n in range(DS):
                    ins = vector.tensor_tensor(xb_f[:, n * T:(n + 1) * T], dxu_t[:],
                                               brep[:, n * T:(n + 1) * T], ALU.mult)
                VEC.inc(ins)

            sca_marks = []
            @block.scalar
            def _(scalar, dg=dg, dl=dl):
                scalar.wait_ge(LDD.sem, LDD.n)
                for half in range(2):
                    if dg >= 1:
                        scalar.wait_ge(VSC.sem, vsc_end[dg - 1] - 8 + 8 * half)
                    for n in range(half * 8, half * 8 + 8):
                        ins = scalar.activation(da_f[:, n * T:(n + 1) * T], dl[:],
                                                AF.Exp, scale=a_sb[:, dg * DS + n: dg * DS + n + 1])
                    SCA.inc(ins)
                    sca_marks.append(SCA.n)
            dg_sca.append(SCA.n)

            @block.vector
            def _(vector, dg=dg):
                vector.wait_ge(CAR.sem, CAR.n)
                for n in range(DS):
                    if n % 8 == 0:
                        vector.wait_ge(SCA.sem, sca_marks[n // 8])
                    VSC.inc(vector.tensor_tensor_scan(h_sb[:, n * T:(n + 1) * T],
                                                      da_f[:, n * T:(n + 1) * T],
                                                      xb_f[:, n * T:(n + 1) * T],
                                                      carry[:, dg * DS + n: dg * DS + n + 1],
                                                      ALU.mult, ALU.add))
            vsc_end.append(VSC.n)

            @block.vector
            def _(vector, dg=dg):
                vector.wait_ge(VSC.sem, VSC.n)
                VEC.inc(vector.tensor_copy(carry[:, dg * DS:(dg + 1) * DS], h_sb[:, T - 1::T]))

            carx = VEC.n

            @block.sync
            def _(gp, dg=dg, carx=carx):
                if no_hs or not hs_send:
                    SND.inc(gp.nop())
                    return
                gp.wait_ge(VEC.sem, carx)
                gp.reg_alu(r_go, r_role, 0, ALU.is_equal)
                with gp.While(r_go):
                    gp.sem_clear(PSM.sem)
                    gp.dma_start(out=sh_carry.ap()[:, dg * DS:(dg + 1) * DS],
                                 in_=carry[:, dg * DS:(dg + 1) * DS]).then_inc(PSM.sem, 16)
                    gp.wait_ge(PSM.sem, 16)
                    gp.sem_clear(PSM.sem)
                    gp.dma_start(out=sh_flag.ap()[0:1, dg * 16:dg * 16 + 1], in_=runid_sb[:]).then_inc(PSM.sem, 16)
                    gp.wait_ge(PSM.sem, 16)
                    gp.reg_mov(r_go, 0)
                SND.inc(gp.nop())

            @block.vector
            def _(vector, dg=dg):
                vector.wait_ge(VEC.sem, VEC.n)
                for q in range(4):
                    q0 = q * 4 * T
                    VEC.inc(vector.tensor_tensor(htmp[:], h_sb[:, q0:q0 + 4 * T],
                                                 crep[:, q0:q0 + 4 * T], ALU.mult))
                    vector.wait_ge(VEC.sem, VEC.n)
                    VEC.inc(vector.tensor_tensor(h_sb[:, q0:q0 + 2 * T], htmp[:, 0:2 * T],
                                                 htmp[:, 2 * T:4 * T], ALU.add))
                    vector.wait_ge(VEC.sem, VEC.n)
                    VEC.inc(vector.tensor_tensor(htmp[:, 0:T], h_sb[:, q0:q0 + T],
                                                 h_sb[:, q0 + T:q0 + 2 * T], ALU.add))
                    vector.wait_ge(VEC.sem, VEC.n)
                    if q == 0:
                        VEC.inc(vector.tensor_copy(h_sb[:, 0:T], htmp[:, 0:T]))
                    else:
                        VEC.inc(vector.tensor_tensor(h_sb[:, 0:T], h_sb[:, 0:T], htmp[:, 0:T], ALU.add))
                    vector.wait_ge(VEC.sem, VEC.n)

            @block.vector
            def _(vector, dg=dg, xc_b=xc_b, z_b=z_b):
                vector.wait_ge(VEC.sem, VEC.n)
                vector.wait_ge(GST.sem, GST.n)
                VEC.inc(vector.scalar_tensor_tensor(yd_t[:], xc_b[:], dsk_sb[:, dg:dg + 1],
                                                    h_sb[:, 0:T], ALU.mult, ALU.add))
                vector.wait_ge(VEC.sem, VEC.n)
                VSY.inc(vector.tensor_tensor(yd_t[:], yd_t[:], z_b[:], ALU.mult))

            @block.gpsimd
            def _(gp, dg=dg):
                gp.wait_ge(VSY.sem, VSY.n)
                gp.dma_start(out=yg_scr.ap()[dg * 128:(dg + 1) * 128, :],
                             in_=yd_t[:]).then_inc(GST.sem, 16)
                if dg < K16:
                    gp.dma_start(out=wo_bf_scr.ap()[dg], in_=w_outT[dg]).then_inc(LDA.sem, 16)
                else:
                    gp.dma_start(out=wd_bf_scr.ap()[dg - K16], in_=w_down[dg - K16]).then_inc(LDA.sem, 16)
            GST.n += 16
            LDA.n += 16

        @block.sync
        def _(gp):
            gp.wait_ge(SND.sem, SND.n)
            gp.dma_start(out=poll_scr.ap(), in_=pstat_sb[:]).then_inc(GST.sem, 16)
        GST.n += 16
        regs.close()

        @block.sync
        def _(sync):
            sync.wait_ge(GST.sem, GST.n)
        barrier()
        g3.close()

        g4 = ExitStack()
        ygsb_b = [g4.enter_context(nc.sbuf_tensor(f"ygsb{i}", [128, DG * 512], BF16)) for i in range(2)]
        attn_b = [g4.enter_context(nc.sbuf_tensor(f"attn{i}", [128, K16 * 512], F32)) for i in range(2)]
        wo_sb = [g4.enter_context(nc.sbuf_tensor(f"wo{i}", [128, DG * 128], BF16)) for i in range(2)]
        x1t = [g4.enter_context(nc.sbuf_tensor(f"x1t{i}", [128, DM], F32)) for i in range(2)]
        v2tt = g4.enter_context(nc.sbuf_tensor("v2tt", [128, DM], BF16))
        v2stg = [g4.enter_context(nc.sbuf_tensor(f"v2stg{i}", [128, K16 * 128], BF16)) for i in range(2)]

        oev = []
        v2_gst_marks = []
        for ci, (ns, nw) in enumerate(NCHUNK):
            ygsb = ygsb_b[ci % 2]

            @block.sync
            def _(sync, ci=ci, ns=ns, nw=nw, ygsb=ygsb):
                for k in range(DG):
                    sync.dma_start(out=ygsb[:, k * 512:k * 512 + nw],
                                   in_=yg_scr.ap()[k * 128:(k + 1) * 128, ns:ns + nw]).then_inc(LDD.sem, 16)
            LDD.n += 16 * DG

        for m in range(K16):
            wb = wo_sb[m % 2]

            @block.scalar
            def _(scalar, m=m, wb=wb):
                if m >= 2:
                    scalar.wait_ge(MM.sem, MM.n - 1)
                else:
                    scalar.wait_ge(LDA.sem, LDA.n)
                scalar.dma_start(out=wb[:], in_=wo_bf_scr.ap()[m]).then_inc(LDC.sem, 16)
            LDC.n += 16

            b0, b1 = pb[(2 * m) % 4], pb[(2 * m + 1) % 4]

            @block.tensor
            def _(tensor, m=m, wb=wb, b0=b0, b1=b1):
                tensor.wait_ge(LDC.sem, LDC.n)
                tensor.wait_ge(LDD.sem, LDD.n)
                if m >= 2:
                    tensor.wait_ge(EVB.sem, oev[2 * m - 3])
                for k in range(DG):
                    tensor.matmul(b0.ap()[:], wb[:, k * 128:(k + 1) * 128],
                                  ygsb_b[0][:, k * 512:(k + 1) * 512],
                                  start=(k == 0), stop=(k == DG - 1))
                    ins = tensor.matmul(b1.ap()[:], wb[:, k * 128:(k + 1) * 128],
                                        ygsb_b[1][:, k * 512:(k + 1) * 512],
                                        start=(k == 0), stop=(k == DG - 1))
                MM.inc(ins)

            for cc in range(2):
                @block.vector
                def _(vector, m=m, cc=cc, bank=(b0, b1)[cc]):
                    vector.wait_ge(MM.sem, MM.n)
                    EVB.inc(vector.tensor_copy(attn_b[cc][:, m * 512:(m + 1) * 512], bank.ap()[:]))
                oev.append(EVB.n)

        for ci, (ns, nw) in enumerate(NCHUNK):
            attn_sb = attn_b[ci]

            for tt in range(nw // 128):
                xt = x1t[tt % 2]

                @block.sync
                def _(sync, ci=ci, ns=ns, tt=tt, xt=xt):
                    sync.wait_ge(GST.sem, GST.n)
                    sync.dma_start(out=xt[:], in_=x0[128 + ns + tt * 128: 128 + ns + tt * 128 + 128, :]).then_inc(LDB.sem, 16)
                LDB.n += 16

                rb4 = [ptf[0], ptf[1], pb[4].ap(), pb[5].ap()]
                for m in range(K16):
                    @block.tensor
                    def _(tensor, m=m, tt=tt):
                        tensor.wait_ge(EVB.sem, EVB.n)
                        lag = 1 if (m < 2 or TRP.n < 4) else 3
                        if TRP.n >= 2:
                            tensor.wait_ge(TCO.sem, TRP.n - lag)
                        TRP.inc(tensor.transpose(rb4[TRP.n % 4][:, 0:128],
                                                 attn_sb[:, m * 512 + tt * 128: m * 512 + tt * 128 + 128],
                                                 ident_f[:]))

                    @block.vector
                    def _(vector, m=m, xt=xt):
                        vector.wait_ge(TRP.sem, TRP.n)
                        vector.wait_ge(LDB.sem, LDB.n)
                        TCO.inc(vector.tensor_tensor(xt[:, m * 128:(m + 1) * 128],
                                                     rb4[(TRP.n - 1) % 4][:, 0:128],
                                                     xt[:, m * 128:(m + 1) * 128], ALU.add))

                @block.scalar
                def _(scalar, ns=ns, tt=tt, xt=xt):
                    scalar.wait_ge(TCO.sem, TCO.n)
                    scalar.dma_start(out=x1_scr.ap()[ns + tt * 128: ns + tt * 128 + 128, :], in_=xt[:]).then_inc(GST.sem, 16)
                GST.n += 16

                @block.vector
                def _(vector, xt=xt):
                    vector.wait_ge(TCO.sem, TCO.n)
                    VEC.inc(vector.tensor_reduce(lnt1[:], xt[:], AX.X, ALU.add))

                @block.scalar
                def _(scalar, xt=xt):
                    scalar.wait_ge(VEC.sem, VEC.n)
                    SCA.inc(scalar.activation(junk[:], xt[:], AF.Square, accum_out=lnt2[:]))

                @block.vector
                def _(vector):
                    vector.wait_ge(SCA.sem, SCA.n)
                    VEC.inc(vector.tensor_scalar(lnt1[:], lnt1[:], 1.0 / DM, None, ALU.mult))
                    vector.wait_ge(VEC.sem, VEC.n)
                    VEC.inc(vector.tensor_scalar(lnt2[:], lnt2[:], 1.0 / DM, None, ALU.mult))
                    vector.wait_ge(VEC.sem, VEC.n)
                    VEC.inc(vector.scalar_tensor_tensor(lnc1[:], lnt1[:], lnt1[:], lnt2[:], ALU.mult, ALU.subtract))
                    vector.wait_ge(VEC.sem, VEC.n)
                    VEC.inc(vector.tensor_scalar(lnc1[:], lnc1[:], -1.0, EPS, ALU.mult, ALU.add))

                @block.scalar
                def _(scalar):
                    scalar.wait_ge(VEC.sem, VEC.n)
                    SCA.inc(scalar.activation(lnc1[:], lnc1[:], AF.Ln))
                    scalar.wait_ge(SCA.sem, SCA.n)
                    SCA.inc(scalar.activation(lnc1[:], lnc1[:], AF.Exp, scale=-0.5))

                @block.vector
                def _(vector, xt=xt):
                    vector.wait_ge(SCA.sem, SCA.n)
                    VEC.inc(vector.tensor_scalar(lnc2[:], lnt1[:], -1.0, lnc1[:], ALU.mult, ALU.mult))
                    vector.wait_ge(VEC.sem, VEC.n)
                    vector.wait_ge(TRP.sem, TRP.n)
                    VEC.inc(vector.tensor_scalar(v2tt[:], xt[:], lnc1[:], lnc2[:], ALU.mult, ALU.add))

                gtt = ci * 4 + tt
                vsb = v2stg[gtt % 2]
                for m in range(K16):
                    @block.tensor
                    def _(tensor, m=m):
                        tensor.wait_ge(VEC.sem, VEC.n)
                        if TRP.n >= 2:
                            tensor.wait_ge(TCO.sem, TRP.n - 1)
                        TRP.inc(tensor.transpose(pt[TRP.n % 2].ap()[:, 0:128],
                                                 v2tt[:, m * 128:(m + 1) * 128], ident_b[:]))

                    @block.vector
                    def _(vector, m=m, gtt=gtt, vsb=vsb):
                        vector.wait_ge(TRP.sem, TRP.n)
                        if m == 0 and gtt >= 2:
                            vector.wait_ge(GST.sem, v2_gst_marks[gtt - 2])
                        TCO.inc(vector.tensor_copy(vsb[:, m * 128:(m + 1) * 128],
                                                   pt[(TRP.n - 1) % 2].ap()[:, 0:128]))

                @block.gpsimd
                def _(gp, ci=ci, ns=ns, tt=tt, vsb=vsb):
                    gp.wait_ge(TCO.sem, TCO.n)
                    gp.dma_start(out=v2t_scr.ap()[:, ns + tt * 128: ns + tt * 128 + 128]
                                 .rearrange("(m p) c -> p m c", m=K16),
                                 in_=vsb[:].rearrange("p (m c) -> p m c", m=K16)).then_inc(GST.sem, 16)
                GST.n += 16
                v2_gst_marks.append(GST.n)

        @block.sync
        def _(sync):
            sync.wait_ge(GST.sem, GST.n)
        barrier()
        g4.close()

        g5 = ExitStack()
        v2k = g5.enter_context(nc.sbuf_tensor("v2k", [128, K16 * 512], BF16))
        fuse = g5.enter_context(nc.sbuf_tensor("fuse", [128, MI * 512], BF16))
        wg_sb = [g5.enter_context(nc.sbuf_tensor(f"wg{i}", [128, K16 * 128], BF16)) for i in range(2)]
        wu_sb = [g5.enter_context(nc.sbuf_tensor(f"wu{i}", [128, K16 * 128], BF16)) for i in range(2)]
        wd_sb = [g5.enter_context(nc.sbuf_tensor(f"wd{i}", [128, MI * 128], BF16)) for i in range(2)]
        gate_t = [g5.enter_context(nc.sbuf_tensor(f"gate_t{i}", [128, 512], BF16)) for i in range(2)]
        ds_t = [g5.enter_context(nc.sbuf_tensor(f"ds_t{i}", [128, 512], F32)) for i in range(2)]
        x1f = [g5.enter_context(nc.sbuf_tensor(f"x1f{i}", [128, DM], F32)) for i in range(4)]

        for ci, (ns, nw) in enumerate(NCHUNK):
            @block.sync
            def _(sync, ci=ci, ns=ns, nw=nw):
                if ci > 0:
                    sync.wait_ge(OUT.sem, OUT.n)
                for k in range(K16):
                    sync.dma_start(out=v2k[:, k * 512:k * 512 + nw],
                                   in_=v2t_scr.ap()[k * 128:(k + 1) * 128, ns:ns + nw]).then_inc(LDD.sem, 16)
                for tt in range(4):
                    sync.dma_start(out=x1f[tt][:],
                                   in_=x1_scr.ap()[ns + tt * 128: ns + tt * 128 + 128, :]).then_inc(LDB.sem, 16)
            LDD.n += 16 * K16
            LDB.n += 16 * 4

            mm_end, sca_end, evb_end = [], [], []
            for mi in range(MI):
                wgb, wub = wg_sb[mi % 2], wu_sb[mi % 2]

                @block.gpsimd
                def _(gp, ci=ci, mi=mi, wgb=wgb, wub=wub):
                    if mi >= 2:
                        gp.wait_ge(MM.sem, mm_end[mi - 2])
                    elif ci > 0:
                        gp.wait_ge(MM.sem, MM.n)
                    gp.dma_start(out=wgb[:], in_=w_gate[mi]).then_inc(LDC.sem, 16)
                    gp.dma_start(out=wub[:], in_=w_up[mi]).then_inc(LDC.sem, 16)
                LDC.n += 32

                @block.tensor
                def _(tensor, ci=ci, mi=mi, nw=nw, wgb=wgb, wub=wub):
                    tensor.wait_ge(LDC.sem, LDC.n)
                    tensor.wait_ge(LDD.sem, LDD.n)
                    if mi >= 2:
                        tensor.wait_ge(SCA.sem, sca_end[mi - 2])
                        tensor.wait_ge(EVB.sem, evb_end[mi - 2])
                    elif ci > 0:
                        tensor.wait_ge(SCA.sem, SCA.n)
                        tensor.wait_ge(EVB.sem, EVB.n)
                    gb, ub = pb[2 + mi % 2], pb[4 + mi % 2]
                    for k in range(K16):
                        tensor.matmul(gb.ap()[:, 0:nw], wgb[:, k * 128:(k + 1) * 128],
                                      v2k[:, k * 512:k * 512 + nw], start=(k == 0), stop=(k == K16 - 1))
                    for k in range(K16):
                        ins = tensor.matmul(ub.ap()[:, 0:nw], wub[:, k * 128:(k + 1) * 128],
                                            v2k[:, k * 512:k * 512 + nw], start=(k == 0), stop=(k == K16 - 1))
                    MM.inc(ins)
                mm_end.append(MM.n)

                @block.scalar
                def _(scalar, mi=mi, nw=nw):
                    scalar.wait_ge(MM.sem, mm_end[mi])
                    if mi >= 1:
                        scalar.wait_ge(EVB.sem, evb_end[mi - 1])
                    SCA.inc(scalar.activation(gate_t[mi % 2][:, 0:nw], pb[2 + mi % 2].ap()[:, 0:nw], AF.Silu,
                                              scale=sg_sb[:, mi:mi + 1]))
                sca_end.append(SCA.n)

                @block.vector
                def _(vector, mi=mi, nw=nw):
                    vector.wait_ge(SCA.sem, sca_end[mi])
                    EVB.inc(vector.scalar_tensor_tensor(fuse[:, mi * 512:mi * 512 + nw],
                                                        pb[4 + mi % 2].ap()[:, 0:nw], su_sb[:, mi:mi + 1],
                                                        gate_t[mi % 2][:, 0:nw], ALU.mult, ALU.mult))
                evb_end.append(EVB.n)

            dn_end, evc_end, trp_end = [], [], []

            def emit_dn_transposes(mh, nw=nw):
                for tt in range(nw // 128):
                    @block.tensor
                    def _(tensor, mh=mh, tt=tt):
                        tensor.wait_ge(EVC.sem, evc_end[mh])
                        if TRP.n >= 2:
                            tensor.wait_ge(TCO.sem, TRP.n - 1)
                        TRP.inc(tensor.transpose(ptf[TRP.n % 2][:, 0:128],
                                                 ds_t[mh % 2][:, tt * 128:(tt + 1) * 128], ident_f[:]))

                    @block.vector
                    def _(vector, mh=mh, tt=tt):
                        vector.wait_ge(TRP.sem, TRP.n)
                        vector.wait_ge(LDB.sem, LDB.n)
                        TCO.inc(vector.tensor_tensor(x1f[tt][:, mh * 128:(mh + 1) * 128],
                                                     ptf[(TRP.n - 1) % 2][:, 0:128],
                                                     x1f[tt][:, mh * 128:(mh + 1) * 128], ALU.add))
                trp_end.append(TRP.n)

            for mh in range(K16):
                wdb = wd_sb[mh % 2]

                @block.scalar
                def _(scalar, ci=ci, mh=mh, wdb=wdb):
                    if mh >= 2:
                        scalar.wait_ge(MM.sem, dn_end[mh - 2])
                    elif ci > 0:
                        scalar.wait_ge(MM.sem, MM.n)
                    else:
                        scalar.wait_ge(LDA.sem, LDA.n)
                    scalar.dma_start(out=wdb[:], in_=wd_bf_scr.ap()[mh]).then_inc(LDC.sem, 16)
                LDC.n += 16

                bank = pb[mh % 2]

                @block.tensor
                def _(tensor, ci=ci, mh=mh, nw=nw, bank=bank, wdb=wdb):
                    tensor.wait_ge(LDC.sem, LDC.n)
                    tensor.wait_ge(EVB.sem, EVB.n)
                    if mh >= 2:
                        tensor.wait_ge(EVC.sem, evc_end[mh - 2])
                    elif ci > 0:
                        tensor.wait_ge(EVC.sem, EVC.n)
                    for k in range(MI):
                        ins = tensor.matmul(bank.ap()[:, 0:nw], wdb[:, k * 128:(k + 1) * 128],
                                            fuse[:, k * 512:k * 512 + nw], start=(k == 0), stop=(k == MI - 1))
                    MM.inc(ins)
                dn_end.append(MM.n)

                @block.vector
                def _(vector, mh=mh, nw=nw, bank=bank):
                    vector.wait_ge(MM.sem, dn_end[mh])
                    if mh >= 2:
                        vector.wait_ge(TRP.sem, trp_end[mh - 2])
                    EVC.inc(vector.tensor_scalar(ds_t[mh % 2][:, 0:nw], bank.ap()[:, 0:nw], sd_sb[:, mh:mh + 1],
                                                 None, ALU.mult))
                evc_end.append(EVC.n)

                if mh >= 1:
                    emit_dn_transposes(mh - 1)
            emit_dn_transposes(K16 - 1)

            @block.gpsimd
            def _(gp, ns=ns):
                gp.wait_ge(TCO.sem, TCO.n)
                for tt in range(4):
                    gp.dma_start(out=y_out[ns + tt * 128: ns + tt * 128 + 128, :],
                                 in_=x1f[tt][:]).then_inc(OUT.sem, 16)
            OUT.n += 16 * 4

        @block.gpsimd
        def _(gp):
            gp.wait_ge(OUT.sem, OUT.n)

        g5.close()

    return nc


def stage_in_maps(inputs):
    hs = np.asarray(inputs["L0_attn_hidden_states"])
    in_proj_w = np.asarray(inputs["L0_attn_in_proj_w"])
    conv_w = np.asarray(inputs["L0_attn_conv_w"])
    conv_b = np.asarray(inputs["L0_attn_conv_b"])
    x_proj_w = np.asarray(inputs["L0_attn_x_proj_w"])
    dt_proj_w = np.asarray(inputs["L0_attn_dt_proj_w"])
    dt_proj_b = np.asarray(inputs["L0_attn_dt_proj_b"])
    a_log = np.asarray(inputs["L0_attn_A_log"])
    d_skip = np.asarray(inputs["L0_attn_D"])
    out_proj_w = np.asarray(inputs["L0_attn_out_proj_w"])
    w_gate = np.asarray(inputs["L0_ffn_w_gate"])
    w_up = np.asarray(inputs["L0_ffn_w_up"])
    w_down = np.asarray(inputs["L0_ffn_w_down"])
    s_gate = np.asarray(inputs["L0_ffn_s_gate"])
    s_up = np.asarray(inputs["L0_ffn_s_up"])
    s_down = np.asarray(inputs["L0_ffn_s_down"])

    Ct = np.ascontiguousarray

    def colmajor(v, g):
        return Ct(v.reshape(g, 128).T)

    runid = int(np.random.default_rng().integers(1 << 16, 1 << 30))
    common = dict(
        w_inT=Ct(in_proj_w.T.reshape(K16, 128, 2 * DI // 128, 128).transpose(2, 1, 0, 3).reshape(2 * DI // 128, 128, K16 * 128)),
        conv_w=Ct(conv_w.reshape(DG, 128, DC).transpose(1, 0, 2).reshape(128, DG * DC)),
        conv_b=colmajor(conv_b, DG),
        x_projT=Ct(x_proj_w.T.reshape(DG, 128, 160).transpose(1, 0, 2).reshape(128, DG * 160)),
        dt_projT=Ct(dt_proj_w.T),
        dt_b=colmajor(dt_proj_b, DG),
        a_log=Ct(a_log.reshape(DG, 128, DS).transpose(1, 0, 2).reshape(128, DG * DS)),
        d_skip=colmajor(d_skip, DG),
        w_outT=Ct(out_proj_w.T.reshape(DG, 128, K16, 128).transpose(2, 1, 0, 3).reshape(K16, 128, DG * 128)),
        w_gate=Ct(w_gate.reshape(K16, 128, MI, 128).transpose(2, 1, 0, 3).reshape(MI, 128, K16 * 128)),
        w_up=Ct(w_up.reshape(K16, 128, MI, 128).transpose(2, 1, 0, 3).reshape(MI, 128, K16 * 128)),
        w_down=Ct(w_down.reshape(MI, 128, K16, 128).transpose(2, 1, 0, 3).reshape(K16, 128, MI * 128)),
        s_gate=colmajor(s_gate, MI),
        s_up=colmajor(s_up, MI),
        s_down=colmajor(s_down, K16),
        ident=np.eye(128, dtype=np.float32),
        runid=np.array([[runid]], np.int32),
    )

    in_maps = []
    for core in range(8):
        b, role = core // 2, core % 2
        o0 = role * S
        xh = np.zeros((W, DM), np.float32)
        if o0 >= 3:
            xh[125:128] = hs[b, o0 - 3:o0]
        xh[128:] = hs[b, o0:o0 + S]
        in_maps.append(dict(common, x0=xh, role=np.array([[role]], np.int32),
                            rolev=np.full((128, 1), float(role), np.float32)))
    return in_maps


def kernel(**inputs):
    nc = build_program()
    in_maps = stage_in_maps(inputs)
    res = run_bass_kernel_spmd(nc, in_maps, core_ids=list(range(8)))
    out = np.empty((B, L, DM), np.float32)
    for core in range(8):
        b, role = core // 2, core % 2
        out[b, role * S:(role + 1) * S] = res.results[core]["y_out"]
    return out


# version 25
# speedup vs baseline: 1.0956x; 1.0078x over previous
import numpy as np
from contextlib import ExitStack

import concourse.bass as bass
import concourse.mybir as mybir
from concourse.bass_utils import run_bass_kernel_spmd

F32 = mybir.dt.float32
BF16 = mybir.dt.bfloat16
I32 = mybir.dt.int32
AF = mybir.ActivationFunctionType
ALU = mybir.AluOpType
AX = mybir.AxisListType

B, L, DM, DI, DS, DC, DTR, INTER = 4, 2048, 2048, 4096, 16, 4, 128, 8192
DG = DI // 128
K16 = DM // 128
MI = INTER // 128
S = 1024
W = S + 128
T = S
NCHUNK = [(0, 512), (512, 512)]
ICHUNK = [(125, 3), (128, 512), (640, 512)]
NT = W // 128
EPS = 1e-5
POLL_BOUND = 2_000_000


class Ctr:
    def __init__(self, sem):
        self.sem = sem
        self.n = 0

    def inc(self, instr, k=1):
        instr.then_inc(self.sem, k)
        self.n += k
        return self.n


def build_program(dbg=False, no_hs=False, hs_poll=True, hs_send=True, hs_local=False):
    nc = bass.Bass("TRN2", target_bir_lowering=False, debug=False)
    KIND = "ExternalOutput" if dbg else "Internal"

    x0 = nc.dram_tensor("x0", [W, DM], F32, kind="ExternalInput").ap()
    w_inT = nc.dram_tensor("w_inT", [2 * DI // 128, 128, K16 * 128], F32, kind="ExternalInput").ap()
    conv_w = nc.dram_tensor("conv_w", [128, DG * DC], F32, kind="ExternalInput").ap()
    conv_b = nc.dram_tensor("conv_b", [128, DG], F32, kind="ExternalInput").ap()
    x_projT = nc.dram_tensor("x_projT", [128, DG * 160], F32, kind="ExternalInput").ap()
    dt_projT = nc.dram_tensor("dt_projT", [DTR, DI], F32, kind="ExternalInput").ap()
    dt_b = nc.dram_tensor("dt_b", [128, DG], F32, kind="ExternalInput").ap()
    a_log = nc.dram_tensor("a_log", [128, DG * DS], F32, kind="ExternalInput").ap()
    d_skip = nc.dram_tensor("d_skip", [128, DG], F32, kind="ExternalInput").ap()
    w_outT = nc.dram_tensor("w_outT", [K16, 128, DG * 128], F32, kind="ExternalInput").ap()
    w_gate = nc.dram_tensor("w_gate", [MI, 128, K16 * 128], mybir.dt.int8, kind="ExternalInput").ap()
    w_up = nc.dram_tensor("w_up", [MI, 128, K16 * 128], mybir.dt.int8, kind="ExternalInput").ap()
    w_down = nc.dram_tensor("w_down", [K16, 128, MI * 128], mybir.dt.int8, kind="ExternalInput").ap()
    s_gate = nc.dram_tensor("s_gate", [128, MI], F32, kind="ExternalInput").ap()
    s_up = nc.dram_tensor("s_up", [128, MI], F32, kind="ExternalInput").ap()
    s_down = nc.dram_tensor("s_down", [128, K16], F32, kind="ExternalInput").ap()
    ident_in = nc.dram_tensor("ident", [128, 128], F32, kind="ExternalInput").ap()
    role_in = nc.dram_tensor("role", [1, 1], I32, kind="ExternalInput").ap()
    rolev_in = nc.dram_tensor("rolev", [128, 1], F32, kind="ExternalInput").ap()
    runid_in = nc.dram_tensor("runid", [1, 1], I32, kind="ExternalInput").ap()

    y_out = nc.dram_tensor("y_out", [S, DM], F32, kind="ExternalOutput").ap()

    xpre_scr = nc.dram_tensor("xpre_scr", [DI, S + 3], BF16, kind=KIND)
    z_scr = nc.dram_tensor("z_scr", [DI, S], BF16, kind=KIND)
    zs_scr = nc.dram_tensor("zs_scr", [DI, S], BF16, kind=KIND)
    xc_scr = nc.dram_tensor("xc_scr", [DI, S], BF16, kind=KIND)
    delta_scr = nc.dram_tensor("delta_scr", [DI, S], F32, kind=KIND)
    bc_scr = nc.dram_tensor("bc_scr", [2 * DS, S], F32, kind=KIND)
    yg_scr = nc.dram_tensor("yg_scr", [DI, S], BF16, kind=KIND)
    x1_scr = nc.dram_tensor("x1_scr", [S, DM], F32, kind=KIND)
    v2t_scr = nc.dram_tensor("v2t_scr", [DM, S], BF16, kind=KIND)
    wo_bf_scr = nc.dram_tensor("wo_bf_scr", [K16, 128, DG * 128], BF16, kind=KIND)
    wd_bf_scr = nc.dram_tensor("wd_bf_scr", [K16, 128, MI * 128], BF16, kind=KIND)

    if hs_local:
        sh_carry = nc.dram_tensor("sh_carry", [128, DG * DS], F32)
        sh_flag = nc.dram_tensor("sh_flag", [1, DG * 16], I32)
    else:
        sh_carry = nc.dram_tensor("sh_carry", [128, DG * DS], F32, addr_space="Shared")
        sh_flag = nc.dram_tensor("sh_flag", [1, DG * 16], I32, addr_space="Shared")
    poll_scr = nc.dram_tensor("poll_scr", [1, DG + 2], I32, kind=KIND)

    ctx = ExitStack()
    with ctx:
        sbuf = lambda name, shape, dt: ctx.enter_context(nc.sbuf_tensor(name, shape, dt))
        ident_f = sbuf("ident_f", [128, 128], F32)
        ident_b = sbuf("ident_b", [128, 128], BF16)
        a_sb = sbuf("a_sb", [128, DG * DS], F32)
        dsk_sb = sbuf("dsk_sb", [128, DG], F32)
        cw_sb = sbuf("cw_sb", [128, DG * DC], F32)
        cb_sb = sbuf("cb_sb", [128, DG], F32)
        dtb_sb = sbuf("dtb_sb", [128, DG], F32)
        sg_sb = sbuf("sg_sb", [128, MI], F32)
        su_sb = sbuf("su_sb", [128, MI], F32)
        sd_sb = sbuf("sd_sb", [128, K16], F32)
        carry = sbuf("carry", [128, DG * DS], F32)
        role_sb = sbuf("role_sb", [1, 1], I32)
        rolev_sb = sbuf("rolev_sb", [128, 1], F32)
        runid_sb = sbuf("runid_sb", [1, 1], I32)
        flag_sb = sbuf("flag_sb", [1, 1], I32)
        pstat_sb = sbuf("pstat_sb", [1, DG + 2], I32)
        cstg = sbuf("cstg", [128, DS], F32)
        ctmp = sbuf("ctmp", [128, DS], F32)
        stgb = [sbuf(f"stgb{i}", [128, 512], BF16) for i in range(5)]
        lnt1 = sbuf("lnt1", [128, 1], F32)
        lnt2 = sbuf("lnt2", [128, 1], F32)
        lnc1 = sbuf("lnc1", [128, 1], F32)
        lnc2 = sbuf("lnc2", [128, 1], F32)
        junk = sbuf("junk", [128, 2048], BF16)

        pb = [ctx.enter_context(nc.psum_tensor(f"pb{i}", [128, 512], F32)) for i in range(6)]
        pt = [ctx.enter_context(nc.psum_tensor(f"pt{i}", [128, 1024], BF16)) for i in range(2)]
        ptf = [p.ap().bitcast(F32) for p in pt]

        C = {}
        for name in ["LDA", "LDB", "LDC", "LDD", "MM", "EVA", "EVB", "EVC", "SCA",
                     "VEC", "VSY", "GST", "BAR", "TRP", "TCO", "OUT", "VSC", "CAR", "PSM", "SND", "GPX"]:
            C[name] = Ctr(ctx.enter_context(nc.semaphore(name)))
        LDA, LDB, LDC, LDD = C["LDA"], C["LDB"], C["LDC"], C["LDD"]
        MM, EVA, EVB, EVC = C["MM"], C["EVA"], C["EVB"], C["EVC"]
        SCA, VEC, VSY = C["SCA"], C["VEC"], C["VSY"]
        GST, BAR, TRP, TCO, OUT, VSC = C["GST"], C["BAR"], C["TRP"], C["TCO"], C["OUT"], C["VSC"]
        CAR, PSM, SND, GPX = C["CAR"], C["PSM"], C["SND"], C["GPX"]

        block = ctx.enter_context(nc.Block())
        ENGS = ["tensor", "vector", "scalar", "gpsimd", "sync"]

        def barrier():
            tgt = BAR.n + 5
            for en in ENGS:
                def mk(en=en, tgt=tgt):
                    def _(e):
                        BAR.inc(e.nop())
                        e.wait_ge(BAR.sem, tgt)
                    return _
                getattr(block, en)(mk())

        @block.sync
        def _(sync):
            for dst, src in [(ident_f, ident_in), (cw_sb, conv_w), (cb_sb, conv_b),
                             (dtb_sb, dt_b), (a_sb, a_log), (dsk_sb, d_skip),
                             (sg_sb, s_gate), (su_sb, s_up), (sd_sb, s_down),
                             (role_sb, role_in), (runid_sb, runid_in), (rolev_sb, rolev_in)]:
                sync.dma_start(out=dst[:], in_=src).then_inc(LDA.sem, 16)
        LDA.n += 16 * 12

        @block.scalar
        def _(scalar):
            scalar.wait_ge(LDA.sem, LDA.n)
            SCA.inc(scalar.activation(a_sb[:], a_sb[:], AF.Exp))

        @block.vector
        def _(vector):
            vector.wait_ge(SCA.sem, SCA.n)
            VEC.inc(vector.tensor_scalar(a_sb[:], a_sb[:], -1.0, None, ALU.mult))
            VEC.inc(vector.memset(carry[:], 0.0))
            VEC.inc(vector.tensor_copy(ident_b[:], ident_f[:]))

        g1 = ExitStack()
        lnxT = g1.enter_context(nc.sbuf_tensor("lnxT", [128, K16 * W], BF16))
        x0t = [g1.enter_context(nc.sbuf_tensor(f"x0t{i}", [128, DM], F32)) for i in range(2)]
        lnxt = g1.enter_context(nc.sbuf_tensor("lnxt", [128, DM], BF16))
        wpool = [g1.enter_context(nc.sbuf_tensor(f"wpool{i}", [128, K16 * 128], BF16)) for i in range(3)]
        wstg = [g1.enter_context(nc.sbuf_tensor(f"wstg{i}", [128, K16 * 128], F32)) for i in range(2)]

        inproj_end = []
        cast_marks = []

        def emit_wload(m):
            ws = wstg[m % 2]
            wb = wpool[m % 3]

            @block.sync
            def _(sync, m=m, ws=ws):
                if m >= 2:
                    sync.wait_ge(VEC.sem, cast_marks[m - 2])
                sync.dma_start(out=ws[:], in_=w_inT[m]).then_inc(LDC.sem, 16)
            LDC.n += 16

            @block.vector
            def _(vector, m=m, wb=wb, ws=ws):
                vector.wait_ge(LDC.sem, LDC.n)
                if m >= 3:
                    vector.wait_ge(MM.sem, inproj_end[m - 3])
                VEC.inc(vector.tensor_copy(wb[:], ws[:]))
            cast_marks.append(VEC.n)

        emit_wload(0)
        emit_wload(1)

        ln_marks = []
        for tt in range(NT):
            buf = x0t[tt % 2]

            @block.sync
            def _(sync, tt=tt, buf=buf):
                if tt >= 2:
                    sync.wait_ge(TRP.sem, (tt - 1) * K16)
                sync.dma_start(out=buf[:], in_=x0[tt * 128:(tt + 1) * 128, :]).then_inc(LDB.sem, 16)
            LDB.n += 16

            @block.vector
            def _(vector, buf=buf):
                vector.wait_ge(LDB.sem, LDB.n)
                VEC.inc(vector.tensor_reduce(lnt1[:], buf[:], AX.X, ALU.add))

            @block.scalar
            def _(scalar, buf=buf):
                scalar.wait_ge(LDB.sem, LDB.n)
                scalar.wait_ge(VEC.sem, VEC.n)
                SCA.inc(scalar.activation(junk[:], buf[:], AF.Square, accum_out=lnt2[:]))

            @block.vector
            def _(vector):
                vector.wait_ge(SCA.sem, SCA.n)
                VEC.inc(vector.tensor_scalar(lnt1[:], lnt1[:], 1.0 / DM, None, ALU.mult))
                vector.wait_ge(VEC.sem, VEC.n)
                VEC.inc(vector.tensor_scalar(lnt2[:], lnt2[:], 1.0 / DM, None, ALU.mult))
                vector.wait_ge(VEC.sem, VEC.n)
                VEC.inc(vector.scalar_tensor_tensor(lnc1[:], lnt1[:], lnt1[:], lnt2[:], ALU.mult, ALU.subtract))
                vector.wait_ge(VEC.sem, VEC.n)
                VEC.inc(vector.tensor_scalar(lnc1[:], lnc1[:], -1.0, EPS, ALU.mult, ALU.add))

            @block.scalar
            def _(scalar):
                scalar.wait_ge(VEC.sem, VEC.n)
                SCA.inc(scalar.activation(lnc1[:], lnc1[:], AF.Ln))
                scalar.wait_ge(SCA.sem, SCA.n)
                SCA.inc(scalar.activation(lnc1[:], lnc1[:], AF.Exp, scale=-0.5))

            @block.vector
            def _(vector, tt=tt, buf=buf):
                vector.wait_ge(SCA.sem, SCA.n)
                VEC.inc(vector.tensor_scalar(lnc2[:], lnt1[:], -1.0, lnc1[:], ALU.mult, ALU.mult))
                vector.wait_ge(VEC.sem, VEC.n)
                if tt >= 1:
                    vector.wait_ge(TRP.sem, tt * K16)
                VEC.inc(vector.tensor_scalar(lnxt[:], buf[:], lnc1[:], lnc2[:], ALU.mult, ALU.add))

            for k in range(K16):
                @block.tensor
                def _(tensor, tt=tt, k=k):
                    tensor.wait_ge(VEC.sem, VEC.n)
                    if TRP.n >= 2:
                        tensor.wait_ge(TCO.sem, TRP.n - 1)
                    TRP.inc(tensor.transpose(pt[TRP.n % 2].ap()[:, 0:128], lnxt[:, k * 128:(k + 1) * 128], ident_b[:]))

                @block.scalar
                def _(scalar, tt=tt, k=k):
                    scalar.wait_ge(TRP.sem, TRP.n)
                    TCO.inc(scalar.copy(lnxT[:, k * W + tt * 128: k * W + tt * 128 + 128],
                                        pt[(TRP.n - 1) % 2].ap()[:, 0:128]))
            ln_marks.append(TCO.n)

        TCO_P1 = TCO.n
        xpt = [g1.enter_context(nc.sbuf_tensor(f"xpt{i}", [128, S + 3], BF16)) for i in range(2)]
        acc_t = g1.enter_context(nc.sbuf_tensor("acc_t", [128, S], F32))
        xct = g1.enter_context(nc.sbuf_tensor("xct", [128, S], BF16))
        cstore_gst = []
        csil_sca = []

        def emit_conv(dg):
            xb = xpt[dg % 2]

            @block.sync
            def _(sync, dg=dg, xb=xb):
                sync.wait_ge(GST.sem, xpre_gst[dg])
                if dg >= 2:
                    sync.wait_ge(VEC.sem, VEC.n)
                sync.dma_start(out=xb[:], in_=xpre_scr.ap()[dg * 128:(dg + 1) * 128, :]).then_inc(LDD.sem, 16)
            LDD.n += 16

            @block.vector
            def _(vector, dg=dg, xb=xb):
                vector.wait_ge(LDD.sem, LDD.n)
                if dg >= 1:
                    vector.wait_ge(SCA.sem, csil_sca[dg - 1])
                VEC.inc(vector.tensor_scalar(acc_t[:], xb[:, 0:S], cw_sb[:, dg * DC: dg * DC + 1],
                                             cb_sb[:, dg:dg + 1], ALU.mult, ALU.add))
                for k in range(1, DC):
                    vector.wait_ge(VEC.sem, VEC.n)
                    VEC.inc(vector.scalar_tensor_tensor(acc_t[:], xb[:, k:k + S],
                                                        cw_sb[:, dg * DC + k: dg * DC + k + 1],
                                                        acc_t[:], ALU.mult, ALU.add))

            @block.scalar
            def _(scalar, dg=dg):
                scalar.wait_ge(VEC.sem, VEC.n)
                if dg >= 1:
                    scalar.wait_ge(GST.sem, cstore_gst[dg - 1])
                ins = scalar.activation(xct[:], acc_t[:], AF.Silu)
                SCA.inc(ins)
                scalar.wait_ge(SCA.sem, SCA.n)
                scalar.dma_start(out=xc_scr.ap()[dg * 128:(dg + 1) * 128, :], in_=xct[:]).then_inc(GST.sem, 16)
            csil_sca.append(SCA.n)
            GST.n += 16
            cstore_gst.append(GST.n)

        mm_g = 0
        xpre_gst = []
        store_marks = []
        NM = 2 * DI // 128

        for m in range(NM):
            wb = wpool[m % 3]
            chunks = ICHUNK if m < DG else ICHUNK[1:]
            if m + 2 < NM:
                emit_wload(m + 2)

            for (ns, nw) in chunks:
                g = mm_g
                bank = pb[g % 4]

                @block.tensor
                def _(tensor, m=m, ns=ns, nw=nw, g=g, bank=bank, wb=wb):
                    tensor.wait_ge(VEC.sem, cast_marks[m])
                    tensor.wait_ge(TCO.sem, TCO_P1)
                    if g >= 4:
                        tensor.wait_ge(EVB.sem, g - 3)
                    for k in range(K16):
                        ins = tensor.matmul(bank.ap()[:, 0:nw], wb[:, k * 128:(k + 1) * 128],
                                            lnxT[:, k * W + ns: k * W + ns + nw],
                                            start=(k == 0), stop=(k == K16 - 1))
                    MM.inc(ins)

                if m < DG:
                    @block.vector
                    def _(vector, m=m, g=g, nw=nw, bank=bank):
                        vector.wait_ge(MM.sem, MM.n)
                        if g >= 5:
                            vector.wait_ge(GST.sem, store_marks[g - 5])
                        EVB.inc(vector.tensor_copy(stgb[g % 5][:, 0:nw], bank.ap()[:, 0:nw]))
                else:
                    @block.scalar
                    def _(scalar, m=m, g=g, nw=nw, bank=bank):
                        scalar.wait_ge(MM.sem, MM.n)
                        if g >= 5:
                            scalar.wait_ge(GST.sem, store_marks[g - 5])
                        EVB.inc(scalar.activation(stgb[g % 5][:, 0:nw], bank.ap()[:, 0:nw], AF.Silu))

                @block.gpsimd
                def _(gp, m=m, ns=ns, nw=nw, g=g):
                    gp.wait_ge(EVB.sem, EVB.n)
                    sg = stgb[g % 5]
                    if m < DG:
                        dst = xpre_scr.ap()[m * 128:(m + 1) * 128, ns - 125: ns - 125 + nw]
                    else:
                        dst = zs_scr.ap()[(m - DG) * 128:(m - DG + 1) * 128, ns - 128: ns - 128 + nw]
                    gp.dma_start(out=dst, in_=sg[:, 0:nw]).then_inc(GST.sem, 16)
                GST.n += 16
                store_marks.append(GST.n)
                mm_g += 1
            inproj_end.append(MM.n)
            if m < DG:
                xpre_gst.append(GST.n)
                if m >= 1:
                    emit_conv(m - 1)
            if m == DG:
                emit_conv(DG - 1)

        g2 = ExitStack()
        xp_sb = g2.enter_context(nc.sbuf_tensor("xp_sb", [128, DG * 160], BF16))
        dtw_sb = g2.enter_context(nc.sbuf_tensor("dtw_sb", [128, DG * 128], BF16))
        dtlow = g2.enter_context(nc.sbuf_tensor("dtlow", [128, S], BF16))
        bc_sb = g2.enter_context(nc.sbuf_tensor("bc_sb", [32, S], F32))
        xcr = [g2.enter_context(nc.sbuf_tensor(f"xcr{i}", [128, 512], BF16)) for i in range(2)]
        sp1 = g2.enter_context(nc.sbuf_tensor("sp1", [128, 512], F32))
        sp2 = g2.enter_context(nc.sbuf_tensor("sp2", [128, 512], F32))
        sp3 = g2.enter_context(nc.sbuf_tensor("sp3", [128, 512], F32))

        @block.gpsimd
        def _(gp):
            gp.dma_start(out=xp_sb[:], in_=x_projT).then_inc(LDC.sem, 16)
            gp.dma_start(out=dtw_sb[:], in_=dt_projT).then_inc(LDC.sem, 16)
        LDC.n += 32

        xp_mm0 = MM.n
        for ci, (ns, nw) in enumerate(NCHUNK):
            ld0 = LDD.n

            @block.sync
            def _(sync, ci=ci, ns=ns, nw=nw, ld0=ld0):
                sync.wait_ge(GST.sem, GST.n)
                for k in range(DG):
                    if k >= 2:
                        sync.wait_ge(MM.sem, xp_mm0 + ci * DG + k - 1)
                    elif ci > 0:
                        sync.wait_ge(MM.sem, xp_mm0 + ci * DG)
                    sync.dma_start(out=xcr[k % 2][:, 0:nw],
                                   in_=xc_scr.ap()[k * 128:(k + 1) * 128, ns:ns + nw]).then_inc(LDD.sem, 16)

            @block.tensor
            def _(tensor, ci=ci, ns=ns, nw=nw, ld0=ld0):
                tensor.wait_ge(LDC.sem, LDC.n)
                if ci >= 1:
                    tensor.wait_ge(EVB.sem, EVB.n)
                for k in range(DG):
                    tensor.wait_ge(LDD.sem, ld0 + (k + 1) * 16)
                    tensor.matmul(pb[4].ap()[:, 0:nw], xp_sb[:, k * 160:k * 160 + 128],
                                  xcr[k % 2][:, 0:nw], start=(k == 0), stop=(k == DG - 1))
                    ins = tensor.matmul(pb[5].ap()[0:32, 0:nw], xp_sb[:, k * 160 + 128:(k + 1) * 160],
                                        xcr[k % 2][:, 0:nw], start=(k == 0), stop=(k == DG - 1))
                    MM.inc(ins)
            LDD.n += 16 * DG

            @block.vector
            def _(vector, ns=ns, nw=nw):
                vector.wait_ge(MM.sem, MM.n)
                VEC.inc(vector.tensor_copy(dtlow[:, ns:ns + nw], pb[4].ap()[:, 0:nw]))
                vector.wait_ge(VEC.sem, VEC.n)
                EVB.inc(vector.tensor_copy(bc_sb[0:32, ns:ns + nw], pb[5].ap()[0:32, 0:nw]))

        @block.gpsimd
        def _(gp):
            gp.wait_ge(EVB.sem, EVB.n)
            gp.dma_start(out=bc_scr.ap(), in_=bc_sb[0:32, :]).then_inc(GST.sem, 16)
        GST.n += 16

        for m in range(DG):
            for ci, (ns, nw) in enumerate(NCHUNK):
                g = m * len(NCHUNK) + ci
                bank = pb[g % 4]

                @block.tensor
                def _(tensor, m=m, ns=ns, nw=nw, g=g, bank=bank):
                    tensor.wait_ge(LDC.sem, LDC.n)
                    tensor.wait_ge(VEC.sem, VEC.n)
                    if g >= 4:
                        tensor.wait_ge(EVC.sem, EVC.n)
                    ins = tensor.matmul(bank.ap()[:, 0:nw], dtw_sb[:, m * 128:(m + 1) * 128],
                                        dtlow[:, ns:ns + nw], start=True, stop=True)
                    MM.inc(ins)

                @block.scalar
                def _(scalar, m=m, nw=nw, bank=bank):
                    scalar.wait_ge(MM.sem, MM.n)
                    scalar.wait_ge(EVC.sem, EVC.n)
                    SCA.inc(scalar.activation(sp1[:, 0:nw], bank.ap()[:, 0:nw], AF.Abs, bias=dtb_sb[:, m:m + 1]))
                    scalar.wait_ge(SCA.sem, SCA.n)
                    SCA.inc(scalar.activation(sp2[:, 0:nw], sp1[:, 0:nw], AF.Exp, scale=-1.0))
                    scalar.wait_ge(SCA.sem, SCA.n)
                    SCA.inc(scalar.activation(sp2[:, 0:nw], sp2[:, 0:nw], AF.Ln, bias=1.0))

                @block.vector
                def _(vector, m=m, nw=nw, bank=bank):
                    vector.wait_ge(MM.sem, MM.n)
                    vector.wait_ge(GST.sem, GST.n)
                    VEC.inc(vector.tensor_scalar(sp3[:, 0:nw], bank.ap()[:, 0:nw], dtb_sb[:, m:m + 1],
                                                 0.0, ALU.add, ALU.max))
                    vector.wait_ge(SCA.sem, SCA.n)
                    vector.wait_ge(VEC.sem, VEC.n)
                    EVC.inc(vector.tensor_tensor(sp3[:, 0:nw], sp3[:, 0:nw], sp2[:, 0:nw], ALU.add))

                @block.gpsimd
                def _(gp, m=m, ns=ns, nw=nw):
                    gp.wait_ge(EVC.sem, EVC.n)
                    gp.dma_start(out=delta_scr.ap()[m * 128:(m + 1) * 128, ns:ns + nw],
                                 in_=sp3[:, 0:nw]).then_inc(GST.sem, 16)
                GST.n += 16

        @block.sync
        def _(sync):
            sync.wait_ge(GST.sem, GST.n)
        barrier()
        g2.close()
        g1.close()

        g3 = ExitStack()
        brep = g3.enter_context(nc.sbuf_tensor("brep", [128, DS * T], BF16))
        crep = g3.enter_context(nc.sbuf_tensor("crep", [128, DS * T], BF16))
        dl_t = [g3.enter_context(nc.sbuf_tensor(f"dl{i}", [128, T], F32)) for i in range(2)]
        xc_t = [g3.enter_context(nc.sbuf_tensor(f"xcs{i}", [128, T], BF16)) for i in range(3)]
        z_t = [g3.enter_context(nc.sbuf_tensor(f"zt{i}", [128, T], BF16)) for i in range(3)]
        dxu_t = g3.enter_context(nc.sbuf_tensor("dxu", [128, T], BF16))
        da_f = g3.enter_context(nc.sbuf_tensor("da_f", [128, DS * T], BF16))
        xb_f = g3.enter_context(nc.sbuf_tensor("xb_f", [128, DS * T], BF16))
        h_sb = g3.enter_context(nc.sbuf_tensor("h_sb", [128, DS * T], BF16))
        htmp = g3.enter_context(nc.sbuf_tensor("htmp", [128, 4 * T], BF16))
        yd_t = g3.enter_context(nc.sbuf_tensor("yd", [128, T], BF16))

        regs = ExitStack()
        r_role = regs.enter_context(nc.sync.register("r_role"))
        r_run = regs.enter_context(nc.sync.register("r_run"))
        r_flag = regs.enter_context(nc.sync.register("r_flag"))
        r_go = regs.enter_context(nc.sync.register("r_go"))
        r_it = regs.enter_context(nc.sync.register("r_it"))
        r_tmp = regs.enter_context(nc.sync.register("r_tmp"))

        @block.sync
        def _(gp):
            gp.wait_ge(LDA.sem, 16 * 12)
            gp.reg_load(r_role, role_sb[0:1, 0:1])
            gp.reg_load(r_run, runid_sb[0:1, 0:1])
            gp.store(pstat_sb[0:1, DG:DG + 1], r_role)
            gp.store(pstat_sb[0:1, DG + 1:DG + 2], r_run)

        @block.gpsimd
        def _(gp):
            src_b = bc_scr.ap()[0:DS, 0:T].partition_broadcast(128)
            src_c = bc_scr.ap()[DS:2 * DS, 0:T].partition_broadcast(128)
            gp.dma_start(out=brep[:].rearrange("p (n t) -> p n t", t=T), in_=src_b).then_inc(LDC.sem, 16)
            gp.dma_start(out=crep[:].rearrange("p (n t) -> p n t", t=T), in_=src_c).then_inc(LDC.sem, 16)
        LDC.n += 32
        ldc_brep = LDC.n

        vsc_end, gpx_end, dg_sca = [], [], []
        for dg in range(DG):
            it = dg
            dl, xc_b, z_b = dl_t[it % 2], xc_t[it % 3], z_t[it % 3]
            vsc0 = VSC.n

            @block.sync
            def _(sync, dg=dg, dl=dl, xc_b=xc_b, z_b=z_b, it=it):
                if it >= 2:
                    sync.wait_ge(VSC.sem, vsc_end[it - 2])
                    sync.wait_ge(SCA.sem, dg_sca[it - 2])
                sync.dma_start(out=dl[:], in_=delta_scr.ap()[dg * 128:(dg + 1) * 128, :]).then_inc(LDD.sem, 16)
                if it >= 3:
                    sync.wait_ge(VSY.sem, it - 2)
                sync.dma_start(out=xc_b[:], in_=xc_scr.ap()[dg * 128:(dg + 1) * 128, :]).then_inc(LDD.sem, 16)
                sync.dma_start(out=z_b[:], in_=zs_scr.ap()[dg * 128:(dg + 1) * 128, :]).then_inc(LDD.sem, 16)
            LDD.n += 48

            @block.sync
            def _(gp, dg=dg):
                if no_hs or not hs_poll:
                    CAR.inc(gp.nop())
                    return
                gp.reg_mov(r_go, r_role)
                gp.reg_mov(r_it, 0)
                with gp.While(r_go):
                    gp.sem_clear(PSM.sem)
                    gp.dma_start(out=flag_sb[:], in_=sh_flag.ap()[0:1, dg * 16:dg * 16 + 1]).then_inc(PSM.sem, 16)
                    gp.wait_ge(PSM.sem, 16)
                    gp.reg_load(r_flag, flag_sb[0:1, 0:1])
                    gp.reg_alu(r_it, r_it, 1, ALU.add)
                    gp.reg_alu(r_go, r_flag, r_run, ALU.not_equal)
                    gp.reg_alu(r_tmp, r_it, POLL_BOUND, ALU.is_lt)
                    gp.reg_alu(r_go, r_go, r_tmp, ALU.mult)
                    gp.reg_alu(r_go, r_go, r_role, ALU.mult)
                gp.store(pstat_sb[0:1, dg:dg + 1], r_it)
                gp.reg_mov(r_go, r_role)
                with gp.While(r_go):
                    gp.sem_clear(PSM.sem)
                    ins = gp.dma_start(out=carry[:, dg * DS:(dg + 1) * DS],
                                       in_=sh_carry.ap()[:, dg * DS:(dg + 1) * DS])
                    ins.then_inc(PSM.sem, 16)
                    gp.wait_ge(PSM.sem, 16)
                    gp.reg_mov(r_go, 0)
                CAR.inc(gp.nop())

            @block.vector
            def _(vector, dg=dg, dl=dl, xc_b=xc_b, ldd_now=LDD.n):
                vector.wait_ge(LDD.sem, ldd_now)
                vector.wait_ge(LDC.sem, ldc_brep)
                VEC.inc(vector.tensor_tensor(dxu_t[:], dl[:], xc_b[:], ALU.mult))
                dxu_b = dxu_t[:].rearrange("p (r t) -> p r t", r=1).broadcast_to([128, 4, T])
                for q in range(4):
                    ins = vector.tensor_tensor(
                        xb_f[:, q * 4 * T:(q + 1) * 4 * T].rearrange("p (r t) -> p r t", r=4),
                        dxu_b,
                        brep[:, q * 4 * T:(q + 1) * 4 * T].rearrange("p (r t) -> p r t", r=4),
                        ALU.mult)
                VEC.inc(ins)

            sca_marks = []
            @block.scalar
            def _(scalar, dg=dg, dl=dl):
                scalar.wait_ge(LDD.sem, LDD.n)
                for half in range(2):
                    if dg >= 1:
                        scalar.wait_ge(VSC.sem, vsc_end[dg - 1] - 8 + 8 * half)
                    for n in range(half * 8, half * 8 + 8):
                        ins = scalar.activation(da_f[:, n * T:(n + 1) * T], dl[:],
                                                AF.Exp, scale=a_sb[:, dg * DS + n: dg * DS + n + 1])
                    SCA.inc(ins)
                    sca_marks.append(SCA.n)
            dg_sca.append(SCA.n)

            @block.vector
            def _(vector, dg=dg):
                vector.wait_ge(CAR.sem, CAR.n)
                for n in range(DS):
                    if n % 8 == 0:
                        vector.wait_ge(SCA.sem, sca_marks[n // 8])
                    VSC.inc(vector.tensor_tensor_scan(h_sb[:, n * T:(n + 1) * T],
                                                      da_f[:, n * T:(n + 1) * T],
                                                      xb_f[:, n * T:(n + 1) * T],
                                                      carry[:, dg * DS + n: dg * DS + n + 1],
                                                      ALU.mult, ALU.add))
            vsc_end.append(VSC.n)

            @block.vector
            def _(vector, dg=dg):
                vector.wait_ge(VSC.sem, VSC.n)
                VEC.inc(vector.tensor_copy(carry[:, dg * DS:(dg + 1) * DS], h_sb[:, T - 1::T]))

            carx = VEC.n

            @block.sync
            def _(gp, dg=dg, carx=carx):
                if no_hs or not hs_send:
                    SND.inc(gp.nop())
                    return
                gp.wait_ge(VEC.sem, carx)
                gp.reg_alu(r_go, r_role, 0, ALU.is_equal)
                with gp.While(r_go):
                    gp.sem_clear(PSM.sem)
                    gp.dma_start(out=sh_carry.ap()[:, dg * DS:(dg + 1) * DS],
                                 in_=carry[:, dg * DS:(dg + 1) * DS]).then_inc(PSM.sem, 16)
                    gp.wait_ge(PSM.sem, 16)
                    gp.sem_clear(PSM.sem)
                    gp.dma_start(out=cstg[0:1, 0:1],
                                 in_=sh_carry.ap()[0:1, dg * DS:dg * DS + 1]).then_inc(PSM.sem, 16)
                    gp.wait_ge(PSM.sem, 16)
                    gp.sem_clear(PSM.sem)
                    gp.dma_start(out=sh_flag.ap()[0:1, dg * 16:dg * 16 + 1], in_=runid_sb[:]).then_inc(PSM.sem, 16)
                    gp.wait_ge(PSM.sem, 16)
                    gp.reg_mov(r_go, 0)
                SND.inc(gp.nop())

            @block.vector
            def _(vector, dg=dg):
                vector.wait_ge(VEC.sem, VEC.n)
                for q in range(4):
                    q0 = q * 4 * T
                    VEC.inc(vector.tensor_tensor(htmp[:], h_sb[:, q0:q0 + 4 * T],
                                                 crep[:, q0:q0 + 4 * T], ALU.mult))
                    vector.wait_ge(VEC.sem, VEC.n)
                    VEC.inc(vector.tensor_tensor(h_sb[:, q0:q0 + 2 * T], htmp[:, 0:2 * T],
                                                 htmp[:, 2 * T:4 * T], ALU.add))
                    vector.wait_ge(VEC.sem, VEC.n)
                    VEC.inc(vector.tensor_tensor(htmp[:, 0:T], h_sb[:, q0:q0 + T],
                                                 h_sb[:, q0 + T:q0 + 2 * T], ALU.add))
                    vector.wait_ge(VEC.sem, VEC.n)
                    if q == 0:
                        VEC.inc(vector.tensor_copy(h_sb[:, 0:T], htmp[:, 0:T]))
                    else:
                        VEC.inc(vector.tensor_tensor(h_sb[:, 0:T], h_sb[:, 0:T], htmp[:, 0:T], ALU.add))
                    vector.wait_ge(VEC.sem, VEC.n)

            @block.vector
            def _(vector, dg=dg, xc_b=xc_b, z_b=z_b):
                vector.wait_ge(VEC.sem, VEC.n)
                vector.wait_ge(GST.sem, GST.n)
                VEC.inc(vector.scalar_tensor_tensor(yd_t[:], xc_b[:], dsk_sb[:, dg:dg + 1],
                                                    h_sb[:, 0:T], ALU.mult, ALU.add))
                vector.wait_ge(VEC.sem, VEC.n)
                VSY.inc(vector.tensor_tensor(yd_t[:], yd_t[:], z_b[:], ALU.mult))

            @block.gpsimd
            def _(gp, dg=dg):
                gp.wait_ge(VSY.sem, VSY.n)
                gp.dma_start(out=yg_scr.ap()[dg * 128:(dg + 1) * 128, :],
                             in_=yd_t[:]).then_inc(GST.sem, 16)
                if dg < K16:
                    gp.dma_start(out=wo_bf_scr.ap()[dg], in_=w_outT[dg]).then_inc(LDA.sem, 16)
                else:
                    gp.dma_start(out=wd_bf_scr.ap()[dg - K16], in_=w_down[dg - K16]).then_inc(LDA.sem, 16)
            GST.n += 16
            LDA.n += 16

        @block.sync
        def _(gp):
            gp.wait_ge(SND.sem, SND.n)
            gp.dma_start(out=poll_scr.ap(), in_=pstat_sb[:]).then_inc(GST.sem, 16)
        GST.n += 16
        regs.close()

        @block.sync
        def _(sync):
            sync.wait_ge(GST.sem, GST.n)
        barrier()
        g3.close()

        g4 = ExitStack()
        ygsb_b = [g4.enter_context(nc.sbuf_tensor(f"ygsb{i}", [128, DG * 512], BF16)) for i in range(2)]
        attn_b = [g4.enter_context(nc.sbuf_tensor(f"attn{i}", [128, K16 * 512], F32)) for i in range(2)]
        wo_sb = [g4.enter_context(nc.sbuf_tensor(f"wo{i}", [128, DG * 128], BF16)) for i in range(2)]
        x1t = [g4.enter_context(nc.sbuf_tensor(f"x1t{i}", [128, DM], F32)) for i in range(2)]
        v2tt = g4.enter_context(nc.sbuf_tensor("v2tt", [128, DM], BF16))
        v2stg = [g4.enter_context(nc.sbuf_tensor(f"v2stg{i}", [128, K16 * 128], BF16)) for i in range(2)]

        oev = []
        v2_gst_marks = []
        for ci, (ns, nw) in enumerate(NCHUNK):
            ygsb = ygsb_b[ci % 2]

            @block.sync
            def _(sync, ci=ci, ns=ns, nw=nw, ygsb=ygsb):
                for k in range(DG):
                    sync.dma_start(out=ygsb[:, k * 512:k * 512 + nw],
                                   in_=yg_scr.ap()[k * 128:(k + 1) * 128, ns:ns + nw]).then_inc(LDD.sem, 16)
            LDD.n += 16 * DG

        for m in range(K16):
            wb = wo_sb[m % 2]

            @block.sync
            def _(sync, m=m, wb=wb):
                if m >= 2:
                    sync.wait_ge(MM.sem, MM.n - 1)
                else:
                    sync.wait_ge(LDA.sem, LDA.n)
                sync.dma_start(out=wb[:], in_=wo_bf_scr.ap()[m]).then_inc(LDC.sem, 16)
            LDC.n += 16

            b0, b1 = pb[(2 * m) % 4], pb[(2 * m + 1) % 4]

            @block.tensor
            def _(tensor, m=m, wb=wb, b0=b0, b1=b1):
                tensor.wait_ge(LDC.sem, LDC.n)
                tensor.wait_ge(LDD.sem, LDD.n)
                if m >= 2:
                    tensor.wait_ge(EVB.sem, oev[2 * m - 3])
                for k in range(DG):
                    tensor.matmul(b0.ap()[:], wb[:, k * 128:(k + 1) * 128],
                                  ygsb_b[0][:, k * 512:(k + 1) * 512],
                                  start=(k == 0), stop=(k == DG - 1))
                    ins = tensor.matmul(b1.ap()[:], wb[:, k * 128:(k + 1) * 128],
                                        ygsb_b[1][:, k * 512:(k + 1) * 512],
                                        start=(k == 0), stop=(k == DG - 1))
                MM.inc(ins)

            for cc in range(2):
                @block.vector
                def _(vector, m=m, cc=cc, bank=(b0, b1)[cc]):
                    vector.wait_ge(MM.sem, MM.n)
                    EVB.inc(vector.tensor_copy(attn_b[cc][:, m * 512:(m + 1) * 512], bank.ap()[:]))
                oev.append(EVB.n)

        for ci, (ns, nw) in enumerate(NCHUNK):
            attn_sb = attn_b[ci]

            for tt in range(nw // 128):
                xt = x1t[tt % 2]

                @block.sync
                def _(sync, ci=ci, ns=ns, tt=tt, xt=xt):
                    sync.wait_ge(GST.sem, GST.n)
                    sync.dma_start(out=xt[:], in_=x0[128 + ns + tt * 128: 128 + ns + tt * 128 + 128, :]).then_inc(LDB.sem, 16)
                LDB.n += 16

                rb4 = [ptf[0], ptf[1], pb[4].ap(), pb[5].ap()]
                for m in range(K16):
                    @block.tensor
                    def _(tensor, m=m, tt=tt):
                        tensor.wait_ge(EVB.sem, EVB.n)
                        lag = 1 if (m < 2 or TRP.n < 4) else 3
                        if TRP.n >= 2:
                            tensor.wait_ge(TCO.sem, TRP.n - lag)
                        TRP.inc(tensor.transpose(rb4[TRP.n % 4][:, 0:128],
                                                 attn_sb[:, m * 512 + tt * 128: m * 512 + tt * 128 + 128],
                                                 ident_f[:]))

                    @block.vector
                    def _(vector, m=m, xt=xt):
                        vector.wait_ge(TRP.sem, TRP.n)
                        vector.wait_ge(LDB.sem, LDB.n)
                        TCO.inc(vector.tensor_tensor(xt[:, m * 128:(m + 1) * 128],
                                                     rb4[(TRP.n - 1) % 4][:, 0:128],
                                                     xt[:, m * 128:(m + 1) * 128], ALU.add))

                @block.scalar
                def _(scalar, ns=ns, tt=tt, xt=xt):
                    scalar.wait_ge(TCO.sem, TCO.n)
                    scalar.dma_start(out=x1_scr.ap()[ns + tt * 128: ns + tt * 128 + 128, :], in_=xt[:]).then_inc(GST.sem, 16)
                GST.n += 16

                @block.vector
                def _(vector, xt=xt):
                    vector.wait_ge(TCO.sem, TCO.n)
                    VEC.inc(vector.tensor_reduce(lnt1[:], xt[:], AX.X, ALU.add))

                @block.scalar
                def _(scalar, xt=xt):
                    scalar.wait_ge(VEC.sem, VEC.n)
                    SCA.inc(scalar.activation(junk[:], xt[:], AF.Square, accum_out=lnt2[:]))

                @block.vector
                def _(vector):
                    vector.wait_ge(SCA.sem, SCA.n)
                    VEC.inc(vector.tensor_scalar(lnt1[:], lnt1[:], 1.0 / DM, None, ALU.mult))
                    vector.wait_ge(VEC.sem, VEC.n)
                    VEC.inc(vector.tensor_scalar(lnt2[:], lnt2[:], 1.0 / DM, None, ALU.mult))
                    vector.wait_ge(VEC.sem, VEC.n)
                    VEC.inc(vector.scalar_tensor_tensor(lnc1[:], lnt1[:], lnt1[:], lnt2[:], ALU.mult, ALU.subtract))
                    vector.wait_ge(VEC.sem, VEC.n)
                    VEC.inc(vector.tensor_scalar(lnc1[:], lnc1[:], -1.0, EPS, ALU.mult, ALU.add))

                @block.scalar
                def _(scalar):
                    scalar.wait_ge(VEC.sem, VEC.n)
                    SCA.inc(scalar.activation(lnc1[:], lnc1[:], AF.Ln))
                    scalar.wait_ge(SCA.sem, SCA.n)
                    SCA.inc(scalar.activation(lnc1[:], lnc1[:], AF.Exp, scale=-0.5))

                @block.vector
                def _(vector, xt=xt):
                    vector.wait_ge(SCA.sem, SCA.n)
                    VEC.inc(vector.tensor_scalar(lnc2[:], lnt1[:], -1.0, lnc1[:], ALU.mult, ALU.mult))
                    vector.wait_ge(VEC.sem, VEC.n)
                    vector.wait_ge(TRP.sem, TRP.n)
                    VEC.inc(vector.tensor_scalar(v2tt[:], xt[:], lnc1[:], lnc2[:], ALU.mult, ALU.add))

                gtt = ci * 4 + tt
                vsb = v2stg[gtt % 2]
                for m in range(K16):
                    @block.tensor
                    def _(tensor, m=m):
                        tensor.wait_ge(VEC.sem, VEC.n)
                        if TRP.n >= 2:
                            tensor.wait_ge(TCO.sem, TRP.n - 1)
                        TRP.inc(tensor.transpose(pt[TRP.n % 2].ap()[:, 0:128],
                                                 v2tt[:, m * 128:(m + 1) * 128], ident_b[:]))

                    @block.vector
                    def _(vector, m=m, gtt=gtt, vsb=vsb):
                        vector.wait_ge(TRP.sem, TRP.n)
                        if m == 0 and gtt >= 2:
                            vector.wait_ge(GST.sem, v2_gst_marks[gtt - 2])
                        TCO.inc(vector.tensor_copy(vsb[:, m * 128:(m + 1) * 128],
                                                   pt[(TRP.n - 1) % 2].ap()[:, 0:128]))

                @block.gpsimd
                def _(gp, ci=ci, ns=ns, tt=tt, vsb=vsb):
                    gp.wait_ge(TCO.sem, TCO.n)
                    gp.dma_start(out=v2t_scr.ap()[:, ns + tt * 128: ns + tt * 128 + 128]
                                 .rearrange("(m p) c -> p m c", m=K16),
                                 in_=vsb[:].rearrange("p (m c) -> p m c", m=K16)).then_inc(GST.sem, 16)
                GST.n += 16
                v2_gst_marks.append(GST.n)

        @block.sync
        def _(sync):
            sync.wait_ge(GST.sem, GST.n)
        barrier()
        g4.close()

        g5 = ExitStack()
        v2k = g5.enter_context(nc.sbuf_tensor("v2k", [128, K16 * 512], BF16))
        fuse = g5.enter_context(nc.sbuf_tensor("fuse", [128, MI * 512], BF16))
        wg_sb = [g5.enter_context(nc.sbuf_tensor(f"wg{i}", [128, K16 * 128], BF16)) for i in range(2)]
        wu_sb = [g5.enter_context(nc.sbuf_tensor(f"wu{i}", [128, K16 * 128], BF16)) for i in range(2)]
        wd_sb = [g5.enter_context(nc.sbuf_tensor(f"wd{i}", [128, MI * 128], BF16)) for i in range(2)]
        gate_t = [g5.enter_context(nc.sbuf_tensor(f"gate_t{i}", [128, 512], BF16)) for i in range(2)]
        ds_t = [g5.enter_context(nc.sbuf_tensor(f"ds_t{i}", [128, 512], F32)) for i in range(2)]
        x1f = [g5.enter_context(nc.sbuf_tensor(f"x1f{i}", [128, DM], F32)) for i in range(4)]

        for ci, (ns, nw) in enumerate(NCHUNK):
            @block.sync
            def _(sync, ci=ci, ns=ns, nw=nw):
                if ci > 0:
                    sync.wait_ge(OUT.sem, OUT.n)
                for k in range(K16):
                    sync.dma_start(out=v2k[:, k * 512:k * 512 + nw],
                                   in_=v2t_scr.ap()[k * 128:(k + 1) * 128, ns:ns + nw]).then_inc(LDD.sem, 16)
                for tt in range(4):
                    sync.dma_start(out=x1f[tt][:],
                                   in_=x1_scr.ap()[ns + tt * 128: ns + tt * 128 + 128, :]).then_inc(LDB.sem, 16)
            LDD.n += 16 * K16
            LDB.n += 16 * 4

            mm_end, sca_end, evb_end = [], [], []
            for mi in range(MI):
                wgb, wub = wg_sb[mi % 2], wu_sb[mi % 2]

                @block.gpsimd
                def _(gp, ci=ci, mi=mi, wgb=wgb, wub=wub):
                    if mi >= 2:
                        gp.wait_ge(MM.sem, mm_end[mi - 2])
                    elif ci > 0:
                        gp.wait_ge(MM.sem, MM.n)
                    gp.dma_start(out=wgb[:], in_=w_gate[mi]).then_inc(LDC.sem, 16)
                    gp.dma_start(out=wub[:], in_=w_up[mi]).then_inc(LDC.sem, 16)
                LDC.n += 32

                @block.tensor
                def _(tensor, ci=ci, mi=mi, nw=nw, wgb=wgb, wub=wub):
                    tensor.wait_ge(LDC.sem, LDC.n)
                    tensor.wait_ge(LDD.sem, LDD.n)
                    if mi >= 2:
                        tensor.wait_ge(SCA.sem, sca_end[mi - 2])
                        tensor.wait_ge(EVB.sem, evb_end[mi - 2])
                    elif ci > 0:
                        tensor.wait_ge(SCA.sem, SCA.n)
                        tensor.wait_ge(EVB.sem, EVB.n)
                    gb, ub = pb[2 + mi % 2], pb[4 + mi % 2]
                    for k in range(K16):
                        tensor.matmul(gb.ap()[:, 0:nw], wgb[:, k * 128:(k + 1) * 128],
                                      v2k[:, k * 512:k * 512 + nw], start=(k == 0), stop=(k == K16 - 1))
                    for k in range(K16):
                        ins = tensor.matmul(ub.ap()[:, 0:nw], wub[:, k * 128:(k + 1) * 128],
                                            v2k[:, k * 512:k * 512 + nw], start=(k == 0), stop=(k == K16 - 1))
                    MM.inc(ins)
                mm_end.append(MM.n)

                @block.scalar
                def _(scalar, mi=mi, nw=nw):
                    scalar.wait_ge(MM.sem, mm_end[mi])
                    if mi >= 1:
                        scalar.wait_ge(EVB.sem, evb_end[mi - 1])
                    SCA.inc(scalar.activation(gate_t[mi % 2][:, 0:nw], pb[2 + mi % 2].ap()[:, 0:nw], AF.Silu,
                                              scale=sg_sb[:, mi:mi + 1]))
                sca_end.append(SCA.n)

                @block.vector
                def _(vector, mi=mi, nw=nw):
                    vector.wait_ge(SCA.sem, sca_end[mi])
                    EVB.inc(vector.scalar_tensor_tensor(fuse[:, mi * 512:mi * 512 + nw],
                                                        pb[4 + mi % 2].ap()[:, 0:nw], su_sb[:, mi:mi + 1],
                                                        gate_t[mi % 2][:, 0:nw], ALU.mult, ALU.mult))
                evb_end.append(EVB.n)

            dn_end, evc_end, trp_end = [], [], []

            def emit_dn_transposes(mh, nw=nw):
                for tt in range(nw // 128):
                    @block.tensor
                    def _(tensor, mh=mh, tt=tt):
                        tensor.wait_ge(EVC.sem, evc_end[mh])
                        if TRP.n >= 2:
                            tensor.wait_ge(TCO.sem, TRP.n - 1)
                        TRP.inc(tensor.transpose(ptf[TRP.n % 2][:, 0:128],
                                                 ds_t[mh % 2][:, tt * 128:(tt + 1) * 128], ident_f[:]))

                    @block.vector
                    def _(vector, mh=mh, tt=tt):
                        vector.wait_ge(TRP.sem, TRP.n)
                        vector.wait_ge(LDB.sem, LDB.n)
                        TCO.inc(vector.tensor_tensor(x1f[tt][:, mh * 128:(mh + 1) * 128],
                                                     ptf[(TRP.n - 1) % 2][:, 0:128],
                                                     x1f[tt][:, mh * 128:(mh + 1) * 128], ALU.add))
                trp_end.append(TRP.n)

            for mh in range(K16):
                wdb = wd_sb[mh % 2]

                @block.sync
                def _(sync, ci=ci, mh=mh, wdb=wdb):
                    if mh >= 2:
                        sync.wait_ge(MM.sem, dn_end[mh - 2])
                    elif ci > 0:
                        sync.wait_ge(MM.sem, MM.n)
                    else:
                        sync.wait_ge(LDA.sem, LDA.n)
                    sync.dma_start(out=wdb[:], in_=wd_bf_scr.ap()[mh]).then_inc(LDC.sem, 16)
                LDC.n += 16

                bank = pb[mh % 2]

                @block.tensor
                def _(tensor, ci=ci, mh=mh, nw=nw, bank=bank, wdb=wdb):
                    tensor.wait_ge(LDC.sem, LDC.n)
                    tensor.wait_ge(EVB.sem, EVB.n)
                    if mh >= 2:
                        tensor.wait_ge(EVC.sem, evc_end[mh - 2])
                    elif ci > 0:
                        tensor.wait_ge(EVC.sem, EVC.n)
                    for k in range(MI):
                        ins = tensor.matmul(bank.ap()[:, 0:nw], wdb[:, k * 128:(k + 1) * 128],
                                            fuse[:, k * 512:k * 512 + nw], start=(k == 0), stop=(k == MI - 1))
                    MM.inc(ins)
                dn_end.append(MM.n)

                @block.vector
                def _(vector, mh=mh, nw=nw, bank=bank):
                    vector.wait_ge(MM.sem, dn_end[mh])
                    if mh >= 2:
                        vector.wait_ge(TRP.sem, trp_end[mh - 2])
                    EVC.inc(vector.tensor_scalar(ds_t[mh % 2][:, 0:nw], bank.ap()[:, 0:nw], sd_sb[:, mh:mh + 1],
                                                 None, ALU.mult))
                evc_end.append(EVC.n)

                if mh >= 1:
                    emit_dn_transposes(mh - 1)
            emit_dn_transposes(K16 - 1)

            @block.gpsimd
            def _(gp, ns=ns):
                gp.wait_ge(TCO.sem, TCO.n)
                for tt in range(4):
                    gp.dma_start(out=y_out[ns + tt * 128: ns + tt * 128 + 128, :],
                                 in_=x1f[tt][:]).then_inc(OUT.sem, 16)
            OUT.n += 16 * 4

        @block.gpsimd
        def _(gp):
            gp.wait_ge(OUT.sem, OUT.n)

        g5.close()

    return nc


def stage_in_maps(inputs):
    hs = np.asarray(inputs["L0_attn_hidden_states"])
    in_proj_w = np.asarray(inputs["L0_attn_in_proj_w"])
    conv_w = np.asarray(inputs["L0_attn_conv_w"])
    conv_b = np.asarray(inputs["L0_attn_conv_b"])
    x_proj_w = np.asarray(inputs["L0_attn_x_proj_w"])
    dt_proj_w = np.asarray(inputs["L0_attn_dt_proj_w"])
    dt_proj_b = np.asarray(inputs["L0_attn_dt_proj_b"])
    a_log = np.asarray(inputs["L0_attn_A_log"])
    d_skip = np.asarray(inputs["L0_attn_D"])
    out_proj_w = np.asarray(inputs["L0_attn_out_proj_w"])
    w_gate = np.asarray(inputs["L0_ffn_w_gate"])
    w_up = np.asarray(inputs["L0_ffn_w_up"])
    w_down = np.asarray(inputs["L0_ffn_w_down"])
    s_gate = np.asarray(inputs["L0_ffn_s_gate"])
    s_up = np.asarray(inputs["L0_ffn_s_up"])
    s_down = np.asarray(inputs["L0_ffn_s_down"])

    Ct = np.ascontiguousarray

    def colmajor(v, g):
        return Ct(v.reshape(g, 128).T)

    runid = int(np.random.default_rng().integers(1 << 16, 1 << 30))
    common = dict(
        w_inT=Ct(in_proj_w.T.reshape(K16, 128, 2 * DI // 128, 128).transpose(2, 1, 0, 3).reshape(2 * DI // 128, 128, K16 * 128)),
        conv_w=Ct(conv_w.reshape(DG, 128, DC).transpose(1, 0, 2).reshape(128, DG * DC)),
        conv_b=colmajor(conv_b, DG),
        x_projT=Ct(x_proj_w.T.reshape(DG, 128, 160).transpose(1, 0, 2).reshape(128, DG * 160)),
        dt_projT=Ct(dt_proj_w.T),
        dt_b=colmajor(dt_proj_b, DG),
        a_log=Ct(a_log.reshape(DG, 128, DS).transpose(1, 0, 2).reshape(128, DG * DS)),
        d_skip=colmajor(d_skip, DG),
        w_outT=Ct(out_proj_w.T.reshape(DG, 128, K16, 128).transpose(2, 1, 0, 3).reshape(K16, 128, DG * 128)),
        w_gate=Ct(w_gate.reshape(K16, 128, MI, 128).transpose(2, 1, 0, 3).reshape(MI, 128, K16 * 128)),
        w_up=Ct(w_up.reshape(K16, 128, MI, 128).transpose(2, 1, 0, 3).reshape(MI, 128, K16 * 128)),
        w_down=Ct(w_down.reshape(MI, 128, K16, 128).transpose(2, 1, 0, 3).reshape(K16, 128, MI * 128)),
        s_gate=colmajor(s_gate, MI),
        s_up=colmajor(s_up, MI),
        s_down=colmajor(s_down, K16),
        ident=np.eye(128, dtype=np.float32),
        runid=np.array([[runid]], np.int32),
    )

    in_maps = []
    for core in range(8):
        b, role = core // 2, core % 2
        o0 = role * S
        xh = np.zeros((W, DM), np.float32)
        if o0 >= 3:
            xh[125:128] = hs[b, o0 - 3:o0]
        xh[128:] = hs[b, o0:o0 + S]
        in_maps.append(dict(common, x0=xh, role=np.array([[role]], np.int32),
                            rolev=np.full((128, 1), float(role), np.float32)))
    return in_maps


def kernel(**inputs):
    nc = build_program()
    in_maps = stage_in_maps(inputs)
    res = run_bass_kernel_spmd(nc, in_maps, core_ids=list(range(8)))
    out = np.empty((B, L, DM), np.float32)
    for core in range(8):
        b, role = core // 2, core % 2
        out[b, role * S:(role + 1) * S] = res.results[core]["y_out"]
    return out


# version 28
# speedup vs baseline: 1.1226x; 1.0246x over previous
import numpy as np
from contextlib import ExitStack

import concourse.bass as bass
import concourse.mybir as mybir
from concourse.bass_utils import run_bass_kernel_spmd

F32 = mybir.dt.float32
BF16 = mybir.dt.bfloat16
I32 = mybir.dt.int32
AF = mybir.ActivationFunctionType
ALU = mybir.AluOpType
AX = mybir.AxisListType

B, L, DM, DI, DS, DC, DTR, INTER = 4, 2048, 2048, 4096, 16, 4, 128, 8192
DG = DI // 128
K16 = DM // 128
MI = INTER // 128
S = 1024
W = S + 128
T = S
NCHUNK = [(0, 512), (512, 512)]
ICHUNK = [(125, 3), (128, 512), (640, 512)]
NT = W // 128
EPS = 1e-5
POLL_BOUND = 2_000_000


class Ctr:
    def __init__(self, sem):
        self.sem = sem
        self.n = 0

    def inc(self, instr, k=1):
        instr.then_inc(self.sem, k)
        self.n += k
        return self.n


def build_program(dbg=False, no_hs=False, hs_poll=True, hs_send=True, hs_local=False):
    nc = bass.Bass("TRN2", target_bir_lowering=False, debug=False)
    KIND = "ExternalOutput" if dbg else "Internal"

    x0 = nc.dram_tensor("x0", [W, DM], F32, kind="ExternalInput").ap()
    w_inT = nc.dram_tensor("w_inT", [2 * DI // 128, 128, K16 * 128], F32, kind="ExternalInput").ap()
    conv_w = nc.dram_tensor("conv_w", [128, DG * DC], F32, kind="ExternalInput").ap()
    conv_b = nc.dram_tensor("conv_b", [128, DG], F32, kind="ExternalInput").ap()
    x_projT = nc.dram_tensor("x_projT", [128, DG * 160], F32, kind="ExternalInput").ap()
    dt_projT = nc.dram_tensor("dt_projT", [DTR, DI], F32, kind="ExternalInput").ap()
    dt_b = nc.dram_tensor("dt_b", [128, DG], F32, kind="ExternalInput").ap()
    a_log = nc.dram_tensor("a_log", [128, DG * DS], F32, kind="ExternalInput").ap()
    d_skip = nc.dram_tensor("d_skip", [128, DG], F32, kind="ExternalInput").ap()
    w_outT = nc.dram_tensor("w_outT", [K16, 128, DG * 128], F32, kind="ExternalInput").ap()
    w_gate = nc.dram_tensor("w_gate", [MI, 128, K16 * 128], mybir.dt.int8, kind="ExternalInput").ap()
    w_up = nc.dram_tensor("w_up", [MI, 128, K16 * 128], mybir.dt.int8, kind="ExternalInput").ap()
    w_down = nc.dram_tensor("w_down", [K16, 128, MI * 128], mybir.dt.int8, kind="ExternalInput").ap()
    s_gate = nc.dram_tensor("s_gate", [128, MI], F32, kind="ExternalInput").ap()
    s_up = nc.dram_tensor("s_up", [128, MI], F32, kind="ExternalInput").ap()
    s_down = nc.dram_tensor("s_down", [128, K16], F32, kind="ExternalInput").ap()
    ident_in = nc.dram_tensor("ident", [128, 128], F32, kind="ExternalInput").ap()
    role_in = nc.dram_tensor("role", [1, 1], I32, kind="ExternalInput").ap()
    rolev_in = nc.dram_tensor("rolev", [128, 1], F32, kind="ExternalInput").ap()
    runid_in = nc.dram_tensor("runid", [1, 1], I32, kind="ExternalInput").ap()

    y_out = nc.dram_tensor("y_out", [S, DM], F32, kind="ExternalOutput").ap()

    xpre_scr = nc.dram_tensor("xpre_scr", [DI, S + 3], BF16, kind=KIND)
    z_scr = nc.dram_tensor("z_scr", [DI, S], BF16, kind=KIND)
    zs_scr = nc.dram_tensor("zs_scr", [DI, S], BF16, kind=KIND)
    xc_scr = nc.dram_tensor("xc_scr", [DI, S], BF16, kind=KIND)
    delta_scr = nc.dram_tensor("delta_scr", [DI, S], F32, kind=KIND)
    bc_scr = nc.dram_tensor("bc_scr", [2 * DS, S], F32, kind=KIND)
    yg_scr = nc.dram_tensor("yg_scr", [DI, S], BF16, kind=KIND)
    x1_scr = nc.dram_tensor("x1_scr", [S, DM], F32, kind=KIND)
    v2t_scr = nc.dram_tensor("v2t_scr", [DM, S], BF16, kind=KIND)
    wo_bf_scr = nc.dram_tensor("wo_bf_scr", [K16, 128, DG * 128], BF16, kind=KIND)
    wd_bf_scr = nc.dram_tensor("wd_bf_scr", [K16, 128, MI * 128], BF16, kind=KIND)

    if hs_local:
        sh_carry = nc.dram_tensor("sh_carry", [128, DG * DS], F32)
        sh_flag = nc.dram_tensor("sh_flag", [1, DG * 16], I32)
    else:
        sh_carry = nc.dram_tensor("sh_carry", [128, DG * DS], F32, addr_space="Shared")
        sh_flag = nc.dram_tensor("sh_flag", [1, DG * 16], I32, addr_space="Shared")
    poll_scr = nc.dram_tensor("poll_scr", [1, DG + 2], I32, kind=KIND)

    ctx = ExitStack()
    with ctx:
        sbuf = lambda name, shape, dt: ctx.enter_context(nc.sbuf_tensor(name, shape, dt))
        ident_f = sbuf("ident_f", [128, 128], F32)
        ident_b = sbuf("ident_b", [128, 128], BF16)
        a_sb = sbuf("a_sb", [128, DG * DS], F32)
        dsk_sb = sbuf("dsk_sb", [128, DG], F32)
        cw_sb = sbuf("cw_sb", [128, DG * DC], F32)
        cb_sb = sbuf("cb_sb", [128, DG], F32)
        dtb_sb = sbuf("dtb_sb", [128, DG], F32)
        sg_sb = sbuf("sg_sb", [128, MI], F32)
        su_sb = sbuf("su_sb", [128, MI], F32)
        sd_sb = sbuf("sd_sb", [128, K16], F32)
        carry = sbuf("carry", [128, DG * DS], F32)
        role_sb = sbuf("role_sb", [1, 1], I32)
        rolev_sb = sbuf("rolev_sb", [128, 1], F32)
        runid_sb = sbuf("runid_sb", [1, 1], I32)
        flag_sb = sbuf("flag_sb", [1, 1], I32)
        pstat_sb = sbuf("pstat_sb", [1, DG + 2], I32)
        cstg = sbuf("cstg", [128, DS], F32)
        ctmp = sbuf("ctmp", [128, DS], F32)
        stgb = [sbuf(f"stgb{i}", [128, 512], BF16) for i in range(5)]
        lnt1 = sbuf("lnt1", [128, 1], F32)
        lnt2 = sbuf("lnt2", [128, 1], F32)
        lnc1 = sbuf("lnc1", [128, 1], F32)
        lnc2 = sbuf("lnc2", [128, 1], F32)
        junk = sbuf("junk", [128, 2048], BF16)

        pb = [ctx.enter_context(nc.psum_tensor(f"pb{i}", [128, 512], F32)) for i in range(6)]
        pt = [ctx.enter_context(nc.psum_tensor(f"pt{i}", [128, 1024], BF16)) for i in range(2)]
        ptf = [p.ap().bitcast(F32) for p in pt]

        C = {}
        for name in ["LDA", "LDB", "LDC", "LDD", "MM", "EVA", "EVB", "EVC", "SCA",
                     "VEC", "VSY", "GST", "BAR", "TRP", "TCO", "OUT", "VSC", "CAR", "PSM", "SND", "GPX"]:
            C[name] = Ctr(ctx.enter_context(nc.semaphore(name)))
        LDA, LDB, LDC, LDD = C["LDA"], C["LDB"], C["LDC"], C["LDD"]
        MM, EVA, EVB, EVC = C["MM"], C["EVA"], C["EVB"], C["EVC"]
        SCA, VEC, VSY = C["SCA"], C["VEC"], C["VSY"]
        GST, BAR, TRP, TCO, OUT, VSC = C["GST"], C["BAR"], C["TRP"], C["TCO"], C["OUT"], C["VSC"]
        CAR, PSM, SND, GPX = C["CAR"], C["PSM"], C["SND"], C["GPX"]

        block = ctx.enter_context(nc.Block())
        ENGS = ["tensor", "vector", "scalar", "gpsimd", "sync"]

        def barrier():
            tgt = BAR.n + 5
            for en in ENGS:
                def mk(en=en, tgt=tgt):
                    def _(e):
                        BAR.inc(e.nop())
                        e.wait_ge(BAR.sem, tgt)
                    return _
                getattr(block, en)(mk())

        @block.sync
        def _(sync):
            for dst, src in [(ident_f, ident_in), (cw_sb, conv_w), (cb_sb, conv_b),
                             (dtb_sb, dt_b), (a_sb, a_log), (dsk_sb, d_skip),
                             (sg_sb, s_gate), (su_sb, s_up), (sd_sb, s_down),
                             (role_sb, role_in), (runid_sb, runid_in), (rolev_sb, rolev_in)]:
                sync.dma_start(out=dst[:], in_=src).then_inc(LDA.sem, 16)
        LDA.n += 16 * 12

        @block.scalar
        def _(scalar):
            scalar.wait_ge(LDA.sem, LDA.n)
            SCA.inc(scalar.activation(a_sb[:], a_sb[:], AF.Exp))

        @block.vector
        def _(vector):
            vector.wait_ge(SCA.sem, SCA.n)
            VEC.inc(vector.tensor_scalar(a_sb[:], a_sb[:], -1.0, None, ALU.mult))
            VEC.inc(vector.memset(carry[:], 0.0))
            VEC.inc(vector.tensor_copy(ident_b[:], ident_f[:]))

        g1 = ExitStack()
        lnxT = g1.enter_context(nc.sbuf_tensor("lnxT", [128, K16 * W], BF16))
        x0t = [g1.enter_context(nc.sbuf_tensor(f"x0t{i}", [128, DM], F32)) for i in range(2)]
        lnxt = g1.enter_context(nc.sbuf_tensor("lnxt", [128, DM], BF16))
        wpool = [g1.enter_context(nc.sbuf_tensor(f"wpool{i}", [128, K16 * 128], BF16)) for i in range(3)]
        wstg = [g1.enter_context(nc.sbuf_tensor(f"wstg{i}", [128, K16 * 128], F32)) for i in range(2)]

        inproj_end = []
        cast_marks = []

        def emit_wload(m):
            ws = wstg[m % 2]
            wb = wpool[m % 3]

            @block.sync
            def _(sync, m=m, ws=ws):
                if m >= 2:
                    sync.wait_ge(VEC.sem, cast_marks[m - 2])
                sync.dma_start(out=ws[:], in_=w_inT[m]).then_inc(LDC.sem, 16)
            LDC.n += 16

            @block.vector
            def _(vector, m=m, wb=wb, ws=ws):
                vector.wait_ge(LDC.sem, LDC.n)
                if m >= 3:
                    vector.wait_ge(MM.sem, inproj_end[m - 3])
                VEC.inc(vector.tensor_copy(wb[:], ws[:]))
            cast_marks.append(VEC.n)

        emit_wload(0)
        emit_wload(1)

        ln_marks = []
        for tt in range(NT):
            buf = x0t[tt % 2]

            @block.sync
            def _(sync, tt=tt, buf=buf):
                if tt >= 2:
                    sync.wait_ge(TRP.sem, (tt - 1) * K16)
                sync.dma_start(out=buf[:], in_=x0[tt * 128:(tt + 1) * 128, :]).then_inc(LDB.sem, 16)
            LDB.n += 16

            @block.vector
            def _(vector, buf=buf):
                vector.wait_ge(LDB.sem, LDB.n)
                VEC.inc(vector.tensor_reduce(lnt1[:], buf[:], AX.X, ALU.add))

            @block.scalar
            def _(scalar, buf=buf):
                scalar.wait_ge(LDB.sem, LDB.n)
                scalar.wait_ge(VEC.sem, VEC.n)
                SCA.inc(scalar.activation(junk[:], buf[:], AF.Square, accum_out=lnt2[:]))

            @block.vector
            def _(vector):
                vector.wait_ge(SCA.sem, SCA.n)
                VEC.inc(vector.tensor_scalar(lnt1[:], lnt1[:], 1.0 / DM, None, ALU.mult))
                vector.wait_ge(VEC.sem, VEC.n)
                VEC.inc(vector.tensor_scalar(lnt2[:], lnt2[:], 1.0 / DM, None, ALU.mult))
                vector.wait_ge(VEC.sem, VEC.n)
                VEC.inc(vector.scalar_tensor_tensor(lnc1[:], lnt1[:], lnt1[:], lnt2[:], ALU.mult, ALU.subtract))
                vector.wait_ge(VEC.sem, VEC.n)
                VEC.inc(vector.tensor_scalar(lnc1[:], lnc1[:], -1.0, EPS, ALU.mult, ALU.add))

            @block.scalar
            def _(scalar):
                scalar.wait_ge(VEC.sem, VEC.n)
                SCA.inc(scalar.activation(lnc1[:], lnc1[:], AF.Ln))
                scalar.wait_ge(SCA.sem, SCA.n)
                SCA.inc(scalar.activation(lnc1[:], lnc1[:], AF.Exp, scale=-0.5))

            @block.vector
            def _(vector, tt=tt, buf=buf):
                vector.wait_ge(SCA.sem, SCA.n)
                VEC.inc(vector.tensor_scalar(lnc2[:], lnt1[:], -1.0, lnc1[:], ALU.mult, ALU.mult))
                vector.wait_ge(VEC.sem, VEC.n)
                if tt >= 1:
                    vector.wait_ge(TRP.sem, tt * K16)
                VEC.inc(vector.tensor_scalar(lnxt[:], buf[:], lnc1[:], lnc2[:], ALU.mult, ALU.add))

            for k in range(K16):
                @block.tensor
                def _(tensor, tt=tt, k=k):
                    tensor.wait_ge(VEC.sem, VEC.n)
                    if TRP.n >= 2:
                        tensor.wait_ge(TCO.sem, TRP.n - 1)
                    TRP.inc(tensor.transpose(pt[TRP.n % 2].ap()[:, 0:128], lnxt[:, k * 128:(k + 1) * 128], ident_b[:]))

                @block.vector
                def _(vector, tt=tt, k=k):
                    vector.wait_ge(TRP.sem, TRP.n)
                    TCO.inc(vector.tensor_copy(lnxT[:, k * W + tt * 128: k * W + tt * 128 + 128],
                                               pt[(TRP.n - 1) % 2].ap()[:, 0:128]))
            ln_marks.append(TCO.n)

        TCO_P1 = TCO.n
        xpt = [g1.enter_context(nc.sbuf_tensor(f"xpt{i}", [128, S + 3], BF16)) for i in range(2)]
        acc_t = g1.enter_context(nc.sbuf_tensor("acc_t", [128, S], F32))
        xct = g1.enter_context(nc.sbuf_tensor("xct", [128, S], BF16))
        cstore_gst = []
        csil_sca = []

        def emit_conv(dg):
            xb = xpt[dg % 2]

            @block.sync
            def _(sync, dg=dg, xb=xb):
                sync.wait_ge(GST.sem, xpre_gst[dg])
                if dg >= 2:
                    sync.wait_ge(VEC.sem, VEC.n)
                sync.dma_start(out=xb[:], in_=xpre_scr.ap()[dg * 128:(dg + 1) * 128, :]).then_inc(LDD.sem, 16)
            LDD.n += 16

            @block.vector
            def _(vector, dg=dg, xb=xb):
                vector.wait_ge(LDD.sem, LDD.n)
                if dg >= 1:
                    vector.wait_ge(SCA.sem, csil_sca[dg - 1])
                VEC.inc(vector.tensor_scalar(acc_t[:], xb[:, 0:S], cw_sb[:, dg * DC: dg * DC + 1],
                                             cb_sb[:, dg:dg + 1], ALU.mult, ALU.add))
                for k in range(1, DC):
                    vector.wait_ge(VEC.sem, VEC.n)
                    VEC.inc(vector.scalar_tensor_tensor(acc_t[:], xb[:, k:k + S],
                                                        cw_sb[:, dg * DC + k: dg * DC + k + 1],
                                                        acc_t[:], ALU.mult, ALU.add))

            @block.scalar
            def _(scalar, dg=dg):
                scalar.wait_ge(VEC.sem, VEC.n)
                if dg >= 1:
                    scalar.wait_ge(GST.sem, cstore_gst[dg - 1])
                ins = scalar.activation(xct[:], acc_t[:], AF.Silu)
                SCA.inc(ins)
                scalar.wait_ge(SCA.sem, SCA.n)
                scalar.dma_start(out=xc_scr.ap()[dg * 128:(dg + 1) * 128, :], in_=xct[:]).then_inc(GST.sem, 16)
            csil_sca.append(SCA.n)
            GST.n += 16
            cstore_gst.append(GST.n)

        mm_g = 0
        xpre_gst = []
        store_marks = []
        NM = 2 * DI // 128

        for m in range(NM):
            wb = wpool[m % 3]
            chunks = ICHUNK if m < DG else ICHUNK[1:]
            if m + 2 < NM:
                emit_wload(m + 2)

            for (ns, nw) in chunks:
                g = mm_g
                bank = pb[g % 4]

                @block.tensor
                def _(tensor, m=m, ns=ns, nw=nw, g=g, bank=bank, wb=wb):
                    tensor.wait_ge(VEC.sem, cast_marks[m])
                    tensor.wait_ge(TCO.sem, ln_marks[(ns + nw + 127) // 128 - 1])
                    if g >= 4:
                        tensor.wait_ge(EVB.sem, g - 3)
                    for k in range(K16):
                        ins = tensor.matmul(bank.ap()[:, 0:nw], wb[:, k * 128:(k + 1) * 128],
                                            lnxT[:, k * W + ns: k * W + ns + nw],
                                            start=(k == 0), stop=(k == K16 - 1))
                    MM.inc(ins)

                if m < DG:
                    @block.vector
                    def _(vector, m=m, g=g, nw=nw, bank=bank):
                        vector.wait_ge(MM.sem, MM.n)
                        if g >= 5:
                            vector.wait_ge(GST.sem, store_marks[g - 5])
                        EVB.inc(vector.tensor_copy(stgb[g % 5][:, 0:nw], bank.ap()[:, 0:nw]))
                else:
                    @block.scalar
                    def _(scalar, m=m, g=g, nw=nw, bank=bank):
                        scalar.wait_ge(MM.sem, MM.n)
                        if g >= 5:
                            scalar.wait_ge(GST.sem, store_marks[g - 5])
                        EVB.inc(scalar.activation(stgb[g % 5][:, 0:nw], bank.ap()[:, 0:nw], AF.Silu))

                @block.gpsimd
                def _(gp, m=m, ns=ns, nw=nw, g=g):
                    gp.wait_ge(EVB.sem, EVB.n)
                    sg = stgb[g % 5]
                    if m < DG:
                        dst = xpre_scr.ap()[m * 128:(m + 1) * 128, ns - 125: ns - 125 + nw]
                    else:
                        dst = zs_scr.ap()[(m - DG) * 128:(m - DG + 1) * 128, ns - 128: ns - 128 + nw]
                    gp.dma_start(out=dst, in_=sg[:, 0:nw]).then_inc(GST.sem, 16)
                GST.n += 16
                store_marks.append(GST.n)
                mm_g += 1
            inproj_end.append(MM.n)
            if m < DG:
                xpre_gst.append(GST.n)
                if m >= 1:
                    emit_conv(m - 1)
            if m == DG:
                emit_conv(DG - 1)

        g2 = ExitStack()
        xp_sb = g2.enter_context(nc.sbuf_tensor("xp_sb", [128, DG * 160], BF16))
        dtw_sb = g2.enter_context(nc.sbuf_tensor("dtw_sb", [128, DG * 128], BF16))
        dtlow = g2.enter_context(nc.sbuf_tensor("dtlow", [128, S], BF16))
        bc_sb = g2.enter_context(nc.sbuf_tensor("bc_sb", [32, S], F32))
        xcr = [g2.enter_context(nc.sbuf_tensor(f"xcr{i}", [128, 512], BF16)) for i in range(2)]
        sp1 = g2.enter_context(nc.sbuf_tensor("sp1", [128, 512], F32))
        sp2 = g2.enter_context(nc.sbuf_tensor("sp2", [128, 512], F32))
        sp3 = g2.enter_context(nc.sbuf_tensor("sp3", [128, 512], F32))

        @block.gpsimd
        def _(gp):
            gp.dma_start(out=xp_sb[:], in_=x_projT).then_inc(LDC.sem, 16)
            gp.dma_start(out=dtw_sb[:], in_=dt_projT).then_inc(LDC.sem, 16)
        LDC.n += 32

        xp_mm0 = MM.n
        for ci, (ns, nw) in enumerate(NCHUNK):
            ld0 = LDD.n

            @block.sync
            def _(sync, ci=ci, ns=ns, nw=nw, ld0=ld0):
                sync.wait_ge(GST.sem, GST.n)
                for k in range(DG):
                    if k >= 2:
                        sync.wait_ge(MM.sem, xp_mm0 + ci * DG + k - 1)
                    elif ci > 0:
                        sync.wait_ge(MM.sem, xp_mm0 + ci * DG)
                    sync.dma_start(out=xcr[k % 2][:, 0:nw],
                                   in_=xc_scr.ap()[k * 128:(k + 1) * 128, ns:ns + nw]).then_inc(LDD.sem, 16)

            @block.tensor
            def _(tensor, ci=ci, ns=ns, nw=nw, ld0=ld0):
                tensor.wait_ge(LDC.sem, LDC.n)
                if ci >= 1:
                    tensor.wait_ge(EVB.sem, EVB.n)
                for k in range(DG):
                    tensor.wait_ge(LDD.sem, ld0 + (k + 1) * 16)
                    tensor.matmul(pb[4].ap()[:, 0:nw], xp_sb[:, k * 160:k * 160 + 128],
                                  xcr[k % 2][:, 0:nw], start=(k == 0), stop=(k == DG - 1))
                    ins = tensor.matmul(pb[5].ap()[0:32, 0:nw], xp_sb[:, k * 160 + 128:(k + 1) * 160],
                                        xcr[k % 2][:, 0:nw], start=(k == 0), stop=(k == DG - 1))
                    MM.inc(ins)
            LDD.n += 16 * DG

            @block.vector
            def _(vector, ns=ns, nw=nw):
                vector.wait_ge(MM.sem, MM.n)
                VEC.inc(vector.tensor_copy(dtlow[:, ns:ns + nw], pb[4].ap()[:, 0:nw]))
                vector.wait_ge(VEC.sem, VEC.n)
                EVB.inc(vector.tensor_copy(bc_sb[0:32, ns:ns + nw], pb[5].ap()[0:32, 0:nw]))

        @block.gpsimd
        def _(gp):
            gp.wait_ge(EVB.sem, EVB.n)
            gp.dma_start(out=bc_scr.ap(), in_=bc_sb[0:32, :]).then_inc(GST.sem, 16)
        GST.n += 16

        for m in range(DG):
            for ci, (ns, nw) in enumerate(NCHUNK):
                g = m * len(NCHUNK) + ci
                bank = pb[g % 4]

                @block.tensor
                def _(tensor, m=m, ns=ns, nw=nw, g=g, bank=bank):
                    tensor.wait_ge(LDC.sem, LDC.n)
                    tensor.wait_ge(VEC.sem, VEC.n)
                    if g >= 4:
                        tensor.wait_ge(EVC.sem, EVC.n)
                    ins = tensor.matmul(bank.ap()[:, 0:nw], dtw_sb[:, m * 128:(m + 1) * 128],
                                        dtlow[:, ns:ns + nw], start=True, stop=True)
                    MM.inc(ins)

                @block.scalar
                def _(scalar, m=m, nw=nw, bank=bank):
                    scalar.wait_ge(MM.sem, MM.n)
                    scalar.wait_ge(EVC.sem, EVC.n)
                    SCA.inc(scalar.activation(sp1[:, 0:nw], bank.ap()[:, 0:nw], AF.Abs, bias=dtb_sb[:, m:m + 1]))
                    scalar.wait_ge(SCA.sem, SCA.n)
                    SCA.inc(scalar.activation(sp2[:, 0:nw], sp1[:, 0:nw], AF.Exp, scale=-1.0))
                    scalar.wait_ge(SCA.sem, SCA.n)
                    SCA.inc(scalar.activation(sp2[:, 0:nw], sp2[:, 0:nw], AF.Ln, bias=1.0))

                @block.vector
                def _(vector, m=m, nw=nw, bank=bank):
                    vector.wait_ge(MM.sem, MM.n)
                    vector.wait_ge(GST.sem, GST.n)
                    VEC.inc(vector.tensor_scalar(sp3[:, 0:nw], bank.ap()[:, 0:nw], dtb_sb[:, m:m + 1],
                                                 0.0, ALU.add, ALU.max))
                    vector.wait_ge(SCA.sem, SCA.n)
                    vector.wait_ge(VEC.sem, VEC.n)
                    EVC.inc(vector.tensor_tensor(sp3[:, 0:nw], sp3[:, 0:nw], sp2[:, 0:nw], ALU.add))

                @block.gpsimd
                def _(gp, m=m, ns=ns, nw=nw):
                    gp.wait_ge(EVC.sem, EVC.n)
                    gp.dma_start(out=delta_scr.ap()[m * 128:(m + 1) * 128, ns:ns + nw],
                                 in_=sp3[:, 0:nw]).then_inc(GST.sem, 16)
                GST.n += 16

        @block.sync
        def _(sync):
            sync.wait_ge(GST.sem, GST.n)
        barrier()
        g2.close()
        g1.close()

        g3 = ExitStack()
        brep = g3.enter_context(nc.sbuf_tensor("brep", [128, DS * T], BF16))
        crep = g3.enter_context(nc.sbuf_tensor("crep", [128, DS * T], BF16))
        dl_t = [g3.enter_context(nc.sbuf_tensor(f"dl{i}", [128, T], F32)) for i in range(2)]
        xc_t = [g3.enter_context(nc.sbuf_tensor(f"xcs{i}", [128, T], BF16)) for i in range(3)]
        z_t = [g3.enter_context(nc.sbuf_tensor(f"zt{i}", [128, T], BF16)) for i in range(3)]
        dxu_t = g3.enter_context(nc.sbuf_tensor("dxu", [128, T], BF16))
        da_f = g3.enter_context(nc.sbuf_tensor("da_f", [128, DS * T], BF16))
        xb_f = g3.enter_context(nc.sbuf_tensor("xb_f", [128, DS * T], BF16))
        h_sb = g3.enter_context(nc.sbuf_tensor("h_sb", [128, DS * T], BF16))
        htmp = g3.enter_context(nc.sbuf_tensor("htmp", [128, 4 * T], BF16))
        yd_t = g3.enter_context(nc.sbuf_tensor("yd", [128, T], BF16))

        regs = ExitStack()
        r_role = regs.enter_context(nc.sync.register("r_role"))
        r_run = regs.enter_context(nc.sync.register("r_run"))
        r_flag = regs.enter_context(nc.sync.register("r_flag"))
        r_go = regs.enter_context(nc.sync.register("r_go"))
        r_it = regs.enter_context(nc.sync.register("r_it"))
        r_tmp = regs.enter_context(nc.sync.register("r_tmp"))

        @block.sync
        def _(gp):
            gp.wait_ge(LDA.sem, 16 * 12)
            gp.reg_load(r_role, role_sb[0:1, 0:1])
            gp.reg_load(r_run, runid_sb[0:1, 0:1])
            gp.store(pstat_sb[0:1, DG:DG + 1], r_role)
            gp.store(pstat_sb[0:1, DG + 1:DG + 2], r_run)

        @block.gpsimd
        def _(gp):
            src_b = bc_scr.ap()[0:DS, 0:T].partition_broadcast(128)
            src_c = bc_scr.ap()[DS:2 * DS, 0:T].partition_broadcast(128)
            gp.dma_start(out=brep[:].rearrange("p (n t) -> p n t", t=T), in_=src_b).then_inc(LDC.sem, 16)
            gp.dma_start(out=crep[:].rearrange("p (n t) -> p n t", t=T), in_=src_c).then_inc(LDC.sem, 16)
        LDC.n += 32
        ldc_brep = LDC.n

        vsc_end, gpx_end, dg_sca = [], [], []
        for dg in range(DG):
            it = dg
            dl, xc_b, z_b = dl_t[it % 2], xc_t[it % 3], z_t[it % 3]
            vsc0 = VSC.n

            @block.sync
            def _(sync, dg=dg, dl=dl, xc_b=xc_b, z_b=z_b, it=it):
                if it >= 2:
                    sync.wait_ge(VSC.sem, vsc_end[it - 2])
                    sync.wait_ge(SCA.sem, dg_sca[it - 2])
                sync.dma_start(out=dl[:], in_=delta_scr.ap()[dg * 128:(dg + 1) * 128, :]).then_inc(LDD.sem, 16)
                if it >= 3:
                    sync.wait_ge(VSY.sem, it - 2)
                sync.dma_start(out=xc_b[:], in_=xc_scr.ap()[dg * 128:(dg + 1) * 128, :]).then_inc(LDD.sem, 16)
                sync.dma_start(out=z_b[:], in_=zs_scr.ap()[dg * 128:(dg + 1) * 128, :]).then_inc(LDD.sem, 16)
            LDD.n += 48

            @block.sync
            def _(gp, dg=dg):
                if no_hs or not hs_poll:
                    CAR.inc(gp.nop())
                    return
                gp.reg_mov(r_go, r_role)
                gp.reg_mov(r_it, 0)
                with gp.While(r_go):
                    gp.sem_clear(PSM.sem)
                    gp.dma_start(out=flag_sb[:], in_=sh_flag.ap()[0:1, dg * 16:dg * 16 + 1]).then_inc(PSM.sem, 16)
                    gp.wait_ge(PSM.sem, 16)
                    gp.reg_load(r_flag, flag_sb[0:1, 0:1])
                    gp.reg_alu(r_it, r_it, 1, ALU.add)
                    gp.reg_alu(r_go, r_flag, r_run, ALU.not_equal)
                    gp.reg_alu(r_tmp, r_it, POLL_BOUND, ALU.is_lt)
                    gp.reg_alu(r_go, r_go, r_tmp, ALU.mult)
                    gp.reg_alu(r_go, r_go, r_role, ALU.mult)
                gp.store(pstat_sb[0:1, dg:dg + 1], r_it)
                gp.reg_mov(r_go, r_role)
                with gp.While(r_go):
                    gp.sem_clear(PSM.sem)
                    ins = gp.dma_start(out=carry[:, dg * DS:(dg + 1) * DS],
                                       in_=sh_carry.ap()[:, dg * DS:(dg + 1) * DS])
                    ins.then_inc(PSM.sem, 16)
                    gp.wait_ge(PSM.sem, 16)
                    gp.reg_mov(r_go, 0)
                CAR.inc(gp.nop())

            @block.vector
            def _(vector, dg=dg, dl=dl, xc_b=xc_b, ldd_now=LDD.n):
                vector.wait_ge(LDD.sem, ldd_now)
                vector.wait_ge(LDC.sem, ldc_brep)
                VEC.inc(vector.tensor_tensor(dxu_t[:], dl[:], xc_b[:], ALU.mult))
                dxu_b = dxu_t[:].rearrange("p (r t) -> p r t", r=1).broadcast_to([128, 4, T])
                for q in range(4):
                    ins = vector.tensor_tensor(
                        xb_f[:, q * 4 * T:(q + 1) * 4 * T].rearrange("p (r t) -> p r t", r=4),
                        dxu_b,
                        brep[:, q * 4 * T:(q + 1) * 4 * T].rearrange("p (r t) -> p r t", r=4),
                        ALU.mult)
                VEC.inc(ins)

            sca_marks = []
            @block.scalar
            def _(scalar, dg=dg, dl=dl):
                scalar.wait_ge(LDD.sem, LDD.n)
                for half in range(2):
                    if dg >= 1:
                        scalar.wait_ge(VSC.sem, vsc_end[dg - 1] - 8 + 8 * half)
                    for n in range(half * 8, half * 8 + 8):
                        ins = scalar.activation(da_f[:, n * T:(n + 1) * T], dl[:],
                                                AF.Exp, scale=a_sb[:, dg * DS + n: dg * DS + n + 1])
                    SCA.inc(ins)
                    sca_marks.append(SCA.n)
            dg_sca.append(SCA.n)

            @block.vector
            def _(vector, dg=dg):
                vector.wait_ge(CAR.sem, CAR.n)
                for n in range(DS):
                    if n % 8 == 0:
                        vector.wait_ge(SCA.sem, sca_marks[n // 8])
                    VSC.inc(vector.tensor_tensor_scan(h_sb[:, n * T:(n + 1) * T],
                                                      da_f[:, n * T:(n + 1) * T],
                                                      xb_f[:, n * T:(n + 1) * T],
                                                      carry[:, dg * DS + n: dg * DS + n + 1],
                                                      ALU.mult, ALU.add))
            vsc_end.append(VSC.n)

            @block.vector
            def _(vector, dg=dg):
                vector.wait_ge(VSC.sem, VSC.n)
                VEC.inc(vector.tensor_copy(carry[:, dg * DS:(dg + 1) * DS], h_sb[:, T - 1::T]))

            carx = VEC.n

            @block.sync
            def _(gp, dg=dg, carx=carx):
                if no_hs or not hs_send:
                    SND.inc(gp.nop())
                    return
                gp.wait_ge(VEC.sem, carx)
                gp.reg_alu(r_go, r_role, 0, ALU.is_equal)
                with gp.While(r_go):
                    gp.sem_clear(PSM.sem)
                    gp.dma_start(out=sh_carry.ap()[:, dg * DS:(dg + 1) * DS],
                                 in_=carry[:, dg * DS:(dg + 1) * DS]).then_inc(PSM.sem, 16)
                    gp.wait_ge(PSM.sem, 16)
                    gp.sem_clear(PSM.sem)
                    gp.dma_start(out=sh_flag.ap()[0:1, dg * 16:dg * 16 + 1], in_=runid_sb[:]).then_inc(PSM.sem, 16)
                    gp.wait_ge(PSM.sem, 16)
                    gp.reg_mov(r_go, 0)
                SND.inc(gp.nop())

            @block.vector
            def _(vector, dg=dg):
                vector.wait_ge(VEC.sem, VEC.n)
                for q in range(4):
                    q0 = q * 4 * T
                    VEC.inc(vector.tensor_tensor(htmp[:], h_sb[:, q0:q0 + 4 * T],
                                                 crep[:, q0:q0 + 4 * T], ALU.mult))
                    vector.wait_ge(VEC.sem, VEC.n)
                    VEC.inc(vector.tensor_tensor(h_sb[:, q0:q0 + 2 * T], htmp[:, 0:2 * T],
                                                 htmp[:, 2 * T:4 * T], ALU.add))
                    vector.wait_ge(VEC.sem, VEC.n)
                    VEC.inc(vector.tensor_tensor(htmp[:, 0:T], h_sb[:, q0:q0 + T],
                                                 h_sb[:, q0 + T:q0 + 2 * T], ALU.add))
                    vector.wait_ge(VEC.sem, VEC.n)
                    if q == 0:
                        VEC.inc(vector.tensor_copy(h_sb[:, 0:T], htmp[:, 0:T]))
                    else:
                        VEC.inc(vector.tensor_tensor(h_sb[:, 0:T], h_sb[:, 0:T], htmp[:, 0:T], ALU.add))
                    vector.wait_ge(VEC.sem, VEC.n)

            @block.vector
            def _(vector, dg=dg, xc_b=xc_b, z_b=z_b):
                vector.wait_ge(VEC.sem, VEC.n)
                vector.wait_ge(GST.sem, GST.n)
                VEC.inc(vector.scalar_tensor_tensor(yd_t[:], xc_b[:], dsk_sb[:, dg:dg + 1],
                                                    h_sb[:, 0:T], ALU.mult, ALU.add))
                vector.wait_ge(VEC.sem, VEC.n)
                VSY.inc(vector.tensor_tensor(yd_t[:], yd_t[:], z_b[:], ALU.mult))

            @block.gpsimd
            def _(gp, dg=dg):
                gp.wait_ge(VSY.sem, VSY.n)
                gp.dma_start(out=yg_scr.ap()[dg * 128:(dg + 1) * 128, :],
                             in_=yd_t[:]).then_inc(GST.sem, 16)
                if dg < K16:
                    gp.dma_start(out=wo_bf_scr.ap()[dg], in_=w_outT[dg]).then_inc(LDA.sem, 16)
                else:
                    gp.dma_start(out=wd_bf_scr.ap()[dg - K16], in_=w_down[dg - K16]).then_inc(LDA.sem, 16)
            GST.n += 16
            LDA.n += 16

        @block.sync
        def _(gp):
            gp.wait_ge(SND.sem, SND.n)
            gp.dma_start(out=poll_scr.ap(), in_=pstat_sb[:]).then_inc(GST.sem, 16)
        GST.n += 16
        regs.close()

        @block.sync
        def _(sync):
            sync.wait_ge(GST.sem, GST.n)
        barrier()
        g3.close()

        g4 = ExitStack()
        ygsb_b = [g4.enter_context(nc.sbuf_tensor(f"ygsb{i}", [128, DG * 512], BF16)) for i in range(2)]
        attn_b = [g4.enter_context(nc.sbuf_tensor(f"attn{i}", [128, K16 * 512], F32)) for i in range(2)]
        wo_sb = [g4.enter_context(nc.sbuf_tensor(f"wo{i}", [128, DG * 128], BF16)) for i in range(2)]
        x1t = [g4.enter_context(nc.sbuf_tensor(f"x1t{i}", [128, DM], F32)) for i in range(2)]
        v2tt = g4.enter_context(nc.sbuf_tensor("v2tt", [128, DM], BF16))
        v2stg = [g4.enter_context(nc.sbuf_tensor(f"v2stg{i}", [128, K16 * 128], BF16)) for i in range(2)]

        oev = []
        v2_gst_marks = []
        for ci, (ns, nw) in enumerate(NCHUNK):
            ygsb = ygsb_b[ci % 2]

            @block.sync
            def _(sync, ci=ci, ns=ns, nw=nw, ygsb=ygsb):
                for k in range(DG):
                    sync.dma_start(out=ygsb[:, k * 512:k * 512 + nw],
                                   in_=yg_scr.ap()[k * 128:(k + 1) * 128, ns:ns + nw]).then_inc(LDD.sem, 16)
            LDD.n += 16 * DG

        for m in range(K16):
            wb = wo_sb[m % 2]

            @block.sync
            def _(sync, m=m, wb=wb):
                if m >= 2:
                    sync.wait_ge(MM.sem, MM.n - 1)
                else:
                    sync.wait_ge(LDA.sem, LDA.n)
                sync.dma_start(out=wb[:], in_=wo_bf_scr.ap()[m]).then_inc(LDC.sem, 16)
            LDC.n += 16

            b0, b1 = pb[(2 * m) % 4], pb[(2 * m + 1) % 4]

            @block.tensor
            def _(tensor, m=m, wb=wb, b0=b0, b1=b1):
                tensor.wait_ge(LDC.sem, LDC.n)
                tensor.wait_ge(LDD.sem, LDD.n)
                if m >= 2:
                    tensor.wait_ge(EVB.sem, oev[2 * m - 3])
                for k in range(DG):
                    tensor.matmul(b0.ap()[:], wb[:, k * 128:(k + 1) * 128],
                                  ygsb_b[0][:, k * 512:(k + 1) * 512],
                                  start=(k == 0), stop=(k == DG - 1))
                    ins = tensor.matmul(b1.ap()[:], wb[:, k * 128:(k + 1) * 128],
                                        ygsb_b[1][:, k * 512:(k + 1) * 512],
                                        start=(k == 0), stop=(k == DG - 1))
                MM.inc(ins)

            for cc in range(2):
                @block.vector
                def _(vector, m=m, cc=cc, bank=(b0, b1)[cc]):
                    vector.wait_ge(MM.sem, MM.n)
                    EVB.inc(vector.tensor_copy(attn_b[cc][:, m * 512:(m + 1) * 512], bank.ap()[:]))
                oev.append(EVB.n)

        for ci, (ns, nw) in enumerate(NCHUNK):
            attn_sb = attn_b[ci]

            for tt in range(nw // 128):
                xt = x1t[tt % 2]

                @block.sync
                def _(sync, ci=ci, ns=ns, tt=tt, xt=xt):
                    sync.wait_ge(GST.sem, GST.n)
                    sync.dma_start(out=xt[:], in_=x0[128 + ns + tt * 128: 128 + ns + tt * 128 + 128, :]).then_inc(LDB.sem, 16)
                LDB.n += 16

                rb4 = [ptf[0], ptf[1], pb[4].ap(), pb[5].ap()]
                for m in range(K16):
                    @block.tensor
                    def _(tensor, m=m, tt=tt):
                        tensor.wait_ge(EVB.sem, EVB.n)
                        lag = 1 if (m < 2 or TRP.n < 4) else 3
                        if TRP.n >= 2:
                            tensor.wait_ge(TCO.sem, TRP.n - lag)
                        TRP.inc(tensor.transpose(rb4[TRP.n % 4][:, 0:128],
                                                 attn_sb[:, m * 512 + tt * 128: m * 512 + tt * 128 + 128],
                                                 ident_f[:]))

                    @block.vector
                    def _(vector, m=m, xt=xt):
                        vector.wait_ge(TRP.sem, TRP.n)
                        vector.wait_ge(LDB.sem, LDB.n)
                        TCO.inc(vector.tensor_tensor(xt[:, m * 128:(m + 1) * 128],
                                                     rb4[(TRP.n - 1) % 4][:, 0:128],
                                                     xt[:, m * 128:(m + 1) * 128], ALU.add))

                @block.scalar
                def _(scalar, ns=ns, tt=tt, xt=xt):
                    scalar.wait_ge(TCO.sem, TCO.n)
                    scalar.dma_start(out=x1_scr.ap()[ns + tt * 128: ns + tt * 128 + 128, :], in_=xt[:]).then_inc(GST.sem, 16)
                GST.n += 16

                @block.vector
                def _(vector, xt=xt):
                    vector.wait_ge(TCO.sem, TCO.n)
                    VEC.inc(vector.tensor_reduce(lnt1[:], xt[:], AX.X, ALU.add))

                @block.scalar
                def _(scalar, xt=xt):
                    scalar.wait_ge(VEC.sem, VEC.n)
                    SCA.inc(scalar.activation(junk[:], xt[:], AF.Square, accum_out=lnt2[:]))

                @block.vector
                def _(vector):
                    vector.wait_ge(SCA.sem, SCA.n)
                    VEC.inc(vector.tensor_scalar(lnt1[:], lnt1[:], 1.0 / DM, None, ALU.mult))
                    vector.wait_ge(VEC.sem, VEC.n)
                    VEC.inc(vector.tensor_scalar(lnt2[:], lnt2[:], 1.0 / DM, None, ALU.mult))
                    vector.wait_ge(VEC.sem, VEC.n)
                    VEC.inc(vector.scalar_tensor_tensor(lnc1[:], lnt1[:], lnt1[:], lnt2[:], ALU.mult, ALU.subtract))
                    vector.wait_ge(VEC.sem, VEC.n)
                    VEC.inc(vector.tensor_scalar(lnc1[:], lnc1[:], -1.0, EPS, ALU.mult, ALU.add))

                @block.scalar
                def _(scalar):
                    scalar.wait_ge(VEC.sem, VEC.n)
                    SCA.inc(scalar.activation(lnc1[:], lnc1[:], AF.Ln))
                    scalar.wait_ge(SCA.sem, SCA.n)
                    SCA.inc(scalar.activation(lnc1[:], lnc1[:], AF.Exp, scale=-0.5))

                @block.vector
                def _(vector, xt=xt):
                    vector.wait_ge(SCA.sem, SCA.n)
                    VEC.inc(vector.tensor_scalar(lnc2[:], lnt1[:], -1.0, lnc1[:], ALU.mult, ALU.mult))
                    vector.wait_ge(VEC.sem, VEC.n)
                    vector.wait_ge(TRP.sem, TRP.n)
                    VEC.inc(vector.tensor_scalar(v2tt[:], xt[:], lnc1[:], lnc2[:], ALU.mult, ALU.add))

                gtt = ci * 4 + tt
                vsb = v2stg[gtt % 2]
                for m in range(K16):
                    @block.tensor
                    def _(tensor, m=m):
                        tensor.wait_ge(VEC.sem, VEC.n)
                        if TRP.n >= 2:
                            tensor.wait_ge(TCO.sem, TRP.n - 1)
                        TRP.inc(tensor.transpose(pt[TRP.n % 2].ap()[:, 0:128],
                                                 v2tt[:, m * 128:(m + 1) * 128], ident_b[:]))

                    @block.vector
                    def _(vector, m=m, gtt=gtt, vsb=vsb):
                        vector.wait_ge(TRP.sem, TRP.n)
                        if m == 0 and gtt >= 2:
                            vector.wait_ge(GST.sem, v2_gst_marks[gtt - 2])
                        TCO.inc(vector.tensor_copy(vsb[:, m * 128:(m + 1) * 128],
                                                   pt[(TRP.n - 1) % 2].ap()[:, 0:128]))

                @block.gpsimd
                def _(gp, ci=ci, ns=ns, tt=tt, vsb=vsb):
                    gp.wait_ge(TCO.sem, TCO.n)
                    gp.dma_start(out=v2t_scr.ap()[:, ns + tt * 128: ns + tt * 128 + 128]
                                 .rearrange("(m p) c -> p m c", m=K16),
                                 in_=vsb[:].rearrange("p (m c) -> p m c", m=K16)).then_inc(GST.sem, 16)
                GST.n += 16
                v2_gst_marks.append(GST.n)

        @block.sync
        def _(sync):
            sync.wait_ge(GST.sem, GST.n)
        barrier()
        g4.close()

        g5 = ExitStack()
        v2k = g5.enter_context(nc.sbuf_tensor("v2k", [128, K16 * 512], BF16))
        fuse = g5.enter_context(nc.sbuf_tensor("fuse", [128, MI * 512], BF16))
        wg_sb = [g5.enter_context(nc.sbuf_tensor(f"wg{i}", [128, K16 * 128], BF16)) for i in range(2)]
        wu_sb = [g5.enter_context(nc.sbuf_tensor(f"wu{i}", [128, K16 * 128], BF16)) for i in range(2)]
        wd_sb = [g5.enter_context(nc.sbuf_tensor(f"wd{i}", [128, MI * 128], BF16)) for i in range(2)]
        gate_t = [g5.enter_context(nc.sbuf_tensor(f"gate_t{i}", [128, 512], BF16)) for i in range(2)]
        ds_t = [g5.enter_context(nc.sbuf_tensor(f"ds_t{i}", [128, 512], F32)) for i in range(2)]
        x1f = [g5.enter_context(nc.sbuf_tensor(f"x1f{i}", [128, DM], F32)) for i in range(4)]

        for ci, (ns, nw) in enumerate(NCHUNK):
            @block.sync
            def _(sync, ci=ci, ns=ns, nw=nw):
                if ci > 0:
                    sync.wait_ge(OUT.sem, OUT.n)
                for k in range(K16):
                    sync.dma_start(out=v2k[:, k * 512:k * 512 + nw],
                                   in_=v2t_scr.ap()[k * 128:(k + 1) * 128, ns:ns + nw]).then_inc(LDD.sem, 16)
                for tt in range(4):
                    sync.dma_start(out=x1f[tt][:],
                                   in_=x1_scr.ap()[ns + tt * 128: ns + tt * 128 + 128, :]).then_inc(LDB.sem, 16)
            LDD.n += 16 * K16
            LDB.n += 16 * 4

            mm_end, sca_end, evb_end = [], [], []
            for mi in range(MI):
                wgb, wub = wg_sb[mi % 2], wu_sb[mi % 2]

                @block.gpsimd
                def _(gp, ci=ci, mi=mi, wgb=wgb, wub=wub):
                    if mi >= 2:
                        gp.wait_ge(MM.sem, mm_end[mi - 2])
                    elif ci > 0:
                        gp.wait_ge(MM.sem, MM.n)
                    gp.dma_start(out=wgb[:], in_=w_gate[mi]).then_inc(LDC.sem, 16)
                    gp.dma_start(out=wub[:], in_=w_up[mi]).then_inc(LDC.sem, 16)
                LDC.n += 32

                @block.tensor
                def _(tensor, ci=ci, mi=mi, nw=nw, wgb=wgb, wub=wub):
                    tensor.wait_ge(LDC.sem, LDC.n)
                    tensor.wait_ge(LDD.sem, LDD.n)
                    if mi >= 2:
                        tensor.wait_ge(SCA.sem, sca_end[mi - 2])
                        tensor.wait_ge(EVB.sem, evb_end[mi - 2])
                    elif ci > 0:
                        tensor.wait_ge(SCA.sem, SCA.n)
                        tensor.wait_ge(EVB.sem, EVB.n)
                    gb, ub = pb[2 + mi % 2], pb[4 + mi % 2]
                    for k in range(K16):
                        tensor.matmul(gb.ap()[:, 0:nw], wgb[:, k * 128:(k + 1) * 128],
                                      v2k[:, k * 512:k * 512 + nw], start=(k == 0), stop=(k == K16 - 1))
                    for k in range(K16):
                        ins = tensor.matmul(ub.ap()[:, 0:nw], wub[:, k * 128:(k + 1) * 128],
                                            v2k[:, k * 512:k * 512 + nw], start=(k == 0), stop=(k == K16 - 1))
                    MM.inc(ins)
                mm_end.append(MM.n)

                @block.scalar
                def _(scalar, mi=mi, nw=nw):
                    scalar.wait_ge(MM.sem, mm_end[mi])
                    if mi >= 1:
                        scalar.wait_ge(EVB.sem, evb_end[mi - 1])
                    SCA.inc(scalar.activation(gate_t[mi % 2][:, 0:nw], pb[2 + mi % 2].ap()[:, 0:nw], AF.Silu,
                                              scale=sg_sb[:, mi:mi + 1]))
                sca_end.append(SCA.n)

                @block.vector
                def _(vector, mi=mi, nw=nw):
                    vector.wait_ge(SCA.sem, sca_end[mi])
                    EVB.inc(vector.scalar_tensor_tensor(fuse[:, mi * 512:mi * 512 + nw],
                                                        pb[4 + mi % 2].ap()[:, 0:nw], su_sb[:, mi:mi + 1],
                                                        gate_t[mi % 2][:, 0:nw], ALU.mult, ALU.mult))
                evb_end.append(EVB.n)

            dn_end, evc_end, trp_end = [], [], []

            def emit_dn_transposes(mh, nw=nw):
                for tt in range(nw // 128):
                    @block.tensor
                    def _(tensor, mh=mh, tt=tt):
                        tensor.wait_ge(EVC.sem, evc_end[mh])
                        if TRP.n >= 2:
                            tensor.wait_ge(TCO.sem, TRP.n - 1)
                        TRP.inc(tensor.transpose(ptf[TRP.n % 2][:, 0:128],
                                                 ds_t[mh % 2][:, tt * 128:(tt + 1) * 128], ident_f[:]))

                    @block.vector
                    def _(vector, mh=mh, tt=tt):
                        vector.wait_ge(TRP.sem, TRP.n)
                        vector.wait_ge(LDB.sem, LDB.n)
                        TCO.inc(vector.tensor_tensor(x1f[tt][:, mh * 128:(mh + 1) * 128],
                                                     ptf[(TRP.n - 1) % 2][:, 0:128],
                                                     x1f[tt][:, mh * 128:(mh + 1) * 128], ALU.add))
                trp_end.append(TRP.n)

            for mh in range(K16):
                wdb = wd_sb[mh % 2]

                @block.sync
                def _(sync, ci=ci, mh=mh, wdb=wdb):
                    if mh >= 2:
                        sync.wait_ge(MM.sem, dn_end[mh - 2])
                    elif ci > 0:
                        sync.wait_ge(MM.sem, MM.n)
                    else:
                        sync.wait_ge(LDA.sem, LDA.n)
                    sync.dma_start(out=wdb[:], in_=wd_bf_scr.ap()[mh]).then_inc(LDC.sem, 16)
                LDC.n += 16

                bank = pb[mh % 2]

                @block.tensor
                def _(tensor, ci=ci, mh=mh, nw=nw, bank=bank, wdb=wdb):
                    tensor.wait_ge(LDC.sem, LDC.n)
                    tensor.wait_ge(EVB.sem, EVB.n)
                    if mh >= 2:
                        tensor.wait_ge(EVC.sem, evc_end[mh - 2])
                    elif ci > 0:
                        tensor.wait_ge(EVC.sem, EVC.n)
                    for k in range(MI):
                        ins = tensor.matmul(bank.ap()[:, 0:nw], wdb[:, k * 128:(k + 1) * 128],
                                            fuse[:, k * 512:k * 512 + nw], start=(k == 0), stop=(k == MI - 1))
                    MM.inc(ins)
                dn_end.append(MM.n)

                @block.vector
                def _(vector, mh=mh, nw=nw, bank=bank):
                    vector.wait_ge(MM.sem, dn_end[mh])
                    if mh >= 2:
                        vector.wait_ge(TRP.sem, trp_end[mh - 2])
                    EVC.inc(vector.tensor_scalar(ds_t[mh % 2][:, 0:nw], bank.ap()[:, 0:nw], sd_sb[:, mh:mh + 1],
                                                 None, ALU.mult))
                evc_end.append(EVC.n)

                if mh >= 1:
                    emit_dn_transposes(mh - 1)
            emit_dn_transposes(K16 - 1)

            @block.gpsimd
            def _(gp, ns=ns):
                gp.wait_ge(TCO.sem, TCO.n)
                for tt in range(4):
                    gp.dma_start(out=y_out[ns + tt * 128: ns + tt * 128 + 128, :],
                                 in_=x1f[tt][:]).then_inc(OUT.sem, 16)
            OUT.n += 16 * 4

        @block.gpsimd
        def _(gp):
            gp.wait_ge(OUT.sem, OUT.n)

        g5.close()

    return nc


def stage_in_maps(inputs):
    hs = np.asarray(inputs["L0_attn_hidden_states"])
    in_proj_w = np.asarray(inputs["L0_attn_in_proj_w"])
    conv_w = np.asarray(inputs["L0_attn_conv_w"])
    conv_b = np.asarray(inputs["L0_attn_conv_b"])
    x_proj_w = np.asarray(inputs["L0_attn_x_proj_w"])
    dt_proj_w = np.asarray(inputs["L0_attn_dt_proj_w"])
    dt_proj_b = np.asarray(inputs["L0_attn_dt_proj_b"])
    a_log = np.asarray(inputs["L0_attn_A_log"])
    d_skip = np.asarray(inputs["L0_attn_D"])
    out_proj_w = np.asarray(inputs["L0_attn_out_proj_w"])
    w_gate = np.asarray(inputs["L0_ffn_w_gate"])
    w_up = np.asarray(inputs["L0_ffn_w_up"])
    w_down = np.asarray(inputs["L0_ffn_w_down"])
    s_gate = np.asarray(inputs["L0_ffn_s_gate"])
    s_up = np.asarray(inputs["L0_ffn_s_up"])
    s_down = np.asarray(inputs["L0_ffn_s_down"])

    Ct = np.ascontiguousarray

    def colmajor(v, g):
        return Ct(v.reshape(g, 128).T)

    runid = int(np.random.default_rng().integers(1 << 16, 1 << 30))
    common = dict(
        w_inT=Ct(in_proj_w.T.reshape(K16, 128, 2 * DI // 128, 128).transpose(2, 1, 0, 3).reshape(2 * DI // 128, 128, K16 * 128)),
        conv_w=Ct(conv_w.reshape(DG, 128, DC).transpose(1, 0, 2).reshape(128, DG * DC)),
        conv_b=colmajor(conv_b, DG),
        x_projT=Ct(x_proj_w.T.reshape(DG, 128, 160).transpose(1, 0, 2).reshape(128, DG * 160)),
        dt_projT=Ct(dt_proj_w.T),
        dt_b=colmajor(dt_proj_b, DG),
        a_log=Ct(a_log.reshape(DG, 128, DS).transpose(1, 0, 2).reshape(128, DG * DS)),
        d_skip=colmajor(d_skip, DG),
        w_outT=Ct(out_proj_w.T.reshape(DG, 128, K16, 128).transpose(2, 1, 0, 3).reshape(K16, 128, DG * 128)),
        w_gate=Ct(w_gate.reshape(K16, 128, MI, 128).transpose(2, 1, 0, 3).reshape(MI, 128, K16 * 128)),
        w_up=Ct(w_up.reshape(K16, 128, MI, 128).transpose(2, 1, 0, 3).reshape(MI, 128, K16 * 128)),
        w_down=Ct(w_down.reshape(MI, 128, K16, 128).transpose(2, 1, 0, 3).reshape(K16, 128, MI * 128)),
        s_gate=colmajor(s_gate, MI),
        s_up=colmajor(s_up, MI),
        s_down=colmajor(s_down, K16),
        ident=np.eye(128, dtype=np.float32),
        runid=np.array([[runid]], np.int32),
    )

    in_maps = []
    for core in range(8):
        b, role = core // 2, core % 2
        o0 = role * S
        xh = np.zeros((W, DM), np.float32)
        if o0 >= 3:
            xh[125:128] = hs[b, o0 - 3:o0]
        xh[128:] = hs[b, o0:o0 + S]
        in_maps.append(dict(common, x0=xh, role=np.array([[role]], np.int32),
                            rolev=np.full((128, 1), float(role), np.float32)))
    return in_maps


def kernel(**inputs):
    nc = build_program()
    in_maps = stage_in_maps(inputs)
    res = run_bass_kernel_spmd(nc, in_maps, core_ids=list(range(8)))
    out = np.empty((B, L, DM), np.float32)
    for core in range(8):
        b, role = core // 2, core % 2
        out[b, role * S:(role + 1) * S] = res.results[core]["y_out"]
    return out
